# version 04
# baseline (speedup 1.0000x reference)
import math
import numpy as np
from contextlib import ExitStack

import concourse.bass as bass
import concourse.mybir as mybir
from concourse.bass_utils import run_bass_kernel_spmd

F32 = mybir.dt.float32
BF16 = mybir.dt.bfloat16
AF = mybir.ActivationFunctionType
ALU = mybir.AluOpType
AX = mybir.AxisListType

MAGIC = float(1.5 * 2 ** 23)
CLIP = 1.25


class Cfg:
    def __init__(self, B=2, S=4096, D=2048, H=16, DH=128, DFF=8192, NCORES=8, sqw_cap=512, stop_after=None, p25_stub=0, taps=False):
        self.B, self.S, self.D, self.H, self.DH, self.DFF, self.NC = B, S, D, H, DH, DFF, NCORES
        self.NTOK = B * S
        self.T = self.NTOK // NCORES
        self.TT = self.T // 128
        self.DC = D // 128
        self.FC = DFF // 128
        self.G = NCORES // B
        self.SR = S // self.G
        assert self.SR == self.T
        self.ST = S // 128
        self.SRT = self.SR // 128
        self.SQW = min(sqw_cap, self.T)
        self.NSQ = self.T // self.SQW
        self.NOT = self.T // self.NSQ // 128
        self.W1R = D // NCORES
        self.W2R = DFF // NCORES
        self.QW = min(2048, max(DFF, D))
        self.VW = min(512, H * DH)
        self.scale = 1.0 / math.sqrt(DH)
        self.stop_after = stop_after
        self.p25_stub = p25_stub
        self.taps = taps


class Prog:
    def __init__(self, nc):
        self.nc = nc
        self.cnt = {}
        self.sem = {}
        self.lastw = {}

    def mksems(self, st, names):
        for n in names:
            self.sem[n] = st.enter_context(self.nc.semaphore(n))
            self.cnt[n] = 0
            self.lastw[n] = {}

    def inc(self, instr, s, n=1):
        self.cnt[s] += n
        instr.then_inc(self.sem[s], n)
        return (s, self.cnt[s])

    def ev(self, s):
        return (s, self.cnt[s])

    def wait(self, eng, engname, ev):
        if ev is None:
            return
        s, v = ev
        if v <= 0:
            return
        if self.lastw[s].get(engname, 0) < v:
            eng.wait_ge(self.sem[s], v)
            self.lastw[s][engname] = v

    def waits(self, eng, engname, evs):
        for ev in evs:
            if ev is not None:
                self.wait(eng, engname, ev)


def build_program(cfg: Cfg):
    nc = bass.Bass("TRN2", target_bir_lowering=False, debug=False)
    c = cfg
    RG_BATCH = [list(range(g * c.G, (g + 1) * c.G)) for g in range(c.B)]
    RG_ALL = [list(range(c.NC))]

    x_own = nc.dram_tensor("x_own", [c.T, c.D], F32, kind="ExternalInput").ap()
    w_qkv = nc.dram_tensor("w_qkv", [c.D, 3 * c.H * c.DH], F32, kind="ExternalInput").ap()
    w_out = nc.dram_tensor("w_out", [c.H * c.DH, c.D], F32, kind="ExternalInput").ap()
    w1s = nc.dram_tensor("w1s", [c.W1R, c.DFF], F32, kind="ExternalInput").ap()
    w2s = nc.dram_tensor("w2s", [c.W2R, c.D], F32, kind="ExternalInput").ap()
    b1 = nc.dram_tensor("b1", [c.DFF], F32, kind="ExternalInput").ap()
    b2 = nc.dram_tensor("b2", [c.D], F32, kind="ExternalInput").ap()
    out_d = nc.dram_tensor("out", [c.T, c.D], F32, kind="ExternalOutput").ap()

    k_own = nc.dram_tensor("k_own", [c.H * c.DH, c.T], BF16).ap()
    v_own = nc.dram_tensor("v_own", [c.T, c.H * c.DH], BF16).ap()
    k_g = nc.dram_tensor("k_g", [c.G * c.H * c.DH, c.T], BF16).ap()
    v_g = nc.dram_tensor("v_g", [c.G * c.T, c.H * c.DH], BF16).ap()
    p_own = nc.dram_tensor("p_own", [1, 2], F32).ap()
    p_g = nc.dram_tensor("p_g", [1, 2], F32).ap()
    w1q_own = nc.dram_tensor("w1q_own", [c.W1R, c.DFF], BF16).ap()
    w2q_own = nc.dram_tensor("w2q_own", [c.W2R, c.D], BF16).ap()
    w1q = nc.dram_tensor("w1q", [c.D, c.DFF], BF16).ap()
    w2q = nc.dram_tensor("w2q", [c.DFF, c.D], BF16).ap()
    a_d = nc.dram_tensor("a_d", [c.T, c.D], F32).ap()
    if c.taps:
        tap_ad = nc.dram_tensor("tap_ad", [c.T, c.D], F32, kind="ExternalOutput").ap()
        tap_kg = nc.dram_tensor("tap_kg", [c.G * c.H * c.DH, c.T], BF16, kind="ExternalOutput").ap()
        tap_vg = nc.dram_tensor("tap_vg", [c.G * c.T, c.H * c.DH], BF16, kind="ExternalOutput").ap()
        tap_w1q = nc.dram_tensor("tap_w1q", [c.D, c.DFF], BF16, kind="ExternalOutput").ap()
        tap_w2q = nc.dram_tensor("tap_w2q", [c.DFF, c.D], BF16, kind="ExternalOutput").ap()
        tap_g = nc.dram_tensor("tap_g", [c.FC * 128, c.SQW], BF16, kind="ExternalOutput").ap()
        tap_sth = nc.dram_tensor("tap_sth", [128, c.SQW], BF16, kind="ExternalOutput").ap()
        tap_stl = nc.dram_tensor("tap_stl", [128, c.SQW], BF16, kind="ExternalOutput").ap()
        tap_trsb = nc.dram_tensor("tap_trsb", [128, 2 * 2 * 128], F32, kind="ExternalOutput").ap()

    p = Prog(nc)

    st = ExitStack()
    e = st.enter_context
    with st:
        p.mksems(st, ["dB", "dX0", "dX1", "dQ0", "dQ1", "dW0", "dW1", "dWV0", "dWV1",
                      "dKst0", "dKst1", "dVst0", "dVst1", "dP", "dQst0", "dQst1",
                      "dKV0", "dKV1", "dWo0", "dWo1", "dA", "dAD", "dF10", "dF11",
                      "dF20", "dF21", "pe", "act", "dve", "pool",
                      "ccS", "ccK", "ccV", "cc1", "cc2", "dO"])

        xnT = e(nc.sbuf_tensor("xnT", [128, c.DC, c.T], BF16))
        identb = e(nc.sbuf_tensor("identb", [128, 128], BF16))
        identf = e(nc.sbuf_tensor("identf", [128, 128], F32))
        onesb_row = e(nc.sbuf_tensor("onesb_row", [1, 128], BF16))
        onesb_col = e(nc.sbuf_tensor("onesb_col", [128, 1], BF16))
        ones_row = e(nc.sbuf_tensor("ones_row", [1, 128], F32))
        ones_col = e(nc.sbuf_tensor("ones_col", [128, 1], F32))
        b1c = e(nc.sbuf_tensor("b1c", [128, c.FC], F32))
        b2c = e(nc.sbuf_tensor("b2c", [128, c.DC], F32))
        brow = e(nc.sbuf_tensor("brow", [max(c.FC, c.DC), 128], F32))
        rsc = e(nc.sbuf_tensor("rsc", [128, 2], F32))
        scl = e(nc.sbuf_tensor("scl", [1, 4], F32))
        sT = e(nc.sbuf_tensor("sT", [128, c.SQW], F32))
        sTh = e(nc.sbuf_tensor("sTh", [128, c.SQW], BF16))
        sTl = e(nc.sbuf_tensor("sTl", [128, c.SQW], BF16))
        tr_sb = e(nc.sbuf_tensor("tr_sb", [128, 2, 2, 128], F32))

        block = e(nc.Block())

        @block.gpsimd
        def _(g):
            for ident in (identb, identf):
                g.memset(ident[:, :], 0.0).then_inc(p.sem["pool"], 1)
                p.cnt["pool"] += 1
                g.wait_ge(p.sem["pool"], p.cnt["pool"])
                ins = g.affine_select(out=ident[:, :], in_=ident[:, :],
                                      compare_op=ALU.not_equal, fill=1.0,
                                      base=0, pattern=[[-1, 128]], channel_multiplier=1)
                p.inc(ins, "pool")
            p.inc(g.memset(onesb_row[:, :], 1.0), "pool")
            p.inc(g.memset(onesb_col[:, :], 1.0), "pool")
            p.inc(g.memset(ones_row[:, :], 1.0), "pool")
            p.inc(g.memset(ones_col[:, :], 1.0), "pool")
        ev_const = p.ev("pool")

        with nc.psum_tensor("psB", [128, max(c.FC, c.DC)], F32) as psB:
            @block.sync
            def _(s):
                p.inc(s.dma_start(out=brow[0:c.FC, :], in_=b1.rearrange("(a p) -> a p", p=128)), "dB", 16)
            ev_b1row = p.ev("dB")

            @block.tensor
            def _(t):
                p.waits(t, "t", [ev_b1row, ev_const])
                p.inc(t.transpose(psB[:, 0:c.FC], brow[0:c.FC, :], identf[0:c.FC, 0:c.FC]), "pe")
            ev_tb1 = p.ev("pe")

            @block.vector
            def _(v):
                p.wait(v, "v", ev_tb1)
                p.inc(v.tensor_copy(b1c[:, :], psB[:, 0:c.FC]), "dve")
            ev_b1c = p.ev("dve")

            @block.sync
            def _(s):
                p.wait(s, "s", ev_tb1)
                p.inc(s.dma_start(out=brow[0:c.DC, :], in_=b2.rearrange("(a p) -> a p", p=128)), "dB", 16)
            ev_b2row = p.ev("dB")

            @block.tensor
            def _(t):
                p.waits(t, "t", [ev_b2row, ev_b1c])
                p.inc(t.transpose(psB[:, 0:c.DC], brow[0:c.DC, :], identf[0:c.DC, 0:c.DC]), "pe")
            ev_tb2 = p.ev("pe")

            @block.vector
            def _(v):
                p.wait(v, "v", ev_tb2)
                p.inc(v.tensor_copy(b2c[:, :], psB[:, 0:c.DC]), "dve")
            ev_b2c = p.ev("dve")

        st0 = ExitStack()
        with st0:
            e0 = st0.enter_context
            xstage = e0(nc.sbuf_tensor("xstage", [128, 2, c.D], F32))
            xn_tok = e0(nc.sbuf_tensor("xn_tok", [128, 2, c.D], BF16))
            lncols = e0(nc.sbuf_tensor("lncols", [128, 8], F32))
            ln_scr = e0(nc.sbuf_tensor("ln_scr", [128, c.QW], F32))
            psTr = e0(nc.psum_tensor("psTr", [128, 2, 1024], BF16))

            ev_x = [None] * c.TT
            ev_sq = [None] * c.TT
            ev_xn = [None] * c.TT
            ev_tr = {}
            ev_evac = {}

            for tt in range(c.TT):
                sl = tt % 2

                @block.sync
                def _(s, tt=tt, sl=sl):
                    if tt >= 2:
                        p.wait(s, "s", ev_xn[tt - 2])
                    sem = "dX0" if tt % 2 == 0 else "dX1"
                    p.inc(s.dma_start(out=xstage[:, sl, :], in_=x_own[tt * 128:(tt + 1) * 128, :]), sem, 16)
                    ev_x[tt] = p.ev(sem)

                @block.vector
                def _(v, tt=tt, sl=sl):
                    p.wait(v, "v", ev_x[tt])
                    p.inc(v.tensor_reduce(lncols[:, 0:1], xstage[:, sl, :], axis=AX.X, op=ALU.add), "dve")
                ev_s1 = p.ev("dve")

                @block.scalar
                def _(a, tt=tt, sl=sl):
                    p.wait(a, "a", ev_x[tt])
                    if tt >= 1:
                        p.wait(a, "a", ev_sq[tt - 1])
                    ins = a.activation(ln_scr[:, 0:c.D], xstage[:, sl, :], AF.Square, accum_out=lncols[:, 1:2])
                    ev_sq[tt] = p.inc(ins, "act")

                @block.vector
                def _(v, tt=tt, ev_s1=ev_s1):
                    p.waits(v, "v", [ev_sq[tt], ev_s1])
                    inv_d = 1.0 / c.D
                    p.inc(v.tensor_scalar(lncols[:, 2:3], lncols[:, 0:1], inv_d, None, op0=ALU.mult, op1=ALU.bypass), "dve")
                    p.wait(v, "v", p.ev("dve"))
                    p.inc(v.tensor_scalar(lncols[:, 3:4], lncols[:, 1:2], inv_d, 1e-5, op0=ALU.mult, op1=ALU.add), "dve")
                    p.wait(v, "v", p.ev("dve"))
                    p.inc(v.tensor_tensor(lncols[:, 4:5], lncols[:, 2:3], lncols[:, 2:3], op=ALU.mult), "dve")
                    p.wait(v, "v", p.ev("dve"))
                    p.inc(v.tensor_tensor(lncols[:, 3:4], lncols[:, 3:4], lncols[:, 4:5], op=ALU.subtract), "dve")
                ev_var = p.ev("dve")

                @block.scalar
                def _(a, tt=tt):
                    p.wait(a, "a", ev_var)
                    p.inc(a.activation(lncols[:, 5:6], lncols[:, 3:4], AF.Ln), "act")
                    p.wait(a, "a", p.ev("act"))
                    p.inc(a.activation(lncols[:, 5:6], lncols[:, 5:6], AF.Exp, scale=-0.5), "act")
                ev_rstd = p.ev("act")

                @block.vector
                def _(v, tt=tt):
                    p.wait(v, "v", ev_rstd)
                    p.inc(v.tensor_tensor(lncols[:, 6:7], lncols[:, 2:3], lncols[:, 5:6], op=ALU.mult), "dve")
                    p.wait(v, "v", p.ev("dve"))
                    p.inc(v.tensor_scalar(lncols[:, 6:7], lncols[:, 6:7], -1.0, None, op0=ALU.mult, op1=ALU.bypass), "dve")
                ev_stats = p.ev("dve")

                @block.scalar
                def _(a, tt=tt, sl=sl):
                    p.wait(a, "a", ev_stats)
                    if tt >= 2:
                        p.wait(a, "a", ev_tr[(tt - 2, c.DC - 1)])
                    ins = a.activation(xn_tok[:, sl, :], xstage[:, sl, :], AF.Identity,
                                       bias=lncols[:, 6:7], scale=lncols[:, 5:6])
                    ev_xn[tt] = p.inc(ins, "act")

                for ec in range(c.DC):
                    pslot = ec % 2

                    @block.tensor
                    def _(t, tt=tt, sl=sl, ec=ec, pslot=pslot):
                        p.waits(t, "t", [ev_xn[tt], ev_const])
                        prev = None
                        if ec >= 2:
                            prev = (tt, ec - 2)
                        elif tt >= 1:
                            prev = (tt - 1, c.DC - 2 + ec)
                        if prev in ev_evac:
                            p.wait(t, "t", ev_evac[prev])
                        ins = t.transpose(psTr[:, pslot, 0:128], xn_tok[:, sl, ec * 128:(ec + 1) * 128], identb[:, :])
                        ev_tr[(tt, ec)] = p.inc(ins, "pe")

                    @block.vector
                    def _(v, tt=tt, ec=ec, pslot=pslot):
                        p.wait(v, "v", ev_tr[(tt, ec)])
                        ins = v.tensor_copy(xnT[:, ec, tt * 128:(tt + 1) * 128], psTr[:, pslot, 0:128])
                        ev_evac[(tt, ec)] = p.inc(ins, "dve")

            ev_xnT = p.ev("dve")

            qtiles = []
            for which, R, W in (("w1", c.W1R, c.DFF), ("w2", c.W2R, c.D)):
                for i in range((R + 127) // 128):
                    for j in range((W + c.QW - 1) // c.QW):
                        qtiles.append((which, i, j, min(128, R - i * 128), min(c.QW, W - j * c.QW)))

            with (
                nc.sbuf_tensor("qin", [128, 2, c.QW], F32) as qin,
                nc.sbuf_tensor("acol", [128, 3], F32) as acol,
                nc.psum_tensor("psS", [1, 2], F32) as psS,
            ):
                ev_qin = {}
                ev_abs = {}

                @block.vector
                def _(v):
                    p.inc(v.memset(acol[:, 1:3], 0.0), "dve")
                ev_az = p.ev("dve")

                for idx, (which, i, j, rows, w) in enumerate(qtiles):
                    sl = idx % 2
                    src = w1s if which == "w1" else w2s

                    @block.sync
                    def _(s, idx=idx, sl=sl, src=src, i=i, j=j, rows=rows, w=w):
                        if idx >= 2:
                            p.wait(s, "s", ev_abs[idx - 2])
                        sem = "dQ0" if idx % 2 == 0 else "dQ1"
                        p.inc(s.dma_start(out=qin[0:rows, sl, 0:w],
                                          in_=src[i * 128:i * 128 + rows, j * c.QW:j * c.QW + w]), sem, 16)
                        ev_qin[idx] = p.ev(sem)

                    @block.scalar
                    def _(a, idx=idx, sl=sl, rows=rows, w=w):
                        p.wait(a, "a", ev_qin[idx])
                        if idx >= 1:
                            p.wait(a, "a", ev_abs[idx - 1])
                        if idx >= 1:
                            p.wait(a, "a", p.ev("dve"))
                        ins = a.activation(ln_scr[0:rows, 0:w], qin[0:rows, sl, 0:w], AF.Abs,
                                           accum_out=acol[0:rows, 0:1])
                        ev_abs[idx] = p.inc(ins, "act")

                    @block.vector
                    def _(v, idx=idx, which=which, rows=rows):
                        p.waits(v, "v", [ev_abs[idx], ev_az])
                        col = 1 if which == "w1" else 2
                        p.wait(v, "v", p.ev("dve"))
                        ins = v.tensor_tensor(acol[0:rows, col:col + 1], acol[0:rows, col:col + 1],
                                              acol[0:rows, 0:1], op=ALU.add)
                        p.inc(ins, "dve")
                ev_acc = p.ev("dve")

                @block.tensor
                def _(t):
                    p.waits(t, "t", [ev_acc, ev_const])
                    p.inc(t.matmul(psS[0:1, 0:1], ones_col[:, :], acol[:, 1:2], start=True, stop=True), "pe")
                    p.inc(t.matmul(psS[0:1, 1:2], ones_col[:, :], acol[:, 2:3], start=True, stop=True), "pe")
                ev_psS = p.ev("pe")

                @block.vector
                def _(v):
                    p.wait(v, "v", ev_psS)
                    p.inc(v.tensor_copy(scl[0:1, 0:2], psS[0:1, 0:2]), "dve")
                ev_partial = p.ev("dve")

        def emit_dummy_out():
            dummy = nc.dram_tensor("dummy_src", [c.T, c.D], F32).ap()
            @block.gpsimd
            def _(g):
                p.waits(g, "g", [p.ev("pe"), p.ev("act"), p.ev("dve")])
                p.inc(g.dma_start(out=out_d[:, :], in_=dummy[:, :]), "dO", 16)
                g.wait_ge(p.sem["dO"], p.cnt["dO"])

        if c.stop_after == "p0":
            emit_dummy_out()
            return nc

        stq = ExitStack()
        eq = stq.enter_context
        qT = eq(nc.sbuf_tensor("qT", [128, c.H, c.T], BF16))
        combn = eq(nc.sbuf_tensor("combn", [128, c.H, c.T], BF16))

        fence0 = [p.ev("pe"), p.ev("act"), p.ev("dve")]
        st1 = ExitStack()
        with st1:
            e1 = st1.enter_context
            wst = e1(nc.sbuf_tensor("wst", [128, 2, c.DC, 128], BF16))
            wvm = e1(nc.sbuf_tensor("wvm", [128, 2, c.DC, c.VW], BF16))
            kstage = e1(nc.sbuf_tensor("kstage", [128, 2, c.T], BF16))
            vstage = e1(nc.sbuf_tensor("vstage", [128, 2, c.VW], BF16))
            psQK = e1(nc.psum_tensor("psQK", [128, 2, c.NSQ, 512], F32))
            psV = e1(nc.psum_tensor("psV", [128, 2, 512], F32))

            NQK = 2 * c.H
            ev_w = [None] * NQK
            ev_mmqk = [None] * NQK
            ev_evqk = [None] * NQK
            ev_kdma = [None] * c.H
            w_qkv_r = w_qkv.rearrange("(ec pp) (hc j) -> hc pp ec j", pp=128, j=128)

            for oc in range(NQK):
                sl = oc % 2

                @block.gpsimd
                def _(g, oc=oc, sl=sl):
                    if oc < 2:
                        p.waits(g, "g", fence0)
                    else:
                        p.wait(g, "g", ev_mmqk[oc - 2])
                    sem = "dW0" if oc % 2 == 0 else "dW1"
                    p.inc(g.dma_start(out=wst[:, sl, :, :], in_=w_qkv_r[oc]), sem, 16)
                    ev_w[oc] = p.ev(sem)

                @block.tensor
                def _(t, oc=oc, sl=sl):
                    p.waits(t, "t", [ev_w[oc], ev_xnT])
                    if oc >= 2:
                        p.wait(t, "t", ev_evqk[oc - 2])
                    for sq in range(c.NSQ):
                        for ec in range(c.DC):
                            ins = t.matmul(psQK[:, sl, sq, 0:c.SQW],
                                           wst[:, sl, ec, :],
                                           xnT[:, ec, sq * c.SQW:(sq + 1) * c.SQW],
                                           start=(ec == 0), stop=(ec == c.DC - 1))
                    ev_mmqk[oc] = p.inc(ins, "pe")

                if oc < c.H:
                    @block.vector
                    def _(v, oc=oc, sl=sl):
                        p.wait(v, "v", ev_mmqk[oc])
                        ins = v.tensor_copy(qT[:, oc, :], psQK[:, sl, :, 0:c.SQW])
                        ev_evqk[oc] = p.inc(ins, "dve")
                else:
                    h = oc - c.H

                    @block.vector
                    def _(v, oc=oc, sl=sl, h=h):
                        p.wait(v, "v", ev_mmqk[oc])
                        if h >= 2:
                            p.wait(v, "v", ev_kdma[h - 2])
                        ins = v.tensor_copy(kstage[:, h % 2, :], psQK[:, sl, :, 0:c.SQW])
                        ev_evqk[oc] = p.inc(ins, "dve")

                    @block.gpsimd
                    def _(g, h=h):
                        p.wait(g, "g", ev_evqk[c.H + h])
                        ins = g.dma_start(out=k_own[h * 128:(h + 1) * 128, :], in_=kstage[:, h % 2, :])
                        ev_kdma[h] = p.inc(ins, "dKst0" if h % 2 == 0 else "dKst1", 16)

            ev_qT = ev_evqk[c.H - 1]

            NVG = (c.H * c.DH) // c.VW
            ev_wv = [None] * NVG
            ev_mmv = {}
            ev_evv = {}
            ev_vdma = [None] * c.TT
            w_qkv_v = w_qkv.rearrange("(ec pp) j -> pp ec j", pp=128)

            ev_vst = {}
            for vg in range(NVG):
                sl = vg % 2

                @block.gpsimd
                def _(g, vg=vg, sl=sl):
                    if vg < 2:
                        p.waits(g, "g", fence0)
                    else:
                        p.wait(g, "g", ev_mmv[(vg - 2, c.TT - 1)])
                    base = 2 * c.H * c.DH
                    sem = "dWV0" if vg % 2 == 0 else "dWV1"
                    p.inc(g.dma_start(out=wvm[:, sl, :, :],
                                      in_=w_qkv_v[:, :, base + vg * c.VW: base + (vg + 1) * c.VW]), sem, 16)
                    ev_wv[vg] = p.ev(sem)

                for tt in range(c.TT):
                    k = vg * c.TT + tt
                    ksl = k % 2

                    @block.tensor
                    def _(t, vg=vg, sl=sl, tt=tt):
                        p.waits(t, "t", [ev_wv[vg], ev_xnT])
                        prev = None
                        if tt >= 2:
                            prev = (vg, tt - 2)
                        elif vg >= 1:
                            prev = (vg - 1, c.TT - 2 + tt) if c.TT >= 2 else (vg - 1, tt)
                        if prev in ev_evv:
                            p.wait(t, "t", ev_evv[prev])
                        for ec in range(c.DC):
                            ins = t.matmul(psV[:, tt % 2, 0:c.VW], xnT[:, ec, tt * 128:(tt + 1) * 128],
                                           wvm[:, sl, ec, :], start=(ec == 0), stop=(ec == c.DC - 1))
                        ev_mmv[(vg, tt)] = p.inc(ins, "pe")

                    @block.vector
                    def _(v, vg=vg, tt=tt, k=k, ksl=ksl):
                        p.wait(v, "v", ev_mmv[(vg, tt)])
                        if k >= 2:
                            p.wait(v, "v", ev_vst[k - 2])
                        ins = v.tensor_copy(vstage[:, ksl, 0:c.VW], psV[:, tt % 2, 0:c.VW])
                        ev_evv[(vg, tt)] = p.inc(ins, "dve")

                    @block.gpsimd
                    def _(g, vg=vg, tt=tt, k=k, ksl=ksl):
                        p.wait(g, "g", ev_evv[(vg, tt)])
                        ins = g.dma_start(out=v_own[tt * 128:(tt + 1) * 128, vg * c.VW:(vg + 1) * c.VW],
                                          in_=vstage[:, ksl, 0:c.VW])
                        ev_vst[k] = p.inc(ins, "dVst0" if ksl == 0 else "dVst1", 16)

            for tt in range(c.TT):
                ev_vdma[tt] = ev_vst[(NVG - 1) * c.TT + tt]

            @block.gpsimd
            def _(g):
                p.wait(g, "g", ev_kdma[c.H - 1])
                if c.H >= 2:
                    p.wait(g, "g", ev_kdma[c.H - 2])
                ins = g.collective_compute("AllGather", ALU.bypass, replica_groups=RG_BATCH,
                                           ins=[k_own.opt()], outs=[k_g.opt()])
                p.inc(ins, "ccK")
                p.wait(g, "g", ev_vdma[c.TT - 1])
                if c.TT >= 2:
                    p.wait(g, "g", ev_vdma[c.TT - 2])
                ins = g.collective_compute("AllGather", ALU.bypass, replica_groups=RG_BATCH,
                                           ins=[v_own.opt()], outs=[v_g.opt()])
                p.inc(ins, "ccV")
            ev_ccK = p.ev("ccK")
            ev_ccV = p.ev("ccV")

        @block.gpsimd
        def _(g):
            p.wait(g, "g", ev_partial)
            p.inc(g.dma_start(out=p_own[:, :], in_=scl[0:1, 0:2]), "dP", 16)
            p.wait(g, "g", p.ev("dP"))
            ins = g.collective_compute("AllReduce", ALU.add, replica_groups=RG_ALL,
                                       ins=[p_own.opt()], outs=[p_g.opt()])
            p.inc(ins, "ccS")
            p.wait(g, "g", p.ev("ccS"))
            p.inc(g.dma_start(out=scl[0:1, 2:4], in_=p_g[:, :]), "dP", 16)
        ev_pg = p.ev("dP")

        with nc.psum_tensor("psBc", [128, 2], F32) as psBc:
            @block.vector
            def _(v):
                p.wait(v, "v", ev_pg)
                n1 = 1.0 / (c.D * c.DFF)
                p.inc(v.tensor_scalar(scl[0:1, 2:3], scl[0:1, 2:3], n1, 1e-8, op0=ALU.mult, op1=ALU.max), "dve")
                p.wait(v, "v", p.ev("dve"))
                p.inc(v.tensor_scalar(scl[0:1, 3:4], scl[0:1, 3:4], n1, 1e-8, op0=ALU.mult, op1=ALU.max), "dve")
                p.wait(v, "v", p.ev("dve"))
                p.inc(v.reciprocal(scl[0:1, 2:4], scl[0:1, 2:4]), "dve")
            ev_rs = p.ev("dve")

            @block.tensor
            def _(t):
                p.waits(t, "t", [ev_rs, ev_const])
                p.inc(t.matmul(psBc[:, 0:2], ones_row[0:1, :], scl[0:1, 2:4], start=True, stop=True), "pe")
            ev_bc = p.ev("pe")

            @block.vector
            def _(v):
                p.wait(v, "v", ev_bc)
                p.inc(v.tensor_copy(rsc[:, 0:2], psBc[:, 0:2]), "dve")
            ev_rsc = p.ev("dve")


        stq2 = ExitStack()
        if True:
            qin2 = stq2.enter_context(nc.sbuf_tensor("qin2", [128, 2, c.QW], F32, side="right"))
            qout = stq2.enter_context(nc.sbuf_tensor("qout", [128, 2, c.QW], BF16, side="right"))
            fence1 = [p.ev("pe"), p.ev("dve"), p.ev("act"), ev_vdma[c.TT - 1], ev_vdma[c.TT - 2] if c.TT >= 2 else None]
            ev_qdma = {}
            ev_qv = {}
            n1t = len([1 for q in qtiles if q[0] == "w1"])
            for idx, (which, i, j, rows, w) in enumerate(qtiles):
                sl = idx % 2
                src = w1s if which == "w1" else w2s
                dst = w1q_own if which == "w1" else w2q_own
                rcol = 0 if which == "w1" else 1

                @block.sync
                def _(s, idx=idx, sl=sl, src=src, i=i, j=j, rows=rows, w=w):
                    if idx < 2:
                        p.waits(s, "s", fence1)
                    else:
                        p.wait(s, "s", ev_qv[idx - 2])
                    sem = "dQ0" if idx % 2 == 0 else "dQ1"
                    p.inc(s.dma_start(out=qin2[0:rows, sl, 0:w],
                                      in_=src[i * 128:i * 128 + rows, j * c.QW:j * c.QW + w]), sem, 16)
                    ev_qdma[idx] = p.ev(sem)

                @block.vector
                def _(v, idx=idx, sl=sl, rows=rows, w=w, rcol=rcol):
                    p.waits(v, "v", [ev_qdma[idx], ev_rsc])
                    a = qin2[0:rows, sl, 0:w]
                    p.inc(v.tensor_scalar(a, a, rsc[0:rows, rcol:rcol + 1], CLIP, op0=ALU.mult, op1=ALU.min), "dve")
                    p.wait(v, "v", p.ev("dve"))
                    p.inc(v.tensor_scalar(a, a, -CLIP, MAGIC, op0=ALU.max, op1=ALU.add), "dve")
                    p.wait(v, "v", p.ev("dve"))
                    if idx >= 2:
                        p.wait(v, "v", ev_qdma.get(("out", idx - 2)))
                    ins = v.tensor_scalar(qout[0:rows, sl, 0:w], a, MAGIC, None, op0=ALU.subtract, op1=ALU.bypass)
                    ev_qv[idx] = p.inc(ins, "dve")

                @block.gpsimd
                def _(g, idx=idx, sl=sl, dst=dst, i=i, j=j, rows=rows, w=w):
                    p.wait(g, "g", ev_qv[idx])
                    ins = g.dma_start(out=dst[i * 128:i * 128 + rows, j * c.QW:j * c.QW + w],
                                      in_=qout[0:rows, sl, 0:w])
                    ev_qdma[("out", idx)] = p.inc(ins, "dQst0" if idx % 2 == 0 else "dQst1", 16)

            @block.gpsimd
            def _(g):
                p.wait(g, "g", ev_qdma[("out", n1t - 1)])
                if n1t >= 2:
                    p.wait(g, "g", ev_qdma[("out", n1t - 2)])
                ins = g.collective_compute("AllGather", ALU.bypass, replica_groups=RG_ALL,
                                           ins=[w1q_own.opt()], outs=[w1q.opt()])
                p.inc(ins, "cc1")
                p.wait(g, "g", ev_qdma[("out", len(qtiles) - 1)])
                if len(qtiles) >= 2:
                    p.wait(g, "g", ev_qdma[("out", len(qtiles) - 2)])
                ins = g.collective_compute("AllGather", ALU.bypass, replica_groups=RG_ALL,
                                           ins=[w2q_own.opt()], outs=[w2q.opt()])
                p.inc(ins, "cc2")
        ev_cc1 = p.ev("cc1")
        ev_cc2 = p.ev("cc2")

        if c.stop_after == "p1":
            emit_dummy_out()
            stq2.close()
            stq.close()
            return nc

        st2 = ExitStack()
        with st2:
            e2 = st2.enter_context
            kbuf = e2(nc.sbuf_tensor("kbuf", [128, 2, c.G, c.SR], BF16))
            vbuf = e2(nc.sbuf_tensor("vbuf", [128, 2, c.ST, c.DH], BF16))
            Ebuf = e2(nc.sbuf_tensor("Ebuf", [128, 2, c.NSQ, c.SQW], BF16))
            Zp = e2(nc.sbuf_tensor("Zp", [128, c.NSQ, c.SQW], BF16))
            rzb = e2(nc.sbuf_tensor("rzb", [1, c.NSQ, c.SQW], F32))
            rzbh = e2(nc.sbuf_tensor("rzbh", [1, c.NSQ, c.SQW], BF16))
            rzbc = e2(nc.sbuf_tensor("rzbc", [128, c.NSQ, c.SQW], BF16))
            psL = e2(nc.psum_tensor("psL", [128, 2, c.NSQ, 512], F32))
            psC = e2(nc.psum_tensor("psC", [128, c.NSQ, 512], F32))
            psZ = e2(nc.psum_tensor("psZ", [1, c.NSQ, 512], F32))

            ev_kv = [None] * c.H
            ev_exp = {}
            ev_mmL = {}
            ev_mmC = {}
            ev_zadd = {}
            ev_comb = [None] * c.H
            ev_rzc = [None] * c.H
            ev_onemm = [None] * c.H
            ev_bcmm = [None] * c.H

            k_gr = k_g.rearrange("(r hc pp) t -> r hc pp t", r=c.G, pp=128)
            v_gr = v_g.rearrange("(r o pp) (hc dv) -> r hc pp o dv", r=c.G, pp=128, dv=c.DH)

            def emit_head_dma(h):
                sl = h % 2

                @block.sync
                def _(s, h=h, sl=sl):
                    p.waits(s, "s", [ev_ccK, ev_ccV])
                    if h >= 2:
                        p.wait(s, "s", ev_mmL[(h - 2, c.ST - 1)])
                        p.wait(s, "s", ev_mmC[(h - 2, c.ST - 1)])
                    sem = "dKV0" if h % 2 == 0 else "dKV1"
                    for r in range(c.G):
                        p.inc(s.dma_start(out=kbuf[:, sl, r, :], in_=k_gr[r, h]), sem, 16)
                        p.inc(s.dma_start(out=vbuf[:, sl, r * c.SRT:(r + 1) * c.SRT, :], in_=v_gr[r, h]), sem, 16)
                    ev_kv[h] = p.ev(sem)

            emit_head_dma(0)
            emit_head_dma(1)

            for h in range(c.H):
                sl = h % 2
                for t in range(c.ST):
                    pslot = t % 2
                    eslot = t % 2

                    @block.tensor
                    def _(tns, h=h, sl=sl, t=t, pslot=pslot):
                        p.waits(tns, "t", [ev_kv[h], ev_qT])
                        if t >= 2:
                            p.wait(tns, "t", ev_exp[(h, t - 2)])
                        elif h >= 1:
                            p.wait(tns, "t", ev_exp[(h - 1, c.ST - 2 + t)])
                            if t == 0:
                                p.wait(tns, "t", ev_comb[h - 1])
                        r, col = t // c.SRT, (t % c.SRT) * 128
                        for sq in range(c.NSQ):
                            ins = tns.matmul(psL[:, pslot, sq, 0:c.SQW], kbuf[:, sl, r, col:col + 128],
                                             qT[:, h, sq * c.SQW:(sq + 1) * c.SQW], start=True, stop=True)
                        ev_mmL[(h, t)] = p.inc(ins, "pe")

                    @block.scalar
                    def _(a, h=h, t=t, pslot=pslot, eslot=eslot):
                        p.wait(a, "a", ev_mmL[(h, t)])
                        if t >= 2:
                            p.waits(a, "a", [ev_mmC[(h, t - 2)], ev_zadd[(h, t - 2)]])
                        elif h >= 1:
                            p.waits(a, "a", [ev_mmC[(h - 1, c.ST - 2 + t)], ev_zadd[(h - 1, c.ST - 2 + t)]])
                        ins = a.activation(Ebuf[:, eslot, :, :], psL[:, pslot, :, 0:c.SQW], AF.Exp, scale=c.scale)
                        ev_exp[(h, t)] = p.inc(ins, "act")

                    @block.tensor
                    def _(tns, h=h, sl=sl, t=t, eslot=eslot):
                        p.wait(tns, "t", ev_exp[(h, t)])
                        for sq in range(c.NSQ):
                            ins = tns.matmul(psC[:, sq, 0:c.SQW], vbuf[:, sl, t, :], Ebuf[:, eslot, sq, :],
                                             start=(t == 0), stop=(t == c.ST - 1))
                        ev_mmC[(h, t)] = p.inc(ins, "pe")

                    @block.vector
                    def _(v, h=h, t=t, eslot=eslot):
                        p.wait(v, "v", ev_exp[(h, t)])
                        if t == 0 and h >= 1:
                            p.wait(v, "v", ev_onemm[h - 1])
                        for sq in range(c.NSQ):
                            if t == 0:
                                ins = v.tensor_copy(Zp[:, sq, :], Ebuf[:, eslot, sq, :])
                            else:
                                p.wait(v, "v", p.ev("dve"))
                                ins = v.tensor_tensor(Zp[:, sq, :], Zp[:, sq, :], Ebuf[:, eslot, sq, :], op=ALU.add)
                        ev_zadd[(h, t)] = p.inc(ins, "dve")

                if h + 2 < c.H:
                    emit_head_dma(h + 2)

                @block.tensor
                def _(tns, h=h):
                    p.wait(tns, "t", ev_zadd[(h, c.ST - 1)])
                    if h >= 1:
                        p.wait(tns, "t", ev_rzc[h - 1])
                    for sq in range(c.NSQ):
                        ins = tns.matmul(psZ[0:1, sq, 0:c.SQW], onesb_col[:, :], Zp[:, sq, :], start=True, stop=True)
                    ev_onemm[h] = p.inc(ins, "pe")

                @block.vector
                def _(v, h=h):
                    p.wait(v, "v", ev_onemm[h])
                    ins = v.reciprocal(rzb[0:1, :, :], psZ[0:1, :, 0:c.SQW])
                    p.inc(ins, "dve")
                    p.wait(v, "v", p.ev("dve"))
                    ins = v.tensor_copy(rzbh[0:1, :, :], rzb[0:1, :, :])
                    ev_rzc[h] = p.inc(ins, "dve")

                @block.tensor
                def _(tns, h=h):
                    p.waits(tns, "t", [ev_rzc[h], ev_exp[(h, c.ST - 1)]])
                    for sq in range(c.NSQ):
                        ins = tns.matmul(psL[:, 0, sq, 0:c.SQW], onesb_row[0:1, :], rzbh[0:1, sq, :], start=True, stop=True)
                    ev_bcmm[h] = p.inc(ins, "pe")

                @block.vector
                def _(v, h=h):
                    p.wait(v, "v", ev_bcmm[h])
                    p.inc(v.tensor_copy(rzbc[:, :, :], psL[:, 0, :, 0:c.SQW]), "dve")
                    p.wait(v, "v", p.ev("dve"))
                    p.wait(v, "v", ev_mmC[(h, c.ST - 1)])
                    for sq in range(c.NSQ):
                        if sq:
                            p.wait(v, "v", p.ev("dve"))
                        ins = v.tensor_tensor(combn[:, h, sq * c.SQW:(sq + 1) * c.SQW],
                                              psC[:, sq, 0:c.SQW], rzbc[:, sq, :], op=ALU.mult)
                    ev_comb[h] = p.inc(ins, "dve")

            ev_attn = ev_comb[c.H - 1]

        def emit_tr_add(sh, dc, sl, prev1, psIn, psT, ev_mm_d, ev_evo_d, ev_tr_d, ev_add_d,
                        ev_xload, bcol):
            NH = max(1, c.NOT // 2)
            for hf in range(NH):
                grp = (sh, dc, "g", hf)
                pgrp = (sh, dc, "g", hf - 1) if hf >= 1 else \
                       ((prev1[0], prev1[1], "g", NH - 1) if prev1 is not None else None)
                tiles = [i for i in range(2 * hf, min(2 * hf + 2, c.NOT))]
                w0, w1_ = tiles[0] * 128, (tiles[-1] + 1) * 128

                @block.vector
                def _(v, sh=sh, dc=dc, sl=sl, grp=grp, pgrp=pgrp, w0=w0, w1_=w1_, hf=hf):
                    p.wait(v, "v", ev_mm_d[(sh, dc)])
                    if bcol is not None:
                        p.wait(v, "v", ev_b2c)
                    if pgrp is not None and ("sth", pgrp) in ev_add_d:
                        p.wait(v, "v", ev_add_d[("sth", pgrp)])
                    if bcol is not None:
                        ins = v.tensor_scalar(sT[:, w0:w1_], psIn[:, sl, w0:w1_], bcol, None,
                                              op0=ALU.add, op1=ALU.bypass)
                    else:
                        ins = v.tensor_copy(sT[:, w0:w1_], psIn[:, sl, w0:w1_])
                    p.inc(ins, "dve")
                    ev_evo_d[grp] = p.ev("dve")
                if hf == NH - 1:
                    ev_evo_d[(sh, dc)] = ev_evo_d[grp]

                @block.scalar
                def _(a, sh=sh, dc=dc, grp=grp, pgrp=pgrp, w0=w0, w1_=w1_):
                    p.wait(a, "a", ev_evo_d[grp])
                    if pgrp is not None and pgrp in ev_tr_d:
                        p.wait(a, "a", ev_tr_d[pgrp])
                    p.inc(a.activation(sTh[:, w0:w1_], sT[:, w0:w1_], AF.Identity), "act")
                    ev_add_d[("sth", grp)] = p.ev("act")

                @block.vector
                def _(v, sh=sh, dc=dc, grp=grp, pgrp=pgrp, w0=w0, w1_=w1_):
                    p.wait(v, "v", ev_add_d[("sth", grp)])
                    if pgrp is not None and pgrp in ev_tr_d:
                        p.wait(v, "v", ev_tr_d[pgrp])
                    p.inc(v.tensor_tensor(sTl[:, w0:w1_], sT[:, w0:w1_], sTh[:, w0:w1_], op=ALU.subtract), "dve")
                    ev_add_d[("stl", grp)] = p.ev("dve")

                @block.tensor
                def _(t, sh=sh, dc=dc, grp=grp, pgrp=pgrp, tiles=tiles):
                    p.waits(t, "t", [ev_add_d[("stl", grp)], ev_add_d[("sth", grp)]])
                    if pgrp is not None and ("ev", pgrp) in ev_add_d:
                        p.wait(t, "t", ev_add_d[("ev", pgrp)])
                    for i in tiles:
                        j = i % 2
                        ins = t.transpose(psT[:, j, 0, 0:128], sTh[:, i * 128:(i + 1) * 128], identb[:, :])
                        p.inc(ins, "pe")
                        ins = t.transpose(psT[:, j, 1, 0:128], sTl[:, i * 128:(i + 1) * 128], identb[:, :])
                        ev_tr_d[(sh, dc, i)] = p.inc(ins, "pe")
                    ev_tr_d[grp] = p.ev("pe")
                if hf == NH - 1:
                    ev_tr_d[(sh, dc)] = ev_tr_d[grp]

                @block.vector
                def _(v, sh=sh, dc=dc, grp=grp, pgrp=pgrp, tiles=tiles):
                    for i in tiles:
                        j = i % 2
                        p.wait(v, "v", ev_tr_d[(sh, dc, i)])
                        if pgrp is not None and (pgrp, j) in ev_add_d:
                            p.wait(v, "v", ev_add_d[(pgrp, j)])
                        v.tensor_copy(tr_sb[:, j, 0, :], psT[:, j, 0, 0:128]).then_inc(p.sem["dve"], 1)
                        p.cnt["dve"] += 1
                        ins = v.tensor_copy(tr_sb[:, j, 1, :], psT[:, j, 1, 0:128])
                        p.inc(ins, "dve")
                        ev_add_d[("evt", (sh, dc, i))] = p.ev("dve")
                    ev_add_d[("ev", grp)] = p.ev("dve")

                @block.vector
                def _(v, sh=sh, dc=dc, grp=grp, tiles=tiles):
                    p.wait(v, "v", ev_xload)
                    for i in tiles:
                        j = i % 2
                        p.wait(v, "v", ev_add_d[("evt", (sh, dc, i))])
                        a = A_stage[:, i, dc * 128:(dc + 1) * 128]
                        v.tensor_tensor(a, a, tr_sb[:, j, 0, :], op=ALU.add).then_inc(p.sem["dve"], 1)
                        p.cnt["dve"] += 1
                        p.wait(v, "v", p.ev("dve"))
                        ins = v.tensor_tensor(a, a, tr_sb[:, j, 1, :], op=ALU.add)
                        ev_add_d[(sh, dc, i)] = p.inc(ins, "dve")
                        ev_add_d[(grp, j)] = p.ev("dve")

        stq2.close()
        if c.stop_after == "p2":
            emit_dummy_out()
            stq.close()
            return nc
        stA = ExitStack()
        eA = stA.enter_context
        A_stage = eA(nc.sbuf_tensor("A_stage", [128, c.NOT, c.D], F32, side="right"))

        with (
            nc.sbuf_tensor("wo", [128, 2, c.DC, 128], BF16) as wo,
            nc.psum_tensor("psO", [128, 2, 512], F32) as psO,
            nc.psum_tensor("psT2", [128, 2, 2, 1024], BF16) as psT2,
        ):
            ev_wo = {}
            ev_xa = {}
            ev_mmo = {}
            ev_evo = {}
            ev_tr2 = {}
            ev_add = {}
            ev_adma = {}
            w_or = w_out.rearrange("(ec pp) (dc j) -> dc pp ec j", pp=128, j=128)

            ev_qst = [p.ev("dQst0"), p.ev("dQst1")]
            for sh in range(c.NSQ):
                @block.sync
                def _(s, sh=sh):
                    p.wait(s, "s", ev_attn)
                    p.waits(s, "s", ev_qst)
                    if sh >= 1:
                        p.wait(s, "s", ev_adma[sh - 1])
                    for i in range(c.NOT):
                        row = (sh * c.NOT + i) * 128
                        p.inc(s.dma_start(out=A_stage[:, i, :], in_=x_own[row:row + 128, :]), "dA", 16)
                    ev_xa[sh] = p.ev("dA")

                for dc in range(c.DC):
                    sl = dc % 2
                    prev2 = None
                    if dc >= 2:
                        prev2 = (sh, dc - 2)
                    elif sh >= 1:
                        prev2 = (sh - 1, c.DC - 2 + dc)
                    prev1 = (sh, dc - 1) if dc >= 1 else ((sh - 1, c.DC - 1) if sh >= 1 else None)

                    @block.gpsimd
                    def _(g, sh=sh, dc=dc, sl=sl, prev2=prev2):
                        p.wait(g, "g", ev_attn)
                        if prev2 in ev_mmo:
                            p.wait(g, "g", ev_mmo[prev2])
                        sem = "dWo0" if dc % 2 == 0 else "dWo1"
                        p.inc(g.dma_start(out=wo[:, sl, :, :], in_=w_or[dc]), sem, 16)
                        ev_wo[(sh, dc)] = p.ev(sem)

                    @block.tensor
                    def _(t, sh=sh, dc=dc, sl=sl, prev2=prev2):
                        p.waits(t, "t", [ev_wo[(sh, dc)], ev_attn])
                        if prev2 in ev_evo:
                            p.wait(t, "t", ev_evo[prev2])
                        for ec in range(c.DC):
                            ins = t.matmul(psO[:, sl, 0:c.SQW], wo[:, sl, ec, :],
                                           combn[:, ec, sh * c.SQW:(sh + 1) * c.SQW],
                                           start=(ec == 0), stop=(ec == c.DC - 1))
                        ev_mmo[(sh, dc)] = p.inc(ins, "pe")

                    emit_tr_add(sh, dc, sl, prev1, psO, psT2, ev_mmo, ev_evo, ev_tr2, ev_add,
                                ev_xa[sh], None)

                @block.gpsimd
                def _(g, sh=sh):
                    for i in range(c.NOT):
                        p.wait(g, "g", ev_add[(sh, c.DC - 1, i)])
                        row = (sh * c.NOT + i) * 128
                        p.inc(g.dma_start(out=a_d[row:row + 128, :], in_=A_stage[:, i, :]), "dAD", 16)
                    ev_adma[sh] = p.ev("dAD")

        if c.stop_after == "p25":
            emit_dummy_out()
            stA.close()
            stq.close()
            return nc
        stq.close()
        fence3 = [p.ev("pe"), p.ev("act"), p.ev("dve")]

        with (
            nc.sbuf_tensor("gbuf", [128, c.FC, c.SQW], BF16) as gbuf,
            nc.sbuf_tensor("w1st", [128, 2, c.DC, 128], BF16) as w1st,
            nc.sbuf_tensor("w2st", [128, 2, c.FC, 128], BF16) as w2st,
            nc.psum_tensor("psH", [128, 2, 512], F32) as psH,
            nc.psum_tensor("psO2", [128, 2, 512], F32) as psO2,
            nc.psum_tensor("psT3", [128, 2, 2, 1024], BF16) as psT3,
        ):
            w1r = w1q.rearrange("(ec pp) (fc j) -> fc pp ec j", pp=128, j=128)
            w2r = w2q.rearrange("(fc pp) (dc j) -> dc pp fc j", pp=128, j=128)
            ev_w1 = {}
            ev_w2 = {}
            ev_mmh = {}
            ev_g = {}
            ev_mm2 = {}
            ev_ev2 = {}
            ev_tr3 = {}
            ev_add3 = {}
            ev_odma = {}
            ev_xa3 = {}

            for sh in range(c.NSQ):
                @block.sync
                def _(s, sh=sh):
                    p.wait(s, "s", ev_adma[c.NSQ - 1])
                    if sh >= 1:
                        p.wait(s, "s", ev_odma[sh - 1])
                    for i in range(c.NOT):
                        row = (sh * c.NOT + i) * 128
                        p.inc(s.dma_start(out=A_stage[:, i, :], in_=a_d[row:row + 128, :]), "dA", 16)
                    ev_xa3[sh] = p.ev("dA")

                for fc in range(c.FC):
                    sl = fc % 2
                    prev2 = None
                    if fc >= 2:
                        prev2 = (sh, fc - 2)
                    elif sh >= 1:
                        prev2 = (sh - 1, c.FC - 2 + fc)

                    @block.sync
                    def _(s, sh=sh, fc=fc, sl=sl, prev2=prev2):
                        p.wait(s, "s", ev_cc1)
                        if sh == 0 and fc < 2:
                            p.waits(s, "s", fence3)
                        if prev2 in ev_mmh:
                            p.wait(s, "s", ev_mmh[prev2])
                        sem = "dF10" if fc % 2 == 0 else "dF11"
                        p.inc(s.dma_start(out=w1st[:, sl, :, :], in_=w1r[fc]), sem, 16)
                        ev_w1[(sh, fc)] = p.ev(sem)

                    @block.tensor
                    def _(t, sh=sh, fc=fc, sl=sl, prev2=prev2):
                        p.wait(t, "t", ev_w1[(sh, fc)])
                        if prev2 in ev_g:
                            p.wait(t, "t", ev_g[prev2])
                        for ec in range(c.DC):
                            ins = t.matmul(psH[:, sl, 0:c.SQW], w1st[:, sl, ec, :],
                                           xnT[:, ec, sh * c.SQW:(sh + 1) * c.SQW],
                                           start=(ec == 0), stop=(ec == c.DC - 1))
                        ev_mmh[(sh, fc)] = p.inc(ins, "pe")

                    @block.scalar
                    def _(a, sh=sh, fc=fc, sl=sl):
                        p.waits(a, "a", [ev_mmh[(sh, fc)], ev_b1c])
                        if sh >= 1 and fc == 0:
                            p.wait(a, "a", ev_mm2[(sh - 1, c.DC - 1)])
                        ins = a.activation(gbuf[:, fc, :], psH[:, sl, 0:c.SQW], AF.Gelu, bias=b1c[:, fc:fc + 1])
                        ev_g[(sh, fc)] = p.inc(ins, "act")

                for dc in range(c.DC):
                    sl = dc % 2
                    prev2 = None
                    if dc >= 2:
                        prev2 = (sh, dc - 2)
                    elif sh >= 1:
                        prev2 = (sh - 1, c.DC - 2 + dc)
                    prev1 = (sh, dc - 1) if dc >= 1 else ((sh - 1, c.DC - 1) if sh >= 1 else None)

                    @block.sync
                    def _(s, sh=sh, dc=dc, sl=sl, prev2=prev2):
                        p.wait(s, "s", ev_cc2)
                        if sh == 0 and dc < 2:
                            p.waits(s, "s", fence3)
                        if prev2 in ev_mm2:
                            p.wait(s, "s", ev_mm2[prev2])
                        sem = "dF20" if dc % 2 == 0 else "dF21"
                        for f0 in range(0, c.FC, 16):
                            f1 = min(f0 + 16, c.FC)
                            p.inc(s.dma_start(out=w2st[:, sl, f0:f1, :], in_=w2r[dc][:, f0:f1, :]), sem, 16)
                        ev_w2[(sh, dc)] = p.ev(sem)

                    @block.tensor
                    def _(t, sh=sh, dc=dc, sl=sl, prev2=prev2):
                        p.waits(t, "t", [ev_w2[(sh, dc)], ev_g[(sh, c.FC - 1)]])
                        if prev2 in ev_ev2:
                            p.wait(t, "t", ev_ev2[prev2])
                        for fc in range(c.FC):
                            ins = t.matmul(psO2[:, sl, 0:c.SQW], w2st[:, sl, fc, :], gbuf[:, fc, :],
                                           start=(fc == 0), stop=(fc == c.FC - 1))
                        ev_mm2[(sh, dc)] = p.inc(ins, "pe")

                    emit_tr_add(sh, dc, sl, prev1, psO2, psT3, ev_mm2, ev_ev2, ev_tr3, ev_add3,
                                ev_xa3[sh], b2c[:, dc:dc + 1])

                @block.gpsimd
                def _(g, sh=sh):
                    for i in range(c.NOT):
                        p.wait(g, "g", ev_add3[(sh, c.DC - 1, i)])
                        row = (sh * c.NOT + i) * 128
                        p.inc(g.dma_start(out=out_d[row:row + 128, :], in_=A_stage[:, i, :]), "dO", 16)
                    ev_odma[sh] = p.ev("dO")

            @block.gpsimd
            def _(g):
                if c.taps:
                    p.inc(g.dma_start(out=tap_g.rearrange("(fc pp) w -> pp fc w", pp=128), in_=gbuf[:, :, :]), "dO", 16)
                    p.inc(g.dma_start(out=tap_ad[:, :], in_=a_d[:, :]), "dO", 16)
                    p.inc(g.dma_start(out=tap_sth[:, :], in_=sTh[:, :]), "dO", 16)
                    p.inc(g.dma_start(out=tap_stl[:, :], in_=sTl[:, :]), "dO", 16)
                    p.inc(g.dma_start(out=tap_trsb[:, :], in_=tr_sb.rearrange("p a b j -> p (a b j)")), "dO", 16)
                    p.inc(g.dma_start(out=tap_kg[:, :], in_=k_g[:, :]), "dO", 16)
                    p.inc(g.dma_start(out=tap_vg[:, :], in_=v_g[:, :]), "dO", 16)
                    p.inc(g.dma_start(out=tap_w1q[:, :], in_=w1q[:, :]), "dO", 16)
                    p.inc(g.dma_start(out=tap_w2q[:, :], in_=w2q[:, :]), "dO", 16)
                g.wait_ge(p.sem["dO"], p.cnt["dO"])

        stA.close()

    return nc



def kernel(L0_attn_x, L0_attn_w_qkv, L0_attn_w_out, L0_ffn_w1, L0_ffn_b1, L0_ffn_w2, L0_ffn_b2):
    c = Cfg()
    x = np.ascontiguousarray(np.asarray(L0_attn_x, dtype=np.float32).reshape(c.NTOK, c.D))
    w_qkv = np.ascontiguousarray(np.asarray(L0_attn_w_qkv, dtype=np.float32))
    w_out = np.ascontiguousarray(np.asarray(L0_attn_w_out, dtype=np.float32))
    w1T = np.ascontiguousarray(np.asarray(L0_ffn_w1, dtype=np.float32).T)
    w2T = np.ascontiguousarray(np.asarray(L0_ffn_w2, dtype=np.float32).T)
    b1 = np.ascontiguousarray(np.asarray(L0_ffn_b1, dtype=np.float32))
    b2 = np.ascontiguousarray(np.asarray(L0_ffn_b2, dtype=np.float32))
    in_maps = []
    for i in range(c.NC):
        in_maps.append({
            "x_own": np.ascontiguousarray(x[i * c.T:(i + 1) * c.T]),
            "w_qkv": w_qkv,
            "w_out": w_out,
            "w1s": np.ascontiguousarray(w1T[i * c.W1R:(i + 1) * c.W1R]),
            "w2s": np.ascontiguousarray(w2T[i * c.W2R:(i + 1) * c.W2R]),
            "b1": b1,
            "b2": b2,
        })
    nc = build_program(c)
    res = run_bass_kernel_spmd(nc, in_maps, core_ids=list(range(c.NC)))
    out = np.concatenate([np.asarray(r["out"]) for r in res.results], axis=0)
    return out.reshape(c.B, c.S, c.D)


# version 08
# speedup vs baseline: 1.0198x; 1.0198x over previous
import math
import numpy as np
from contextlib import ExitStack

import concourse.bass as bass
import concourse.mybir as mybir
from concourse.bass_utils import run_bass_kernel_spmd

F32 = mybir.dt.float32
BF16 = mybir.dt.bfloat16
AF = mybir.ActivationFunctionType
ALU = mybir.AluOpType
AX = mybir.AxisListType

MAGIC = float(1.5 * 2 ** 23)
CLIP = 1.25


class Cfg:
    def __init__(self, B=2, S=4096, D=2048, H=16, DH=128, DFF=8192, NCORES=8, sqw_cap=512, stop_after=None, p25_stub=0, taps=False):
        self.B, self.S, self.D, self.H, self.DH, self.DFF, self.NC = B, S, D, H, DH, DFF, NCORES
        self.NTOK = B * S
        self.T = self.NTOK // NCORES
        self.TT = self.T // 128
        self.DC = D // 128
        self.FC = DFF // 128
        self.G = NCORES // B
        self.SR = S // self.G
        assert self.SR == self.T
        self.ST = S // 128
        self.SRT = self.SR // 128
        self.SQW = min(sqw_cap, self.T)
        self.NSQ = self.T // self.SQW
        self.NOT = self.T // self.NSQ // 128
        self.W1R = D // NCORES
        self.W2R = DFF // NCORES
        self.QW = min(2048, max(DFF, D))
        self.VW = min(512, H * DH)
        self.scale = 1.0 / math.sqrt(DH)
        self.stop_after = stop_after
        self.p25_stub = p25_stub
        self.taps = taps


class Prog:
    def __init__(self, nc):
        self.nc = nc
        self.cnt = {}
        self.sem = {}
        self.lastw = {}

    def mksems(self, st, names):
        for n in names:
            self.sem[n] = st.enter_context(self.nc.semaphore(n))
            self.cnt[n] = 0
            self.lastw[n] = {}

    def inc(self, instr, s, n=1):
        self.cnt[s] += n
        instr.then_inc(self.sem[s], n)
        return (s, self.cnt[s])

    def ev(self, s):
        return (s, self.cnt[s])

    def wait(self, eng, engname, ev):
        if ev is None:
            return
        s, v = ev
        if v <= 0:
            return
        if self.lastw[s].get(engname, 0) < v:
            eng.wait_ge(self.sem[s], v)
            self.lastw[s][engname] = v

    def waits(self, eng, engname, evs):
        for ev in evs:
            if ev is not None:
                self.wait(eng, engname, ev)


def build_program(cfg: Cfg):
    nc = bass.Bass("TRN2", target_bir_lowering=False, debug=False)
    c = cfg
    RG_BATCH = [list(range(g * c.G, (g + 1) * c.G)) for g in range(c.B)]
    RG_ALL = [list(range(c.NC))]

    x_own = nc.dram_tensor("x_own", [c.T, c.D], F32, kind="ExternalInput").ap()
    w_qkv = nc.dram_tensor("w_qkv", [c.D, 3 * c.H * c.DH], F32, kind="ExternalInput").ap()
    w_out = nc.dram_tensor("w_out", [c.H * c.DH, c.D], F32, kind="ExternalInput").ap()
    w1s = nc.dram_tensor("w1s", [c.W1R, c.DFF], F32, kind="ExternalInput").ap()
    w2s = nc.dram_tensor("w2s", [c.W2R, c.D], F32, kind="ExternalInput").ap()
    b1 = nc.dram_tensor("b1", [c.DFF], F32, kind="ExternalInput").ap()
    b2 = nc.dram_tensor("b2", [c.D], F32, kind="ExternalInput").ap()
    out_d = nc.dram_tensor("out", [c.T, c.D], F32, kind="ExternalOutput").ap()

    k_own = nc.dram_tensor("k_own", [c.H * c.DH, c.T], BF16).ap()
    v_own = nc.dram_tensor("v_own", [c.T, c.H * c.DH], BF16).ap()
    k_g = nc.dram_tensor("k_g", [c.G * c.H * c.DH, c.T], BF16).ap()
    v_g = nc.dram_tensor("v_g", [c.G * c.T, c.H * c.DH], BF16).ap()
    p_own = nc.dram_tensor("p_own", [1, 2], F32).ap()
    p_g = nc.dram_tensor("p_g", [1, 2], F32).ap()
    w1q_own = nc.dram_tensor("w1q_own", [c.W1R, c.DFF], BF16).ap()
    w2q_own = nc.dram_tensor("w2q_own", [c.W2R, c.D], BF16).ap()
    w1q = nc.dram_tensor("w1q", [c.D, c.DFF], BF16).ap()
    w2q = nc.dram_tensor("w2q", [c.DFF, c.D], BF16).ap()
    a_d = nc.dram_tensor("a_d", [c.T, c.D], F32).ap()
    if c.taps:
        tap_ad = nc.dram_tensor("tap_ad", [c.T, c.D], F32, kind="ExternalOutput").ap()
        tap_kg = nc.dram_tensor("tap_kg", [c.G * c.H * c.DH, c.T], BF16, kind="ExternalOutput").ap()
        tap_vg = nc.dram_tensor("tap_vg", [c.G * c.T, c.H * c.DH], BF16, kind="ExternalOutput").ap()
        tap_w1q = nc.dram_tensor("tap_w1q", [c.D, c.DFF], BF16, kind="ExternalOutput").ap()
        tap_w2q = nc.dram_tensor("tap_w2q", [c.DFF, c.D], BF16, kind="ExternalOutput").ap()
        tap_g = nc.dram_tensor("tap_g", [c.FC * 128, c.SQW], BF16, kind="ExternalOutput").ap()
        tap_sth = nc.dram_tensor("tap_sth", [128, c.SQW], BF16, kind="ExternalOutput").ap()
        tap_stl = nc.dram_tensor("tap_stl", [128, c.SQW], BF16, kind="ExternalOutput").ap()
        tap_trsb = nc.dram_tensor("tap_trsb", [128, 2 * 2 * 128], F32, kind="ExternalOutput").ap()
        tap_xnt = nc.dram_tensor("tap_xnt", [c.D, c.T], BF16, kind="ExternalOutput").ap()
        tap_xnt_e = nc.dram_tensor("tap_xnt_e", [c.D, c.T], BF16, kind="ExternalOutput").ap()
        tap_xnt_m = nc.dram_tensor("tap_xnt_m", [c.D, c.T], BF16, kind="ExternalOutput").ap()

    p = Prog(nc)

    st = ExitStack()
    e = st.enter_context
    with st:
        p.mksems(st, ["dB", "dX0", "dX1", "dQ0", "dQ1", "dW0", "dW1", "dWV0", "dWV1",
                      "dKst0", "dKst1", "dVst0", "dVst1", "dP", "dQst0", "dQst1",
                      "dKV0", "dKV1", "dWo0", "dWo1", "dA", "dAD", "dF10", "dF11",
                      "dF20", "dF21", "pe", "act", "dve", "pool",
                      "ccS", "ccK", "ccV", "cc1", "cc2", "dO"])

        xnT = e(nc.sbuf_tensor("xnT", [128, c.DC, c.T], BF16))
        identb = e(nc.sbuf_tensor("identb", [128, 128], BF16))
        identf = e(nc.sbuf_tensor("identf", [128, 128], F32))
        onesb_row = e(nc.sbuf_tensor("onesb_row", [1, 128], BF16))
        onesb_col = e(nc.sbuf_tensor("onesb_col", [128, 1], BF16))
        ones_row = e(nc.sbuf_tensor("ones_row", [1, 128], F32))
        ones_col = e(nc.sbuf_tensor("ones_col", [128, 1], F32))
        b1c = e(nc.sbuf_tensor("b1c", [128, c.FC], F32))
        b2c = e(nc.sbuf_tensor("b2c", [128, c.DC], F32))
        brow = e(nc.sbuf_tensor("brow", [max(c.FC, c.DC), 128], F32))
        rsc = e(nc.sbuf_tensor("rsc", [128, 2], F32))
        scl = e(nc.sbuf_tensor("scl", [1, 4], F32))
        sT = e(nc.sbuf_tensor("sT", [128, c.SQW], F32))
        sTh = e(nc.sbuf_tensor("sTh", [128, c.SQW], BF16))
        sTl = e(nc.sbuf_tensor("sTl", [128, c.SQW], BF16))
        tr_sb = e(nc.sbuf_tensor("tr_sb", [128, 2, 2, 128], F32))

        block = e(nc.Block())

        @block.gpsimd
        def _(g):
            for ident in (identb, identf):
                g.memset(ident[:, :], 0.0).then_inc(p.sem["pool"], 1)
                p.cnt["pool"] += 1
                g.wait_ge(p.sem["pool"], p.cnt["pool"])
                ins = g.affine_select(out=ident[:, :], in_=ident[:, :],
                                      compare_op=ALU.not_equal, fill=1.0,
                                      base=0, pattern=[[-1, 128]], channel_multiplier=1)
                p.inc(ins, "pool")
            p.inc(g.memset(onesb_row[:, :], 1.0), "pool")
            p.inc(g.memset(onesb_col[:, :], 1.0), "pool")
            p.inc(g.memset(ones_row[:, :], 1.0), "pool")
            p.inc(g.memset(ones_col[:, :], 1.0), "pool")
        ev_const = p.ev("pool")

        with nc.psum_tensor("psB", [128, max(c.FC, c.DC)], F32) as psB:
            @block.sync
            def _(s):
                p.inc(s.dma_start(out=brow[0:c.FC, :], in_=b1.rearrange("(a p) -> a p", p=128)), "dB", 16)
            ev_b1row = p.ev("dB")

            @block.tensor
            def _(t):
                p.waits(t, "t", [ev_b1row, ev_const])
                p.inc(t.transpose(psB[:, 0:c.FC], brow[0:c.FC, :], identf[0:c.FC, 0:c.FC]), "pe")
            ev_tb1 = p.ev("pe")

            @block.vector
            def _(v):
                p.wait(v, "v", ev_tb1)
                p.inc(v.tensor_copy(b1c[:, :], psB[:, 0:c.FC]), "dve")
            ev_b1c = p.ev("dve")

            @block.sync
            def _(s):
                p.wait(s, "s", ev_tb1)
                p.inc(s.dma_start(out=brow[0:c.DC, :], in_=b2.rearrange("(a p) -> a p", p=128)), "dB", 16)
            ev_b2row = p.ev("dB")

            @block.tensor
            def _(t):
                p.waits(t, "t", [ev_b2row, ev_b1c])
                p.inc(t.transpose(psB[:, 0:c.DC], brow[0:c.DC, :], identf[0:c.DC, 0:c.DC]), "pe")
            ev_tb2 = p.ev("pe")

            @block.vector
            def _(v):
                p.wait(v, "v", ev_tb2)
                p.inc(v.tensor_copy(b2c[:, :], psB[:, 0:c.DC]), "dve")
            ev_b2c = p.ev("dve")

        st0 = ExitStack()
        with st0:
            e0 = st0.enter_context
            xstage = e0(nc.sbuf_tensor("xstage", [128, 2, c.D], F32))
            xn_tok = e0(nc.sbuf_tensor("xn_tok", [128, 2, c.D], BF16))
            lncols = e0(nc.sbuf_tensor("lncols", [128, 8], F32))
            ln_scr = e0(nc.sbuf_tensor("ln_scr", [128, c.QW], F32))
            psTr = e0(nc.psum_tensor("psTr", [128, 2, 1024], BF16))

            ev_x = [None] * c.TT
            ev_sq = [None] * c.TT
            ev_xn = [None] * c.TT
            ev_tr = {}
            ev_evac = {}

            for tt in range(c.TT):
                sl = tt % 2

                @block.sync
                def _(s, tt=tt, sl=sl):
                    if tt >= 2:
                        p.wait(s, "s", ev_xn[tt - 2])
                    sem = "dX0" if tt % 2 == 0 else "dX1"
                    p.inc(s.dma_start(out=xstage[:, sl, :], in_=x_own[tt * 128:(tt + 1) * 128, :]), sem, 16)
                    ev_x[tt] = p.ev(sem)

                @block.vector
                def _(v, tt=tt, sl=sl):
                    p.wait(v, "v", ev_x[tt])
                    p.inc(v.tensor_reduce(lncols[:, 0:1], xstage[:, sl, :], axis=AX.X, op=ALU.add), "dve")
                ev_s1 = p.ev("dve")

                @block.vector
                def _(v, tt=tt, sl=sl):
                    p.wait(v, "v", ev_x[tt])
                    if tt >= 1:
                        p.wait(v, "v", ev_sq[tt - 1])
                    p.inc(v.tensor_tensor(ln_scr[:, 0:c.D], xstage[:, sl, :], xstage[:, sl, :], op=ALU.mult), "dve")
                    p.wait(v, "v", p.ev("dve"))
                    ins = v.tensor_reduce(lncols[:, 1:2], ln_scr[:, 0:c.D], axis=AX.X, op=ALU.add)
                    ev_sq[tt] = p.inc(ins, "dve")

                @block.vector
                def _(v, tt=tt, ev_s1=ev_s1):
                    p.waits(v, "v", [ev_sq[tt], ev_s1])
                    inv_d = 1.0 / c.D
                    p.inc(v.tensor_scalar(lncols[:, 2:3], lncols[:, 0:1], inv_d, None, op0=ALU.mult, op1=ALU.bypass), "dve")
                    p.wait(v, "v", p.ev("dve"))
                    p.inc(v.tensor_scalar(lncols[:, 3:4], lncols[:, 1:2], inv_d, 1e-5, op0=ALU.mult, op1=ALU.add), "dve")
                    p.wait(v, "v", p.ev("dve"))
                    p.inc(v.tensor_tensor(lncols[:, 4:5], lncols[:, 2:3], lncols[:, 2:3], op=ALU.mult), "dve")
                    p.wait(v, "v", p.ev("dve"))
                    p.inc(v.tensor_tensor(lncols[:, 3:4], lncols[:, 3:4], lncols[:, 4:5], op=ALU.subtract), "dve")
                ev_var = p.ev("dve")

                @block.scalar
                def _(a, tt=tt):
                    p.wait(a, "a", ev_var)
                    p.inc(a.activation(lncols[:, 5:6], lncols[:, 3:4], AF.Ln), "act")
                    p.wait(a, "a", p.ev("act"))
                    p.inc(a.activation(lncols[:, 5:6], lncols[:, 5:6], AF.Exp, scale=-0.5), "act")
                ev_rstd = p.ev("act")

                @block.vector
                def _(v, tt=tt):
                    p.wait(v, "v", ev_rstd)
                    p.inc(v.tensor_tensor(lncols[:, 6:7], lncols[:, 2:3], lncols[:, 5:6], op=ALU.mult), "dve")
                    p.wait(v, "v", p.ev("dve"))
                    p.inc(v.tensor_scalar(lncols[:, 6:7], lncols[:, 6:7], -1.0, None, op0=ALU.mult, op1=ALU.bypass), "dve")
                ev_stats = p.ev("dve")

                @block.vector
                def _(v, tt=tt, sl=sl):
                    p.wait(v, "v", ev_stats)
                    if tt >= 2:
                        p.wait(v, "v", ev_tr[(tt - 2, c.DC - 1)])
                    ins = v.tensor_scalar(xn_tok[:, sl, :], xstage[:, sl, :],
                                          lncols[:, 5:6], lncols[:, 6:7], op0=ALU.mult, op1=ALU.add)
                    ev_xn[tt] = p.inc(ins, "dve")

                for ec in range(c.DC):
                    pslot = ec % 2

                    @block.tensor
                    def _(t, tt=tt, sl=sl, ec=ec, pslot=pslot):
                        p.waits(t, "t", [ev_xn[tt], ev_const])
                        prev = None
                        if ec >= 2:
                            prev = (tt, ec - 2)
                        elif tt >= 1:
                            prev = (tt - 1, c.DC - 2 + ec)
                        if prev in ev_evac:
                            p.wait(t, "t", ev_evac[prev])
                        ins = t.transpose(psTr[:, pslot, 0:128], xn_tok[:, sl, ec * 128:(ec + 1) * 128], identb[:, :])
                        ev_tr[(tt, ec)] = p.inc(ins, "pe")

                    @block.vector
                    def _(v, tt=tt, ec=ec, pslot=pslot):
                        p.wait(v, "v", ev_tr[(tt, ec)])
                        ins = v.tensor_copy(xnT[:, ec, tt * 128:(tt + 1) * 128], psTr[:, pslot, 0:128])
                        ev_evac[(tt, ec)] = p.inc(ins, "dve")

            ev_xnT = p.ev("dve")

            qtiles = []
            for which, R, W in (("w1", c.W1R, c.DFF), ("w2", c.W2R, c.D)):
                for i in range((R + 127) // 128):
                    for j in range((W + c.QW - 1) // c.QW):
                        qtiles.append((which, i, j, min(128, R - i * 128), min(c.QW, W - j * c.QW)))

            with (
                nc.sbuf_tensor("qin", [128, 2, c.QW], F32) as qin,
                nc.sbuf_tensor("acol", [128, 3], F32) as acol,
                nc.psum_tensor("psS", [1, 2], F32) as psS,
            ):
                ev_qin = {}
                ev_abs = {}

                @block.vector
                def _(v):
                    p.inc(v.memset(acol[:, 1:3], 0.0), "dve")
                ev_az = p.ev("dve")

                for idx, (which, i, j, rows, w) in enumerate(qtiles):
                    sl = idx % 2
                    src = w1s if which == "w1" else w2s

                    @block.sync
                    def _(s, idx=idx, sl=sl, src=src, i=i, j=j, rows=rows, w=w):
                        if idx >= 2:
                            p.wait(s, "s", ev_abs[idx - 2])
                        sem = "dQ0" if idx % 2 == 0 else "dQ1"
                        p.inc(s.dma_start(out=qin[0:rows, sl, 0:w],
                                          in_=src[i * 128:i * 128 + rows, j * c.QW:j * c.QW + w]), sem, 16)
                        ev_qin[idx] = p.ev(sem)

                    @block.vector
                    def _(v, idx=idx, sl=sl, rows=rows, w=w):
                        p.wait(v, "v", ev_qin[idx])
                        if idx >= 1:
                            p.wait(v, "v", ev_abs[idx - 1])
                            p.wait(v, "v", p.ev("dve"))
                        ins = v.tensor_reduce(acol[0:rows, 0:1], qin[0:rows, sl, 0:w], axis=AX.X,
                                              op=ALU.add, apply_absolute_value=True)
                        ev_abs[idx] = p.inc(ins, "dve")

                    @block.vector
                    def _(v, idx=idx, which=which, rows=rows):
                        p.waits(v, "v", [ev_abs[idx], ev_az])
                        col = 1 if which == "w1" else 2
                        p.wait(v, "v", p.ev("dve"))
                        ins = v.tensor_tensor(acol[0:rows, col:col + 1], acol[0:rows, col:col + 1],
                                              acol[0:rows, 0:1], op=ALU.add)
                        p.inc(ins, "dve")
                ev_acc = p.ev("dve")

                @block.tensor
                def _(t):
                    p.waits(t, "t", [ev_acc, ev_const])
                    p.inc(t.matmul(psS[0:1, 0:1], ones_col[:, :], acol[:, 1:2], start=True, stop=True), "pe")
                    p.inc(t.matmul(psS[0:1, 1:2], ones_col[:, :], acol[:, 2:3], start=True, stop=True), "pe")
                ev_psS = p.ev("pe")

                @block.vector
                def _(v):
                    p.wait(v, "v", ev_psS)
                    p.inc(v.tensor_copy(scl[0:1, 0:2], psS[0:1, 0:2]), "dve")
                ev_partial = p.ev("dve")

        def emit_dummy_out():
            dummy = nc.dram_tensor("dummy_src", [c.T, c.D], F32).ap()
            @block.gpsimd
            def _(g):
                p.waits(g, "g", [p.ev("pe"), p.ev("act"), p.ev("dve")])
                p.inc(g.dma_start(out=out_d[:, :], in_=dummy[:, :]), "dO", 16)
                g.wait_ge(p.sem["dO"], p.cnt["dO"])

        if c.stop_after == "p0":
            emit_dummy_out()
            return nc

        stq = ExitStack()
        eq = stq.enter_context
        qT = eq(nc.sbuf_tensor("qT", [128, c.H, c.T], BF16))
        combn = eq(nc.sbuf_tensor("combn", [128, c.H, c.T], BF16))

        fence0 = [p.ev("pe"), p.ev("act"), p.ev("dve")]
        st1 = ExitStack()
        with st1:
            e1 = st1.enter_context
            wst = e1(nc.sbuf_tensor("wst", [128, 2, c.DC, 128], BF16))
            wvm = e1(nc.sbuf_tensor("wvm", [128, 2, c.DC, c.VW], BF16))
            kstage = e1(nc.sbuf_tensor("kstage", [128, 2, c.T], BF16))
            vstage = e1(nc.sbuf_tensor("vstage", [128, 2, c.VW], BF16))
            psQK = e1(nc.psum_tensor("psQK", [128, 2, c.NSQ, 512], F32))
            psV = e1(nc.psum_tensor("psV", [128, 2, 512], F32))

            NQK = 2 * c.H
            ev_w = [None] * NQK
            ev_mmqk = [None] * NQK
            ev_evqk = [None] * NQK
            ev_kdma = [None] * c.H
            w_qkv_r = w_qkv.rearrange("(ec pp) (hc j) -> hc pp ec j", pp=128, j=128)

            for oc in range(NQK):
                sl = oc % 2

                @block.gpsimd
                def _(g, oc=oc, sl=sl):
                    if oc < 2:
                        p.waits(g, "g", fence0)
                    else:
                        p.wait(g, "g", ev_mmqk[oc - 2])
                    sem = "dW0" if oc % 2 == 0 else "dW1"
                    p.inc(g.dma_start(out=wst[:, sl, :, :], in_=w_qkv_r[oc]), sem, 16)
                    ev_w[oc] = p.ev(sem)

                @block.tensor
                def _(t, oc=oc, sl=sl):
                    p.waits(t, "t", [ev_w[oc], ev_xnT])
                    if oc >= 2:
                        p.wait(t, "t", ev_evqk[oc - 2])
                    for sq in range(c.NSQ):
                        for ec in range(c.DC):
                            ins = t.matmul(psQK[:, sl, sq, 0:c.SQW],
                                           wst[:, sl, ec, :],
                                           xnT[:, ec, sq * c.SQW:(sq + 1) * c.SQW],
                                           start=(ec == 0), stop=(ec == c.DC - 1))
                    ev_mmqk[oc] = p.inc(ins, "pe")

                if oc < c.H:
                    @block.vector
                    def _(v, oc=oc, sl=sl):
                        p.wait(v, "v", ev_mmqk[oc])
                        ins = v.tensor_copy(qT[:, oc, :], psQK[:, sl, :, 0:c.SQW])
                        ev_evqk[oc] = p.inc(ins, "dve")
                else:
                    h = oc - c.H

                    @block.vector
                    def _(v, oc=oc, sl=sl, h=h):
                        p.wait(v, "v", ev_mmqk[oc])
                        if h >= 2:
                            p.wait(v, "v", ev_kdma[h - 2])
                        ins = v.tensor_copy(kstage[:, h % 2, :], psQK[:, sl, :, 0:c.SQW])
                        ev_evqk[oc] = p.inc(ins, "dve")

                    @block.gpsimd
                    def _(g, h=h):
                        p.wait(g, "g", ev_evqk[c.H + h])
                        ins = g.dma_start(out=k_own[h * 128:(h + 1) * 128, :], in_=kstage[:, h % 2, :])
                        ev_kdma[h] = p.inc(ins, "dKst0" if h % 2 == 0 else "dKst1", 16)

            ev_qT = ev_evqk[c.H - 1]

            NVG = (c.H * c.DH) // c.VW
            ev_wv = [None] * NVG
            ev_mmv = {}
            ev_evv = {}
            ev_vdma = [None] * c.TT
            w_qkv_v = w_qkv.rearrange("(ec pp) j -> pp ec j", pp=128)

            ev_vst = {}
            for vg in range(NVG):
                sl = vg % 2

                @block.gpsimd
                def _(g, vg=vg, sl=sl):
                    if vg < 2:
                        p.waits(g, "g", fence0)
                    else:
                        p.wait(g, "g", ev_mmv[(vg - 2, c.TT - 1)])
                    base = 2 * c.H * c.DH
                    sem = "dWV0" if vg % 2 == 0 else "dWV1"
                    p.inc(g.dma_start(out=wvm[:, sl, :, :],
                                      in_=w_qkv_v[:, :, base + vg * c.VW: base + (vg + 1) * c.VW]), sem, 16)
                    ev_wv[vg] = p.ev(sem)

                for tt in range(c.TT):
                    k = vg * c.TT + tt
                    ksl = k % 2

                    @block.tensor
                    def _(t, vg=vg, sl=sl, tt=tt):
                        p.waits(t, "t", [ev_wv[vg], ev_xnT])
                        prev = None
                        if tt >= 2:
                            prev = (vg, tt - 2)
                        elif vg >= 1:
                            prev = (vg - 1, c.TT - 2 + tt) if c.TT >= 2 else (vg - 1, tt)
                        if prev in ev_evv:
                            p.wait(t, "t", ev_evv[prev])
                        for ec in range(c.DC):
                            ins = t.matmul(psV[:, tt % 2, 0:c.VW], xnT[:, ec, tt * 128:(tt + 1) * 128],
                                           wvm[:, sl, ec, :], start=(ec == 0), stop=(ec == c.DC - 1))
                        ev_mmv[(vg, tt)] = p.inc(ins, "pe")

                    @block.vector
                    def _(v, vg=vg, tt=tt, k=k, ksl=ksl):
                        p.wait(v, "v", ev_mmv[(vg, tt)])
                        if k >= 2:
                            p.wait(v, "v", ev_vst[k - 2])
                        ins = v.tensor_copy(vstage[:, ksl, 0:c.VW], psV[:, tt % 2, 0:c.VW])
                        ev_evv[(vg, tt)] = p.inc(ins, "dve")

                    @block.gpsimd
                    def _(g, vg=vg, tt=tt, k=k, ksl=ksl):
                        p.wait(g, "g", ev_evv[(vg, tt)])
                        ins = g.dma_start(out=v_own[tt * 128:(tt + 1) * 128, vg * c.VW:(vg + 1) * c.VW],
                                          in_=vstage[:, ksl, 0:c.VW])
                        ev_vst[k] = p.inc(ins, "dVst0" if ksl == 0 else "dVst1", 16)

            for tt in range(c.TT):
                ev_vdma[tt] = ev_vst[(NVG - 1) * c.TT + tt]

            @block.gpsimd
            def _(g):
                p.wait(g, "g", ev_kdma[c.H - 1])
                if c.H >= 2:
                    p.wait(g, "g", ev_kdma[c.H - 2])
                ins = g.collective_compute("AllGather", ALU.bypass, replica_groups=RG_BATCH,
                                           ins=[k_own.opt()], outs=[k_g.opt()])
                p.inc(ins, "ccK")
                p.wait(g, "g", ev_vdma[c.TT - 1])
                if c.TT >= 2:
                    p.wait(g, "g", ev_vdma[c.TT - 2])
                ins = g.collective_compute("AllGather", ALU.bypass, replica_groups=RG_BATCH,
                                           ins=[v_own.opt()], outs=[v_g.opt()])
                p.inc(ins, "ccV")
            ev_ccK = p.ev("ccK")
            ev_ccV = p.ev("ccV")

        @block.gpsimd
        def _(g):
            p.wait(g, "g", ev_partial)
            p.inc(g.dma_start(out=p_own[:, :], in_=scl[0:1, 0:2]), "dP", 16)
            p.wait(g, "g", p.ev("dP"))
            ins = g.collective_compute("AllReduce", ALU.add, replica_groups=RG_ALL,
                                       ins=[p_own.opt()], outs=[p_g.opt()])
            p.inc(ins, "ccS")
            p.wait(g, "g", p.ev("ccS"))
            p.inc(g.dma_start(out=scl[0:1, 2:4], in_=p_g[:, :]), "dP", 16)
        ev_pg = p.ev("dP")

        with nc.psum_tensor("psBc", [128, 2], F32) as psBc:
            @block.vector
            def _(v):
                p.wait(v, "v", ev_pg)
                n1 = 1.0 / (c.D * c.DFF)
                p.inc(v.tensor_scalar(scl[0:1, 2:3], scl[0:1, 2:3], n1, 1e-8, op0=ALU.mult, op1=ALU.max), "dve")
                p.wait(v, "v", p.ev("dve"))
                p.inc(v.tensor_scalar(scl[0:1, 3:4], scl[0:1, 3:4], n1, 1e-8, op0=ALU.mult, op1=ALU.max), "dve")
                p.wait(v, "v", p.ev("dve"))
                p.inc(v.reciprocal(scl[0:1, 2:4], scl[0:1, 2:4]), "dve")
            ev_rs = p.ev("dve")

            @block.tensor
            def _(t):
                p.waits(t, "t", [ev_rs, ev_const])
                p.inc(t.matmul(psBc[:, 0:2], ones_row[0:1, :], scl[0:1, 2:4], start=True, stop=True), "pe")
            ev_bc = p.ev("pe")

            @block.vector
            def _(v):
                p.wait(v, "v", ev_bc)
                p.inc(v.tensor_copy(rsc[:, 0:2], psBc[:, 0:2]), "dve")
            ev_rsc = p.ev("dve")


        stq2 = ExitStack()
        if True:
            qin2 = stq2.enter_context(nc.sbuf_tensor("qin2", [128, 2, c.QW], F32, side="right"))
            qout = stq2.enter_context(nc.sbuf_tensor("qout", [128, 2, c.QW], BF16, side="right"))
            fence1 = [p.ev("pe"), p.ev("dve"), p.ev("act"), ev_vdma[c.TT - 1], ev_vdma[c.TT - 2] if c.TT >= 2 else None]
            ev_qdma = {}
            ev_qv = {}
            n1t = len([1 for q in qtiles if q[0] == "w1"])
            for idx, (which, i, j, rows, w) in enumerate(qtiles):
                sl = idx % 2
                src = w1s if which == "w1" else w2s
                dst = w1q_own if which == "w1" else w2q_own
                rcol = 0 if which == "w1" else 1

                @block.sync
                def _(s, idx=idx, sl=sl, src=src, i=i, j=j, rows=rows, w=w):
                    if idx < 2:
                        p.waits(s, "s", fence1)
                    else:
                        p.wait(s, "s", ev_qv[idx - 2])
                    sem = "dQ0" if idx % 2 == 0 else "dQ1"
                    p.inc(s.dma_start(out=qin2[0:rows, sl, 0:w],
                                      in_=src[i * 128:i * 128 + rows, j * c.QW:j * c.QW + w]), sem, 16)
                    ev_qdma[idx] = p.ev(sem)

                @block.vector
                def _(v, idx=idx, sl=sl, rows=rows, w=w, rcol=rcol):
                    p.waits(v, "v", [ev_qdma[idx], ev_rsc])
                    a = qin2[0:rows, sl, 0:w]
                    p.inc(v.tensor_scalar(a, a, rsc[0:rows, rcol:rcol + 1], CLIP, op0=ALU.mult, op1=ALU.min), "dve")
                    p.wait(v, "v", p.ev("dve"))
                    p.inc(v.tensor_scalar(a, a, -CLIP, MAGIC, op0=ALU.max, op1=ALU.add), "dve")
                    p.wait(v, "v", p.ev("dve"))
                    if idx >= 2:
                        p.wait(v, "v", ev_qdma.get(("out", idx - 2)))
                    ins = v.tensor_scalar(qout[0:rows, sl, 0:w], a, MAGIC, None, op0=ALU.subtract, op1=ALU.bypass)
                    ev_qv[idx] = p.inc(ins, "dve")

                @block.gpsimd
                def _(g, idx=idx, sl=sl, dst=dst, i=i, j=j, rows=rows, w=w):
                    p.wait(g, "g", ev_qv[idx])
                    ins = g.dma_start(out=dst[i * 128:i * 128 + rows, j * c.QW:j * c.QW + w],
                                      in_=qout[0:rows, sl, 0:w])
                    ev_qdma[("out", idx)] = p.inc(ins, "dQst0" if idx % 2 == 0 else "dQst1", 16)

            @block.gpsimd
            def _(g):
                p.wait(g, "g", ev_qdma[("out", n1t - 1)])
                if n1t >= 2:
                    p.wait(g, "g", ev_qdma[("out", n1t - 2)])
                ins = g.collective_compute("AllGather", ALU.bypass, replica_groups=RG_ALL,
                                           ins=[w1q_own.opt()], outs=[w1q.opt()])
                p.inc(ins, "cc1")
                p.wait(g, "g", ev_qdma[("out", len(qtiles) - 1)])
                if len(qtiles) >= 2:
                    p.wait(g, "g", ev_qdma[("out", len(qtiles) - 2)])
                ins = g.collective_compute("AllGather", ALU.bypass, replica_groups=RG_ALL,
                                           ins=[w2q_own.opt()], outs=[w2q.opt()])
                p.inc(ins, "cc2")
        ev_cc1 = p.ev("cc1")
        ev_cc2 = p.ev("cc2")

        if c.taps:
            @block.gpsimd
            def _(g):
                p.wait(g, "g", ev_xnT)
                p.inc(g.dma_start(out=tap_xnt_e.rearrange("(ec pp) t -> pp ec t", pp=128), in_=xnT[:, :, :]), "dO", 16)

        if c.stop_after == "p1":
            emit_dummy_out()
            stq2.close()
            stq.close()
            return nc

        st2 = ExitStack()
        with st2:
            e2 = st2.enter_context
            kbuf = e2(nc.sbuf_tensor("kbuf", [128, 2, c.G, c.SR], BF16))
            vbuf = e2(nc.sbuf_tensor("vbuf", [128, 2, c.ST, c.DH], BF16))
            Ebuf = e2(nc.sbuf_tensor("Ebuf", [128, 2, 2, c.NSQ, c.SQW], BF16, side="right"))
            Lbuf = e2(nc.sbuf_tensor("Lbuf", [128, 2, 2, c.NSQ, c.SQW], BF16, side="right"))
            Zp = e2(nc.sbuf_tensor("Zp", [128, c.NSQ, c.SQW], BF16))
            rzb = e2(nc.sbuf_tensor("rzb", [1, c.NSQ, c.SQW], F32))
            rzbh = e2(nc.sbuf_tensor("rzbh", [1, c.NSQ, c.SQW], BF16))
            rzbc = e2(nc.sbuf_tensor("rzbc", [128, c.NSQ, c.SQW], BF16))
            psL = e2(nc.psum_tensor("psL", [128, 2, c.NSQ, 512], F32))
            psC = e2(nc.psum_tensor("psC", [128, c.NSQ, 512], F32))
            psZ = e2(nc.psum_tensor("psZ", [1, c.NSQ, 512], F32))

            ev_kv = [None] * c.H
            ev_exp = {}
            ev_mmL = {}
            ev_mmC = {}
            ev_zadd = {}
            ev_comb = [None] * c.H
            ev_rzc = [None] * c.H
            ev_onemm = [None] * c.H
            ev_bcmm = [None] * c.H

            k_gr = k_g.rearrange("(r hc pp) t -> r hc pp t", r=c.G, pp=128)
            v_gr = v_g.rearrange("(r o pp) (hc dv) -> r hc pp o dv", r=c.G, pp=128, dv=c.DH)

            def emit_head_dma(h):
                sl = h % 2

                @block.sync
                def _(s, h=h, sl=sl):
                    p.waits(s, "s", [ev_ccK, ev_ccV])
                    if h >= 2:
                        p.wait(s, "s", ev_mmL[(h - 2, c.ST - 1)])
                        p.wait(s, "s", ev_mmC[(h - 2, c.ST - 1)])
                    sem = "dKV0" if h % 2 == 0 else "dKV1"
                    for r in range(c.G):
                        p.inc(s.dma_start(out=kbuf[:, sl, r, :], in_=k_gr[r, h]), sem, 16)
                        p.inc(s.dma_start(out=vbuf[:, sl, r * c.SRT:(r + 1) * c.SRT, :], in_=v_gr[r, h]), sem, 16)
                    ev_kv[h] = p.ev(sem)

            emit_head_dma(0)
            emit_head_dma(1)

            ev_evL = {}
            ev_zaddp = {}
            NG = c.ST // 2
            for h in range(c.H):
                sl = h % 2
                for g in range(NG):
                    gsl = g % 2
                    ts_ = (2 * g, 2 * g + 1)

                    for t in ts_:
                        @block.tensor
                        def _(tns, h=h, sl=sl, t=t):
                            p.waits(tns, "t", [ev_kv[h], ev_qT])
                            if t >= 2:
                                p.wait(tns, "t", ev_evL[(h, t - 2)])
                            elif h >= 1:
                                p.wait(tns, "t", ev_evL[(h - 1, c.ST - 2 + t)])
                                if t == 0:
                                    p.wait(tns, "t", ev_comb[h - 1])
                            r, col = t // c.SRT, (t % c.SRT) * 128
                            for sq in range(c.NSQ):
                                ins = tns.matmul(psL[:, t % 2, sq, 0:c.SQW], kbuf[:, sl, r, col:col + 128],
                                                 qT[:, h, sq * c.SQW:(sq + 1) * c.SQW], start=True, stop=True)
                            ev_mmL[(h, t)] = p.inc(ins, "pe")

                    @block.vector
                    def _(v, h=h, g=g, gsl=gsl, ts_=ts_):
                        if g >= 2:
                            p.wait(v, "v", ev_exp[(h, g - 2)])
                        elif h >= 1:
                            p.wait(v, "v", ev_exp[(h - 1, NG - 2 + g)])
                        for t in ts_:
                            p.wait(v, "v", ev_mmL[(h, t)])
                            for sq in range(c.NSQ):
                                ins = v.tensor_copy(Lbuf[:, gsl, t - 2 * g, sq, :], psL[:, t % 2, sq, 0:c.SQW])
                                p.inc(ins, "dve")
                            ev_evL[(h, t)] = p.ev("dve")

                    @block.scalar
                    def _(a, h=h, g=g, gsl=gsl, ts_=ts_):
                        p.wait(a, "a", ev_evL[(h, ts_[-1])])
                        pg = (h, g - 2) if g >= 2 else ((h - 1, NG - 2 + g) if h >= 1 else None)
                        if pg is not None:
                            pts = (2 * pg[1], 2 * pg[1] + 1)
                            p.waits(a, "a", [ev_mmC.get((pg[0], pts[1])), ev_zadd.get((pg[0], pts[1])),
                                             ev_zaddp.get((pg[0], pts[1]))])
                        if c.p25_stub == 7:
                            for j in range(2):
                                ins = a.activation(Ebuf[:, gsl, j, :, :], Lbuf[:, gsl, j, :, :], AF.Exp, scale=c.scale)
                                if j == 0:
                                    p.inc(ins, "act")
                            ev_exp[(h, g)] = p.inc(ins, "act")
                        else:
                            ins = a.activation(Ebuf[:, gsl, :, :, :], Lbuf[:, gsl, :, :, :], AF.Exp, scale=c.scale)
                            ev_exp[(h, g)] = p.inc(ins, "act")

                    for t in ts_:
                        @block.tensor
                        def _(tns, h=h, sl=sl, t=t, g=g, gsl=gsl):
                            p.wait(tns, "t", ev_exp[(h, g)])
                            for sq in range(c.NSQ):
                                ins = tns.matmul(psC[:, sq, 0:c.SQW], vbuf[:, sl, t, :],
                                                 Ebuf[:, gsl, t - 2 * g, sq, :],
                                                 start=(t == 0), stop=(t == c.ST - 1))
                            ev_mmC[(h, t)] = p.inc(ins, "pe")

                    @block.vector
                    def _(v, h=h, g=g, gsl=gsl, ts_=ts_):
                        p.wait(v, "v", ev_exp[(h, g)])
                        if g == 0 and h >= 1:
                            p.wait(v, "v", ev_onemm[h - 1])
                        for t in ts_:
                            if t == 0:
                                ins = v.tensor_copy(Zp[:, 0, :], Ebuf[:, gsl, 0, 0, :])
                            else:
                                p.wait(v, "v", p.ev("dve"))
                                ins = v.tensor_tensor(Zp[:, 0, :], Zp[:, 0, :], Ebuf[:, gsl, t - 2 * g, 0, :], op=ALU.add)
                            ev_zadd[(h, t)] = p.inc(ins, "dve")

                    if c.NSQ > 1:
                        @block.gpsimd
                        def _(gp, h=h, g=g, gsl=gsl, ts_=ts_):
                            p.wait(gp, "g", ev_exp[(h, g)])
                            if g == 0 and h >= 1:
                                p.wait(gp, "g", ev_onemm[h - 1])
                            for t in ts_:
                                for sq in range(1, c.NSQ):
                                    if t == 0:
                                        ins = gp.tensor_copy(Zp[:, sq, :], Ebuf[:, gsl, 0, sq, :])
                                    else:
                                        p.wait(gp, "g", p.ev("pool"))
                                        ins = gp.tensor_tensor(Zp[:, sq, :], Zp[:, sq, :],
                                                               Ebuf[:, gsl, t - 2 * g, sq, :], op=ALU.add)
                                    ev_zaddp[(h, t)] = p.inc(ins, "pool")
                    else:
                        ev_zaddp[(h, 2 * g)] = None
                        ev_zaddp[(h, 2 * g + 1)] = None

                if h + 2 < c.H:
                    emit_head_dma(h + 2)

                @block.tensor
                def _(tns, h=h):
                    p.wait(tns, "t", ev_zadd[(h, c.ST - 1)])
                    if ev_zaddp.get((h, c.ST - 1)) is not None:
                        p.wait(tns, "t", ev_zaddp[(h, c.ST - 1)])
                    if h >= 1:
                        p.wait(tns, "t", ev_rzc[h - 1])
                    for sq in range(c.NSQ):
                        ins = tns.matmul(psZ[0:1, sq, 0:c.SQW], onesb_col[:, :], Zp[:, sq, :], start=True, stop=True)
                    ev_onemm[h] = p.inc(ins, "pe")

                @block.vector
                def _(v, h=h):
                    p.wait(v, "v", ev_onemm[h])
                    ins = v.reciprocal(rzb[0:1, :, :], psZ[0:1, :, 0:c.SQW])
                    p.inc(ins, "dve")
                    p.wait(v, "v", p.ev("dve"))
                    ins = v.tensor_copy(rzbh[0:1, :, :], rzb[0:1, :, :])
                    ev_rzc[h] = p.inc(ins, "dve")

                @block.tensor
                def _(tns, h=h):
                    p.waits(tns, "t", [ev_rzc[h], ev_evL[(h, c.ST - 2)], ev_evL[(h, c.ST - 1)]])
                    for sq in range(c.NSQ):
                        ins = tns.matmul(psL[:, 0, sq, 0:c.SQW], onesb_row[0:1, :], rzbh[0:1, sq, :], start=True, stop=True)
                    ev_bcmm[h] = p.inc(ins, "pe")

                @block.vector
                def _(v, h=h):
                    p.wait(v, "v", ev_bcmm[h])
                    p.inc(v.tensor_copy(rzbc[:, :, :], psL[:, 0, :, 0:c.SQW]), "dve")
                    p.wait(v, "v", p.ev("dve"))
                    p.wait(v, "v", ev_mmC[(h, c.ST - 1)])
                    for sq in range(c.NSQ):
                        if sq:
                            p.wait(v, "v", p.ev("dve"))
                        ins = v.tensor_tensor(combn[:, h, sq * c.SQW:(sq + 1) * c.SQW],
                                              psC[:, sq, 0:c.SQW], rzbc[:, sq, :], op=ALU.mult)
                    ev_comb[h] = p.inc(ins, "dve")

            ev_attn = ev_comb[c.H - 1]

        def emit_tr_add(sh, dc, sl, prev1, psIn, psT, ev_mm_d, ev_evo_d, ev_tr_d, ev_add_d,
                        ev_xload, bcol):
            NH = max(1, c.NOT // 2)
            for hf in range(NH):
                grp = (sh, dc, "g", hf)
                pgrp = (sh, dc, "g", hf - 1) if hf >= 1 else \
                       ((prev1[0], prev1[1], "g", NH - 1) if prev1 is not None else None)
                tiles = [i for i in range(2 * hf, min(2 * hf + 2, c.NOT))]
                w0, w1_ = tiles[0] * 128, (tiles[-1] + 1) * 128

                @block.vector
                def _(v, sh=sh, dc=dc, sl=sl, grp=grp, pgrp=pgrp, w0=w0, w1_=w1_, hf=hf):
                    p.wait(v, "v", ev_mm_d[(sh, dc)])
                    if bcol is not None:
                        p.wait(v, "v", ev_b2c)
                    if pgrp is not None and ("sth", pgrp) in ev_add_d:
                        p.wait(v, "v", ev_add_d[("sth", pgrp)])
                    if bcol is not None:
                        ins = v.tensor_scalar(sT[:, w0:w1_], psIn[:, sl, w0:w1_], bcol, None,
                                              op0=ALU.add, op1=ALU.bypass)
                    else:
                        ins = v.tensor_copy(sT[:, w0:w1_], psIn[:, sl, w0:w1_])
                    p.inc(ins, "dve")
                    ev_evo_d[grp] = p.ev("dve")
                if hf == NH - 1:
                    ev_evo_d[(sh, dc)] = ev_evo_d[grp]

                @block.vector
                def _(v, sh=sh, dc=dc, grp=grp, pgrp=pgrp, w0=w0, w1_=w1_):
                    p.wait(v, "v", ev_evo_d[grp])
                    if pgrp is not None and pgrp in ev_tr_d:
                        p.wait(v, "v", ev_tr_d[pgrp])
                    p.inc(v.tensor_copy(sTh[:, w0:w1_], sT[:, w0:w1_]), "dve")
                    ev_add_d[("sth", grp)] = p.ev("dve")

                @block.vector
                def _(v, sh=sh, dc=dc, grp=grp, pgrp=pgrp, w0=w0, w1_=w1_):
                    p.wait(v, "v", ev_add_d[("sth", grp)])
                    if pgrp is not None and pgrp in ev_tr_d:
                        p.wait(v, "v", ev_tr_d[pgrp])
                    p.inc(v.tensor_tensor(sTl[:, w0:w1_], sT[:, w0:w1_], sTh[:, w0:w1_], op=ALU.subtract), "dve")
                    ev_add_d[("stl", grp)] = p.ev("dve")

                @block.tensor
                def _(t, sh=sh, dc=dc, grp=grp, pgrp=pgrp, tiles=tiles):
                    p.waits(t, "t", [ev_add_d[("stl", grp)], ev_add_d[("sth", grp)]])
                    if pgrp is not None and ("ev", pgrp) in ev_add_d:
                        p.wait(t, "t", ev_add_d[("ev", pgrp)])
                    for i in tiles:
                        j = i % 2
                        ins = t.transpose(psT[:, j, 0, 0:128], sTh[:, i * 128:(i + 1) * 128], identb[:, :])
                        p.inc(ins, "pe")
                        ins = t.transpose(psT[:, j, 1, 0:128], sTl[:, i * 128:(i + 1) * 128], identb[:, :])
                        ev_tr_d[(sh, dc, i)] = p.inc(ins, "pe")
                    ev_tr_d[grp] = p.ev("pe")
                if hf == NH - 1:
                    ev_tr_d[(sh, dc)] = ev_tr_d[grp]

                @block.vector
                def _(v, sh=sh, dc=dc, grp=grp, pgrp=pgrp, tiles=tiles):
                    for i in tiles:
                        j = i % 2
                        p.wait(v, "v", ev_tr_d[(sh, dc, i)])
                        if pgrp is not None and (pgrp, j) in ev_add_d:
                            p.wait(v, "v", ev_add_d[(pgrp, j)])
                        v.tensor_copy(tr_sb[:, j, 0, :], psT[:, j, 0, 0:128]).then_inc(p.sem["dve"], 1)
                        p.cnt["dve"] += 1
                        ins = v.tensor_copy(tr_sb[:, j, 1, :], psT[:, j, 1, 0:128])
                        p.inc(ins, "dve")
                        ev_add_d[("evt", (sh, dc, i))] = p.ev("dve")
                    ev_add_d[("ev", grp)] = p.ev("dve")

                @block.vector
                def _(v, sh=sh, dc=dc, grp=grp, tiles=tiles):
                    p.wait(v, "v", ev_xload)
                    for i in tiles:
                        j = i % 2
                        p.wait(v, "v", ev_add_d[("evt", (sh, dc, i))])
                        a = A_stage[:, i, dc * 128:(dc + 1) * 128]
                        v.tensor_tensor(a, a, tr_sb[:, j, 0, :], op=ALU.add).then_inc(p.sem["dve"], 1)
                        p.cnt["dve"] += 1
                        p.wait(v, "v", p.ev("dve"))
                        ins = v.tensor_tensor(a, a, tr_sb[:, j, 1, :], op=ALU.add)
                        ev_add_d[(sh, dc, i)] = p.inc(ins, "dve")
                        ev_add_d[(grp, j)] = p.ev("dve")

        if c.taps:
            @block.gpsimd
            def _(g):
                p.wait(g, "g", ev_attn)
                p.inc(g.dma_start(out=tap_xnt_m.rearrange("(ec pp) t -> pp ec t", pp=128), in_=xnT[:, :, :]), "dO", 16)

        stq2.close()
        if c.stop_after == "p2":
            emit_dummy_out()
            stq.close()
            return nc
        stA = ExitStack()
        eA = stA.enter_context
        A_stage = eA(nc.sbuf_tensor("A_stage", [128, c.NOT, c.D], F32, side="right"))

        with (
            nc.sbuf_tensor("wo", [128, 2, c.DC, 128], BF16) as wo,
            nc.psum_tensor("psO", [128, 2, 512], F32) as psO,
            nc.psum_tensor("psT2", [128, 2, 2, 1024], BF16) as psT2,
        ):
            ev_wo = {}
            ev_xa = {}
            ev_mmo = {}
            ev_evo = {}
            ev_tr2 = {}
            ev_add = {}
            ev_adma = {}
            w_or = w_out.rearrange("(ec pp) (dc j) -> dc pp ec j", pp=128, j=128)

            ev_qst = [p.ev("dQst0"), p.ev("dQst1")]
            for sh in range(c.NSQ):
                @block.sync
                def _(s, sh=sh):
                    p.wait(s, "s", ev_attn)
                    p.waits(s, "s", ev_qst)
                    if sh >= 1:
                        p.wait(s, "s", ev_adma[sh - 1])
                    for i in range(c.NOT):
                        row = (sh * c.NOT + i) * 128
                        p.inc(s.dma_start(out=A_stage[:, i, :], in_=x_own[row:row + 128, :]), "dA", 16)
                    ev_xa[sh] = p.ev("dA")

                for dc in range(c.DC):
                    sl = dc % 2
                    prev2 = None
                    if dc >= 2:
                        prev2 = (sh, dc - 2)
                    elif sh >= 1:
                        prev2 = (sh - 1, c.DC - 2 + dc)
                    prev1 = (sh, dc - 1) if dc >= 1 else ((sh - 1, c.DC - 1) if sh >= 1 else None)

                    @block.gpsimd
                    def _(g, sh=sh, dc=dc, sl=sl, prev2=prev2):
                        p.wait(g, "g", ev_attn)
                        if prev2 in ev_mmo:
                            p.wait(g, "g", ev_mmo[prev2])
                        sem = "dWo0" if dc % 2 == 0 else "dWo1"
                        p.inc(g.dma_start(out=wo[:, sl, :, :], in_=w_or[dc]), sem, 16)
                        ev_wo[(sh, dc)] = p.ev(sem)

                    @block.tensor
                    def _(t, sh=sh, dc=dc, sl=sl, prev2=prev2):
                        p.waits(t, "t", [ev_wo[(sh, dc)], ev_attn])
                        if prev2 in ev_evo:
                            p.wait(t, "t", ev_evo[prev2])
                        for ec in range(c.DC):
                            ins = t.matmul(psO[:, sl, 0:c.SQW], wo[:, sl, ec, :],
                                           combn[:, ec, sh * c.SQW:(sh + 1) * c.SQW],
                                           start=(ec == 0), stop=(ec == c.DC - 1))
                        ev_mmo[(sh, dc)] = p.inc(ins, "pe")

                    emit_tr_add(sh, dc, sl, prev1, psO, psT2, ev_mmo, ev_evo, ev_tr2, ev_add,
                                ev_xa[sh], None)

                @block.gpsimd
                def _(g, sh=sh):
                    for i in range(c.NOT):
                        p.wait(g, "g", ev_add[(sh, c.DC - 1, i)])
                        row = (sh * c.NOT + i) * 128
                        p.inc(g.dma_start(out=a_d[row:row + 128, :], in_=A_stage[:, i, :]), "dAD", 16)
                    ev_adma[sh] = p.ev("dAD")

        if c.stop_after == "p25":
            emit_dummy_out()
            stA.close()
            stq.close()
            return nc
        stq.close()
        fence3 = [p.ev("pe"), p.ev("act"), p.ev("dve")]

        with (
            nc.sbuf_tensor("gbuf", [128, c.FC, c.SQW], BF16) as gbuf,
            nc.sbuf_tensor("hbuf", [128, 2, min(4, c.FC), c.SQW], BF16) as hbuf,
            nc.sbuf_tensor("w1st", [128, 2, c.DC, 128], BF16) as w1st,
            nc.sbuf_tensor("w2st", [128, 2, c.FC, 128], BF16) as w2st,
            nc.psum_tensor("psH", [128, 2, 512], F32) as psH,
            nc.psum_tensor("psO2", [128, 2, 512], F32) as psO2,
            nc.psum_tensor("psT3", [128, 2, 2, 1024], BF16) as psT3,
        ):
            w1r = w1q.rearrange("(ec pp) (fc j) -> fc pp ec j", pp=128, j=128)
            w2r = w2q.rearrange("(fc pp) (dc j) -> dc pp fc j", pp=128, j=128)
            ev_w1 = {}
            ev_w2 = {}
            ev_mmh = {}
            ev_g = {}
            ev_mm2 = {}
            ev_ev2 = {}
            ev_tr3 = {}
            ev_add3 = {}
            ev_odma = {}
            ev_xa3 = {}

            for sh in range(c.NSQ):
                @block.sync
                def _(s, sh=sh):
                    p.wait(s, "s", ev_adma[c.NSQ - 1])
                    if sh >= 1:
                        p.wait(s, "s", ev_odma[sh - 1])
                    for i in range(c.NOT):
                        row = (sh * c.NOT + i) * 128
                        p.inc(s.dma_start(out=A_stage[:, i, :], in_=a_d[row:row + 128, :]), "dA", 16)
                    ev_xa3[sh] = p.ev("dA")

                FCG = min(4, c.FC)
                NFG = c.FC // FCG
                ev_hb = {}
                for fc in range(c.FC):
                    sl = fc % 2
                    fg = fc // FCG
                    fgsl = fg % 2
                    prev2 = None
                    if fc >= 2:
                        prev2 = (sh, fc - 2)
                    elif sh >= 1:
                        prev2 = (sh - 1, c.FC - 2 + fc)

                    @block.sync
                    def _(s, sh=sh, fc=fc, sl=sl, prev2=prev2):
                        p.wait(s, "s", ev_cc1)
                        if sh == 0 and fc < 2:
                            p.waits(s, "s", fence3)
                        if prev2 in ev_mmh:
                            p.wait(s, "s", ev_mmh[prev2])
                        sem = "dF10" if fc % 2 == 0 else "dF11"
                        p.inc(s.dma_start(out=w1st[:, sl, :, :], in_=w1r[fc]), sem, 16)
                        ev_w1[(sh, fc)] = p.ev(sem)

                    @block.tensor
                    def _(t, sh=sh, fc=fc, sl=sl, prev2=prev2):
                        p.wait(t, "t", ev_w1[(sh, fc)])
                        if prev2 in ev_hb:
                            p.wait(t, "t", ev_hb[prev2])
                        for ec in range(c.DC):
                            ins = t.matmul(psH[:, sl, 0:c.SQW], w1st[:, sl, ec, :],
                                           xnT[:, ec, sh * c.SQW:(sh + 1) * c.SQW],
                                           start=(ec == 0), stop=(ec == c.DC - 1))
                        ev_mmh[(sh, fc)] = p.inc(ins, "pe")

                    @block.vector
                    def _(v, sh=sh, fc=fc, sl=sl, fg=fg, fgsl=fgsl):
                        p.waits(v, "v", [ev_mmh[(sh, fc)], ev_b1c])
                        pfg = (sh, fg - 2) if fg >= 2 else ((sh - 1, NFG - 2 + fg) if sh >= 1 else None)
                        if pfg in ev_g:
                            p.wait(v, "v", ev_g[pfg])
                        ins = v.tensor_scalar(hbuf[:, fgsl, fc % FCG, :], psH[:, sl, 0:c.SQW],
                                              b1c[:, fc:fc + 1], None, op0=ALU.add, op1=ALU.bypass)
                        ev_hb[(sh, fc)] = p.inc(ins, "dve")

                    if fc % FCG == FCG - 1:
                        @block.scalar
                        def _(a, sh=sh, fc=fc, fg=fg, fgsl=fgsl):
                            p.wait(a, "a", ev_hb[(sh, fc)])
                            if sh >= 1 and fg == 0:
                                p.wait(a, "a", ev_mm2[(sh - 1, c.DC - 1)])
                            ins = a.activation(gbuf[:, fg * FCG:(fg + 1) * FCG, :],
                                               hbuf[:, fgsl, :, :], AF.Gelu)
                            ev_g[(sh, fg)] = p.inc(ins, "act")

                for dc in range(c.DC):
                    sl = dc % 2
                    prev2 = None
                    if dc >= 2:
                        prev2 = (sh, dc - 2)
                    elif sh >= 1:
                        prev2 = (sh - 1, c.DC - 2 + dc)
                    prev1 = (sh, dc - 1) if dc >= 1 else ((sh - 1, c.DC - 1) if sh >= 1 else None)

                    @block.sync
                    def _(s, sh=sh, dc=dc, sl=sl, prev2=prev2):
                        p.wait(s, "s", ev_cc2)
                        if sh == 0 and dc < 2:
                            p.waits(s, "s", fence3)
                        if prev2 in ev_mm2:
                            p.wait(s, "s", ev_mm2[prev2])
                        sem = "dF20" if dc % 2 == 0 else "dF21"
                        for f0 in range(0, c.FC, 16):
                            f1 = min(f0 + 16, c.FC)
                            p.inc(s.dma_start(out=w2st[:, sl, f0:f1, :], in_=w2r[dc][:, f0:f1, :]), sem, 16)
                        ev_w2[(sh, dc)] = p.ev(sem)

                    @block.tensor
                    def _(t, sh=sh, dc=dc, sl=sl, prev2=prev2):
                        p.waits(t, "t", [ev_w2[(sh, dc)], ev_g[(sh, c.FC // min(4, c.FC) - 1)]])
                        if prev2 in ev_ev2:
                            p.wait(t, "t", ev_ev2[prev2])
                        for fc in range(c.FC):
                            ins = t.matmul(psO2[:, sl, 0:c.SQW], w2st[:, sl, fc, :], gbuf[:, fc, :],
                                           start=(fc == 0), stop=(fc == c.FC - 1))
                        ev_mm2[(sh, dc)] = p.inc(ins, "pe")

                    emit_tr_add(sh, dc, sl, prev1, psO2, psT3, ev_mm2, ev_ev2, ev_tr3, ev_add3,
                                ev_xa3[sh], b2c[:, dc:dc + 1])

                @block.gpsimd
                def _(g, sh=sh):
                    for i in range(c.NOT):
                        p.wait(g, "g", ev_add3[(sh, c.DC - 1, i)])
                        row = (sh * c.NOT + i) * 128
                        p.inc(g.dma_start(out=out_d[row:row + 128, :], in_=A_stage[:, i, :]), "dO", 16)
                    ev_odma[sh] = p.ev("dO")

            @block.gpsimd
            def _(g):
                if c.taps:
                    p.inc(g.dma_start(out=tap_g.rearrange("(fc pp) w -> pp fc w", pp=128), in_=gbuf[:, :, :]), "dO", 16)
                    p.inc(g.dma_start(out=tap_ad[:, :], in_=a_d[:, :]), "dO", 16)
                    p.inc(g.dma_start(out=tap_sth[:, :], in_=sTh[:, :]), "dO", 16)
                    p.inc(g.dma_start(out=tap_stl[:, :], in_=sTl[:, :]), "dO", 16)
                    p.inc(g.dma_start(out=tap_trsb[:, :], in_=tr_sb.rearrange("p a b j -> p (a b j)")), "dO", 16)
                    p.inc(g.dma_start(out=tap_xnt.rearrange("(ec pp) t -> pp ec t", pp=128), in_=xnT[:, :, :]), "dO", 16)
                    p.inc(g.dma_start(out=tap_kg[:, :], in_=k_g[:, :]), "dO", 16)
                    p.inc(g.dma_start(out=tap_vg[:, :], in_=v_g[:, :]), "dO", 16)
                    p.inc(g.dma_start(out=tap_w1q[:, :], in_=w1q[:, :]), "dO", 16)
                    p.inc(g.dma_start(out=tap_w2q[:, :], in_=w2q[:, :]), "dO", 16)
                g.wait_ge(p.sem["dO"], p.cnt["dO"])

        stA.close()

    return nc



def kernel(L0_attn_x, L0_attn_w_qkv, L0_attn_w_out, L0_ffn_w1, L0_ffn_b1, L0_ffn_w2, L0_ffn_b2):
    c = Cfg()
    x = np.ascontiguousarray(np.asarray(L0_attn_x, dtype=np.float32).reshape(c.NTOK, c.D))
    w_qkv = np.ascontiguousarray(np.asarray(L0_attn_w_qkv, dtype=np.float32))
    w_out = np.ascontiguousarray(np.asarray(L0_attn_w_out, dtype=np.float32))
    w1T = np.ascontiguousarray(np.asarray(L0_ffn_w1, dtype=np.float32).T)
    w2T = np.ascontiguousarray(np.asarray(L0_ffn_w2, dtype=np.float32).T)
    b1 = np.ascontiguousarray(np.asarray(L0_ffn_b1, dtype=np.float32))
    b2 = np.ascontiguousarray(np.asarray(L0_ffn_b2, dtype=np.float32))
    in_maps = []
    for i in range(c.NC):
        in_maps.append({
            "x_own": np.ascontiguousarray(x[i * c.T:(i + 1) * c.T]),
            "w_qkv": w_qkv,
            "w_out": w_out,
            "w1s": np.ascontiguousarray(w1T[i * c.W1R:(i + 1) * c.W1R]),
            "w2s": np.ascontiguousarray(w2T[i * c.W2R:(i + 1) * c.W2R]),
            "b1": b1,
            "b2": b2,
        })
    nc = build_program(c)
    res = run_bass_kernel_spmd(nc, in_maps, core_ids=list(range(c.NC)))
    out = np.concatenate([np.asarray(r["out"]) for r in res.results], axis=0)
    return out.reshape(c.B, c.S, c.D)


# version 11
# speedup vs baseline: 1.1447x; 1.1224x over previous
import math
import numpy as np
from contextlib import ExitStack

import concourse.bass as bass
import concourse.mybir as mybir
from concourse.bass_utils import run_bass_kernel_spmd

F32 = mybir.dt.float32
BF16 = mybir.dt.bfloat16
AF = mybir.ActivationFunctionType
ALU = mybir.AluOpType
AX = mybir.AxisListType

MAGIC = float(1.5 * 2 ** 23)
CLIP = 1.25


class Cfg:
    def __init__(self, B=2, S=4096, D=2048, H=16, DH=128, DFF=8192, NCORES=8, sqw_cap=512, stop_after=None, p25_stub=0, taps=False):
        self.B, self.S, self.D, self.H, self.DH, self.DFF, self.NC = B, S, D, H, DH, DFF, NCORES
        self.NTOK = B * S
        self.T = self.NTOK // NCORES
        self.TT = self.T // 128
        self.DC = D // 128
        self.FC = DFF // 128
        self.G = NCORES // B
        self.SR = S // self.G
        assert self.SR == self.T
        self.ST = S // 128
        self.SRT = self.SR // 128
        self.SQW = min(sqw_cap, self.T)
        self.NSQ = self.T // self.SQW
        self.NOT = self.T // self.NSQ // 128
        self.W1R = D // NCORES
        self.W2R = DFF // NCORES
        self.QW = min(2048, max(DFF, D))
        self.VW = min(512, H * DH)
        self.scale = 1.0 / math.sqrt(DH)
        self.stop_after = stop_after
        self.p25_stub = p25_stub
        self.taps = taps


class Prog:
    def __init__(self, nc):
        self.nc = nc
        self.cnt = {}
        self.sem = {}
        self.lastw = {}

    def mksems(self, st, names):
        for n in names:
            self.sem[n] = st.enter_context(self.nc.semaphore(n))
            self.cnt[n] = 0
            self.lastw[n] = {}

    def inc(self, instr, s, n=1):
        self.cnt[s] += n
        instr.then_inc(self.sem[s], n)
        return (s, self.cnt[s])

    def ev(self, s):
        return (s, self.cnt[s])

    def wait(self, eng, engname, ev):
        if ev is None:
            return
        s, v = ev
        if v <= 0:
            return
        if self.lastw[s].get(engname, 0) < v:
            eng.wait_ge(self.sem[s], v)
            self.lastw[s][engname] = v

    def waits(self, eng, engname, evs):
        for ev in evs:
            if ev is not None:
                self.wait(eng, engname, ev)


def build_program(cfg: Cfg):
    nc = bass.Bass("TRN2", target_bir_lowering=False, debug=False)
    c = cfg
    RG_BATCH = [list(range(g * c.G, (g + 1) * c.G)) for g in range(c.B)]
    RG_ALL = [list(range(c.NC))]

    x_own = nc.dram_tensor("x_own", [c.T, c.D], F32, kind="ExternalInput").ap()
    w_qkv = nc.dram_tensor("w_qkv", [c.D, 3 * c.H * c.DH], F32, kind="ExternalInput").ap()
    w_out = nc.dram_tensor("w_out", [c.H * c.DH, c.D], F32, kind="ExternalInput").ap()
    w1s = nc.dram_tensor("w1s", [c.W1R, c.DFF], F32, kind="ExternalInput").ap()
    w2s = nc.dram_tensor("w2s", [c.W2R, c.D], F32, kind="ExternalInput").ap()
    b1 = nc.dram_tensor("b1", [c.DFF], F32, kind="ExternalInput").ap()
    b2 = nc.dram_tensor("b2", [c.D], F32, kind="ExternalInput").ap()
    out_d = nc.dram_tensor("out", [c.T, c.D], F32, kind="ExternalOutput").ap()

    k_own = nc.dram_tensor("k_own", [c.H * c.DH, c.T], BF16).ap()
    v_own = nc.dram_tensor("v_own", [c.T, c.H * c.DH], BF16).ap()
    k_g = nc.dram_tensor("k_g", [c.G * c.H * c.DH, c.T], BF16).ap()
    v_g = nc.dram_tensor("v_g", [c.G * c.T, c.H * c.DH], BF16).ap()
    p_own = nc.dram_tensor("p_own", [1, 2], F32).ap()
    p_g = nc.dram_tensor("p_g", [1, 2], F32).ap()
    w1q_own = nc.dram_tensor("w1q_own", [c.W1R, c.DFF], BF16).ap()
    w2q_own = nc.dram_tensor("w2q_own", [c.W2R, c.D], BF16).ap()
    w1q = nc.dram_tensor("w1q", [c.D, c.DFF], BF16).ap()
    w2q = nc.dram_tensor("w2q", [c.DFF, c.D], BF16).ap()
    a_d = nc.dram_tensor("a_d", [c.T, c.D], F32).ap()
    if c.taps:
        tap_ad = nc.dram_tensor("tap_ad", [c.T, c.D], F32, kind="ExternalOutput").ap()
        tap_kg = nc.dram_tensor("tap_kg", [c.G * c.H * c.DH, c.T], BF16, kind="ExternalOutput").ap()
        tap_vg = nc.dram_tensor("tap_vg", [c.G * c.T, c.H * c.DH], BF16, kind="ExternalOutput").ap()
        tap_w1q = nc.dram_tensor("tap_w1q", [c.D, c.DFF], BF16, kind="ExternalOutput").ap()
        tap_w2q = nc.dram_tensor("tap_w2q", [c.DFF, c.D], BF16, kind="ExternalOutput").ap()
        tap_g = nc.dram_tensor("tap_g", [c.FC * 128, c.SQW], BF16, kind="ExternalOutput").ap()
        tap_sth = nc.dram_tensor("tap_sth", [128, c.SQW], BF16, kind="ExternalOutput").ap()
        tap_stl = nc.dram_tensor("tap_stl", [128, c.SQW], BF16, kind="ExternalOutput").ap()
        tap_trsb = nc.dram_tensor("tap_trsb", [128, 2 * 2 * 128], F32, kind="ExternalOutput").ap()

    p = Prog(nc)

    st = ExitStack()
    e = st.enter_context
    with st:
        p.mksems(st, ["dB", "dX0", "dX1", "dQ0", "dQ1", "dW0", "dW1", "dWV0", "dWV1",
                      "dKst0", "dKst1", "dVst0", "dVst1", "dP", "dQst0", "dQst1",
                      "dKV0", "dKV1", "dWo0", "dWo1", "dA", "dAD", "dF10", "dF11",
                      "dF20", "dF21", "pe", "act", "dve", "pool",
                      "ccS", "ccK", "ccV", "cc1", "cc2", "dO"])

        xnT = e(nc.sbuf_tensor("xnT", [128, c.DC, c.T], BF16))
        identb = e(nc.sbuf_tensor("identb", [128, 128], BF16))
        identf = e(nc.sbuf_tensor("identf", [128, 128], F32))
        onesb_row = e(nc.sbuf_tensor("onesb_row", [1, 128], BF16))
        onesb_col = e(nc.sbuf_tensor("onesb_col", [128, 1], BF16))
        ones_row = e(nc.sbuf_tensor("ones_row", [1, 128], F32))
        ones_col = e(nc.sbuf_tensor("ones_col", [128, 1], F32))
        b1c = e(nc.sbuf_tensor("b1c", [128, c.FC], F32))
        b2c = e(nc.sbuf_tensor("b2c", [128, c.DC], F32))
        brow = e(nc.sbuf_tensor("brow", [max(c.FC, c.DC), 128], F32))
        rsc = e(nc.sbuf_tensor("rsc", [128, 2], F32))
        scl = e(nc.sbuf_tensor("scl", [1, 4], F32))
        sT = e(nc.sbuf_tensor("sT", [128, c.SQW], F32))
        sTh = e(nc.sbuf_tensor("sTh", [128, c.SQW], BF16))
        sTl = e(nc.sbuf_tensor("sTl", [128, c.SQW], BF16))
        tr_sb = e(nc.sbuf_tensor("tr_sb", [128, 2, 2, 128], F32))

        block = e(nc.Block())

        @block.gpsimd
        def _(g):
            for ident in (identb, identf):
                g.memset(ident[:, :], 0.0).then_inc(p.sem["pool"], 1)
                p.cnt["pool"] += 1
                g.wait_ge(p.sem["pool"], p.cnt["pool"])
                ins = g.affine_select(out=ident[:, :], in_=ident[:, :],
                                      compare_op=ALU.not_equal, fill=1.0,
                                      base=0, pattern=[[-1, 128]], channel_multiplier=1)
                p.inc(ins, "pool")
            p.inc(g.memset(onesb_row[:, :], 1.0), "pool")
            p.inc(g.memset(onesb_col[:, :], 1.0), "pool")
            p.inc(g.memset(ones_row[:, :], 1.0), "pool")
            p.inc(g.memset(ones_col[:, :], 1.0), "pool")
        ev_const = p.ev("pool")

        with nc.psum_tensor("psB", [128, max(c.FC, c.DC)], F32) as psB:
            @block.sync
            def _(s):
                p.inc(s.dma_start(out=brow[0:c.FC, :], in_=b1.rearrange("(a p) -> a p", p=128)), "dB", 16)
            ev_b1row = p.ev("dB")

            @block.tensor
            def _(t):
                p.waits(t, "t", [ev_b1row, ev_const])
                p.inc(t.transpose(psB[:, 0:c.FC], brow[0:c.FC, :], identf[0:c.FC, 0:c.FC]), "pe")
            ev_tb1 = p.ev("pe")

            @block.vector
            def _(v):
                p.wait(v, "v", ev_tb1)
                p.inc(v.tensor_copy(b1c[:, :], psB[:, 0:c.FC]), "dve")
            ev_b1c = p.ev("dve")

            @block.sync
            def _(s):
                p.wait(s, "s", ev_tb1)
                p.inc(s.dma_start(out=brow[0:c.DC, :], in_=b2.rearrange("(a p) -> a p", p=128)), "dB", 16)
            ev_b2row = p.ev("dB")

            @block.tensor
            def _(t):
                p.waits(t, "t", [ev_b2row, ev_b1c])
                p.inc(t.transpose(psB[:, 0:c.DC], brow[0:c.DC, :], identf[0:c.DC, 0:c.DC]), "pe")
            ev_tb2 = p.ev("pe")

            @block.vector
            def _(v):
                p.wait(v, "v", ev_tb2)
                p.inc(v.tensor_copy(b2c[:, :], psB[:, 0:c.DC]), "dve")
            ev_b2c = p.ev("dve")

        st0 = ExitStack()
        with st0:
            e0 = st0.enter_context
            xstage = e0(nc.sbuf_tensor("xstage", [128, 2, c.D], F32))
            xn_tok = e0(nc.sbuf_tensor("xn_tok", [128, 2, c.D], BF16))
            lncols = e0(nc.sbuf_tensor("lncols", [128, 8], F32))
            ln_scr = e0(nc.sbuf_tensor("ln_scr", [128, c.QW], F32))
            psTr = e0(nc.psum_tensor("psTr", [128, 2, 1024], BF16))

            ev_x = [None] * c.TT
            ev_sq = [None] * c.TT
            ev_xn = [None] * c.TT
            ev_tr = {}
            ev_evac = {}

            for tt in range(c.TT):
                sl = tt % 2

                @block.sync
                def _(s, tt=tt, sl=sl):
                    if tt >= 2:
                        p.wait(s, "s", ev_xn[tt - 2])
                    sem = "dX0" if tt % 2 == 0 else "dX1"
                    p.inc(s.dma_start(out=xstage[:, sl, :], in_=x_own[tt * 128:(tt + 1) * 128, :]), sem, 16)
                    ev_x[tt] = p.ev(sem)

                @block.vector
                def _(v, tt=tt, sl=sl):
                    p.wait(v, "v", ev_x[tt])
                    p.inc(v.tensor_reduce(lncols[:, 0:1], xstage[:, sl, :], axis=AX.X, op=ALU.add), "dve")
                ev_s1 = p.ev("dve")

                @block.vector
                def _(v, tt=tt, sl=sl):
                    p.wait(v, "v", ev_x[tt])
                    if tt >= 1:
                        p.wait(v, "v", ev_sq[tt - 1])
                    p.inc(v.tensor_tensor(ln_scr[:, 0:c.D], xstage[:, sl, :], xstage[:, sl, :], op=ALU.mult), "dve")
                    p.wait(v, "v", p.ev("dve"))
                    ins = v.tensor_reduce(lncols[:, 1:2], ln_scr[:, 0:c.D], axis=AX.X, op=ALU.add)
                    ev_sq[tt] = p.inc(ins, "dve")

                @block.vector
                def _(v, tt=tt, ev_s1=ev_s1):
                    p.waits(v, "v", [ev_sq[tt], ev_s1])
                    inv_d = 1.0 / c.D
                    p.inc(v.tensor_scalar(lncols[:, 2:3], lncols[:, 0:1], inv_d, None, op0=ALU.mult, op1=ALU.bypass), "dve")
                    p.wait(v, "v", p.ev("dve"))
                    p.inc(v.tensor_scalar(lncols[:, 3:4], lncols[:, 1:2], inv_d, 1e-5, op0=ALU.mult, op1=ALU.add), "dve")
                    p.wait(v, "v", p.ev("dve"))
                    p.inc(v.tensor_tensor(lncols[:, 4:5], lncols[:, 2:3], lncols[:, 2:3], op=ALU.mult), "dve")
                    p.wait(v, "v", p.ev("dve"))
                    p.inc(v.tensor_tensor(lncols[:, 3:4], lncols[:, 3:4], lncols[:, 4:5], op=ALU.subtract), "dve")
                ev_var = p.ev("dve")

                @block.scalar
                def _(a, tt=tt):
                    p.wait(a, "a", ev_var)
                    p.inc(a.activation(lncols[:, 5:6], lncols[:, 3:4], AF.Ln), "act")
                    p.wait(a, "a", p.ev("act"))
                    p.inc(a.activation(lncols[:, 5:6], lncols[:, 5:6], AF.Exp, scale=-0.5), "act")
                ev_rstd = p.ev("act")

                @block.vector
                def _(v, tt=tt):
                    p.wait(v, "v", ev_rstd)
                    p.inc(v.tensor_tensor(lncols[:, 6:7], lncols[:, 2:3], lncols[:, 5:6], op=ALU.mult), "dve")
                    p.wait(v, "v", p.ev("dve"))
                    p.inc(v.tensor_scalar(lncols[:, 6:7], lncols[:, 6:7], -1.0, None, op0=ALU.mult, op1=ALU.bypass), "dve")
                ev_stats = p.ev("dve")

                @block.vector
                def _(v, tt=tt, sl=sl):
                    p.wait(v, "v", ev_stats)
                    if tt >= 2:
                        p.wait(v, "v", ev_tr[(tt - 2, c.DC - 1)])
                    ins = v.tensor_scalar(xn_tok[:, sl, :], xstage[:, sl, :],
                                          lncols[:, 5:6], lncols[:, 6:7], op0=ALU.mult, op1=ALU.add)
                    ev_xn[tt] = p.inc(ins, "dve")

                for ec in range(c.DC):
                    pslot = ec % 2

                    @block.tensor
                    def _(t, tt=tt, sl=sl, ec=ec, pslot=pslot):
                        p.waits(t, "t", [ev_xn[tt], ev_const])
                        prev = None
                        if ec >= 2:
                            prev = (tt, ec - 2)
                        elif tt >= 1:
                            prev = (tt - 1, c.DC - 2 + ec)
                        if prev in ev_evac:
                            p.wait(t, "t", ev_evac[prev])
                        ins = t.transpose(psTr[:, pslot, 0:128], xn_tok[:, sl, ec * 128:(ec + 1) * 128], identb[:, :])
                        ev_tr[(tt, ec)] = p.inc(ins, "pe")

                    @block.vector
                    def _(v, tt=tt, ec=ec, pslot=pslot):
                        p.wait(v, "v", ev_tr[(tt, ec)])
                        ins = v.tensor_copy(xnT[:, ec, tt * 128:(tt + 1) * 128], psTr[:, pslot, 0:128])
                        ev_evac[(tt, ec)] = p.inc(ins, "dve")

            ev_xnT = p.ev("dve")

            qtiles = []
            for which, R, W in (("w1", c.W1R, c.DFF), ("w2", c.W2R, c.D)):
                for i in range((R + 127) // 128):
                    for j in range((W + c.QW - 1) // c.QW):
                        qtiles.append((which, i, j, min(128, R - i * 128), min(c.QW, W - j * c.QW)))

            with (
                nc.sbuf_tensor("qin", [128, 2, c.QW], F32) as qin,
                nc.sbuf_tensor("acol", [128, 3], F32) as acol,
                nc.psum_tensor("psS", [1, 2], F32) as psS,
            ):
                ev_qin = {}
                ev_abs = {}

                @block.vector
                def _(v):
                    p.inc(v.memset(acol[:, 1:3], 0.0), "dve")
                ev_az = p.ev("dve")

                for idx, (which, i, j, rows, w) in enumerate(qtiles):
                    sl = idx % 2
                    src = w1s if which == "w1" else w2s

                    @block.sync
                    def _(s, idx=idx, sl=sl, src=src, i=i, j=j, rows=rows, w=w):
                        if idx >= 2:
                            p.wait(s, "s", ev_abs[idx - 2])
                        sem = "dQ0" if idx % 2 == 0 else "dQ1"
                        p.inc(s.dma_start(out=qin[0:rows, sl, 0:w],
                                          in_=src[i * 128:i * 128 + rows, j * c.QW:j * c.QW + w]), sem, 16)
                        ev_qin[idx] = p.ev(sem)

                    @block.vector
                    def _(v, idx=idx, sl=sl, rows=rows, w=w):
                        p.wait(v, "v", ev_qin[idx])
                        if idx >= 1:
                            p.wait(v, "v", ev_abs[idx - 1])
                            p.wait(v, "v", p.ev("dve"))
                        ins = v.tensor_reduce(acol[0:rows, 0:1], qin[0:rows, sl, 0:w], axis=AX.X,
                                              op=ALU.add, apply_absolute_value=True)
                        ev_abs[idx] = p.inc(ins, "dve")

                    @block.vector
                    def _(v, idx=idx, which=which, rows=rows):
                        p.waits(v, "v", [ev_abs[idx], ev_az])
                        col = 1 if which == "w1" else 2
                        p.wait(v, "v", p.ev("dve"))
                        ins = v.tensor_tensor(acol[0:rows, col:col + 1], acol[0:rows, col:col + 1],
                                              acol[0:rows, 0:1], op=ALU.add)
                        p.inc(ins, "dve")
                ev_acc = p.ev("dve")

                @block.tensor
                def _(t):
                    p.waits(t, "t", [ev_acc, ev_const])
                    p.inc(t.matmul(psS[0:1, 0:1], ones_col[:, :], acol[:, 1:2], start=True, stop=True), "pe")
                    p.inc(t.matmul(psS[0:1, 1:2], ones_col[:, :], acol[:, 2:3], start=True, stop=True), "pe")
                ev_psS = p.ev("pe")

                @block.vector
                def _(v):
                    p.wait(v, "v", ev_psS)
                    p.inc(v.tensor_copy(scl[0:1, 0:2], psS[0:1, 0:2]), "dve")
                ev_partial = p.ev("dve")

        def emit_dummy_out():
            dummy = nc.dram_tensor("dummy_src", [c.T, c.D], F32).ap()
            @block.gpsimd
            def _(g):
                p.waits(g, "g", [p.ev("pe"), p.ev("act"), p.ev("dve")])
                p.inc(g.dma_start(out=out_d[:, :], in_=dummy[:, :]), "dO", 16)
                g.wait_ge(p.sem["dO"], p.cnt["dO"])

        if c.stop_after == "p0":
            emit_dummy_out()
            return nc

        stq = ExitStack()
        eq = stq.enter_context
        qT = eq(nc.sbuf_tensor("qT", [128, c.H, c.T], BF16))
        combn = eq(nc.sbuf_tensor("combn", [128, c.H, c.T], BF16))

        fence0 = [p.ev("pe"), p.ev("act"), p.ev("dve")]
        st1 = ExitStack()
        with st1:
            e1 = st1.enter_context
            wst = e1(nc.sbuf_tensor("wst", [128, 2, c.DC, 128], BF16))
            wvm = e1(nc.sbuf_tensor("wvm", [128, 2, c.DC, c.VW], BF16))
            kstage = e1(nc.sbuf_tensor("kstage", [128, 2, c.T], BF16))
            vstage = e1(nc.sbuf_tensor("vstage", [128, 2, c.VW], BF16))
            psQK = e1(nc.psum_tensor("psQK", [128, 2, c.NSQ, 512], F32))
            psV = e1(nc.psum_tensor("psV", [128, 2, 512], F32))

            NQK = 2 * c.H
            ev_w = [None] * NQK
            ev_mmqk = [None] * NQK
            ev_evqk = [None] * NQK
            ev_kdma = [None] * c.H
            w_qkv_r = w_qkv.rearrange("(ec pp) (hc j) -> hc pp ec j", pp=128, j=128)

            oc_order = list(range(c.H, 2 * c.H)) + list(range(c.H))
            for pos, oc in enumerate(oc_order):
                sl = pos % 2
                prev_oc = oc_order[pos - 2] if pos >= 2 else None

                @block.gpsimd
                def _(g, oc=oc, sl=sl, pos=pos, prev_oc=prev_oc):
                    if pos < 2:
                        p.waits(g, "g", fence0)
                    else:
                        p.wait(g, "g", ev_mmqk[prev_oc])
                    sem = "dW0" if pos % 2 == 0 else "dW1"
                    p.inc(g.dma_start(out=wst[:, sl, :, :], in_=w_qkv_r[oc]), sem, 16)
                    ev_w[oc] = p.ev(sem)

                @block.tensor
                def _(t, oc=oc, sl=sl, pos=pos, prev_oc=prev_oc):
                    p.waits(t, "t", [ev_w[oc], ev_xnT])
                    if pos >= 2:
                        p.wait(t, "t", ev_evqk[prev_oc])
                    for sq in range(c.NSQ):
                        for ec in range(c.DC):
                            ins = t.matmul(psQK[:, sl, sq, 0:c.SQW],
                                           wst[:, sl, ec, :],
                                           xnT[:, ec, sq * c.SQW:(sq + 1) * c.SQW],
                                           start=(ec == 0), stop=(ec == c.DC - 1))
                    ev_mmqk[oc] = p.inc(ins, "pe")

                if oc < c.H:
                    @block.vector
                    def _(v, oc=oc, sl=sl):
                        p.wait(v, "v", ev_mmqk[oc])
                        ins = v.tensor_copy(qT[:, oc, :], psQK[:, sl, :, 0:c.SQW])
                        ev_evqk[oc] = p.inc(ins, "dve")
                else:
                    h = oc - c.H

                    @block.vector
                    def _(v, oc=oc, sl=sl, h=h):
                        p.wait(v, "v", ev_mmqk[oc])
                        if h >= 2:
                            p.wait(v, "v", ev_kdma[h - 2])
                        ins = v.tensor_copy(kstage[:, h % 2, :], psQK[:, sl, :, 0:c.SQW])
                        ev_evqk[oc] = p.inc(ins, "dve")

                    @block.gpsimd
                    def _(g, h=h):
                        p.wait(g, "g", ev_evqk[c.H + h])
                        ins = g.dma_start(out=k_own[h * 128:(h + 1) * 128, :], in_=kstage[:, h % 2, :])
                        ev_kdma[h] = p.inc(ins, "dKst0" if h % 2 == 0 else "dKst1", 16)

                    if h == c.H - 1:
                        @block.gpsimd
                        def _(g):
                            p.wait(g, "g", ev_kdma[c.H - 1])
                            if c.H >= 2:
                                p.wait(g, "g", ev_kdma[c.H - 2])
                            ins = g.collective_compute("AllGather", ALU.bypass, replica_groups=RG_BATCH,
                                                       ins=[k_own.opt()], outs=[k_g.opt()])
                            p.inc(ins, "ccK")

            ev_qT = ev_evqk[c.H - 1]

            NVG = (c.H * c.DH) // c.VW
            ev_wv = [None] * NVG
            ev_mmv = {}
            ev_evv = {}
            ev_vdma = [None] * c.TT
            w_qkv_v = w_qkv.rearrange("(ec pp) j -> pp ec j", pp=128)

            ev_vst = {}
            for vg in range(NVG):
                sl = vg % 2

                @block.gpsimd
                def _(g, vg=vg, sl=sl):
                    if vg < 2:
                        p.waits(g, "g", fence0)
                    else:
                        p.wait(g, "g", ev_mmv[(vg - 2, c.TT - 1)])
                    base = 2 * c.H * c.DH
                    sem = "dWV0" if vg % 2 == 0 else "dWV1"
                    p.inc(g.dma_start(out=wvm[:, sl, :, :],
                                      in_=w_qkv_v[:, :, base + vg * c.VW: base + (vg + 1) * c.VW]), sem, 16)
                    ev_wv[vg] = p.ev(sem)

                for tt in range(c.TT):
                    k = vg * c.TT + tt
                    ksl = k % 2

                    @block.tensor
                    def _(t, vg=vg, sl=sl, tt=tt):
                        p.waits(t, "t", [ev_wv[vg], ev_xnT])
                        prev = None
                        if tt >= 2:
                            prev = (vg, tt - 2)
                        elif vg >= 1:
                            prev = (vg - 1, c.TT - 2 + tt) if c.TT >= 2 else (vg - 1, tt)
                        if prev in ev_evv:
                            p.wait(t, "t", ev_evv[prev])
                        for ec in range(c.DC):
                            ins = t.matmul(psV[:, tt % 2, 0:c.VW], xnT[:, ec, tt * 128:(tt + 1) * 128],
                                           wvm[:, sl, ec, :], start=(ec == 0), stop=(ec == c.DC - 1))
                        ev_mmv[(vg, tt)] = p.inc(ins, "pe")

                    @block.vector
                    def _(v, vg=vg, tt=tt, k=k, ksl=ksl):
                        p.wait(v, "v", ev_mmv[(vg, tt)])
                        if k >= 2:
                            p.wait(v, "v", ev_vst[k - 2])
                        ins = v.tensor_copy(vstage[:, ksl, 0:c.VW], psV[:, tt % 2, 0:c.VW])
                        ev_evv[(vg, tt)] = p.inc(ins, "dve")

                    @block.gpsimd
                    def _(g, vg=vg, tt=tt, k=k, ksl=ksl):
                        p.wait(g, "g", ev_evv[(vg, tt)])
                        ins = g.dma_start(out=v_own[tt * 128:(tt + 1) * 128, vg * c.VW:(vg + 1) * c.VW],
                                          in_=vstage[:, ksl, 0:c.VW])
                        ev_vst[k] = p.inc(ins, "dVst0" if ksl == 0 else "dVst1", 16)

            for tt in range(c.TT):
                ev_vdma[tt] = ev_vst[(NVG - 1) * c.TT + tt]

            @block.gpsimd
            def _(g):
                p.wait(g, "g", ev_vdma[c.TT - 1])
                if c.TT >= 2:
                    p.wait(g, "g", ev_vdma[c.TT - 2])
                ins = g.collective_compute("AllGather", ALU.bypass, replica_groups=RG_BATCH,
                                           ins=[v_own.opt()], outs=[v_g.opt()])
                p.inc(ins, "ccV")
            ev_ccK = p.ev("ccK")
            ev_ccV = p.ev("ccV")

        @block.gpsimd
        def _(g):
            p.wait(g, "g", ev_partial)
            p.inc(g.dma_start(out=p_own[:, :], in_=scl[0:1, 0:2]), "dP", 16)
            p.wait(g, "g", p.ev("dP"))
            ins = g.collective_compute("AllReduce", ALU.add, replica_groups=RG_ALL,
                                       ins=[p_own.opt()], outs=[p_g.opt()])
            p.inc(ins, "ccS")
            p.wait(g, "g", p.ev("ccS"))
            p.inc(g.dma_start(out=scl[0:1, 2:4], in_=p_g[:, :]), "dP", 16)
        ev_pg = p.ev("dP")

        with nc.psum_tensor("psBc", [128, 2], F32) as psBc:
            @block.vector
            def _(v):
                p.wait(v, "v", ev_pg)
                n1 = 1.0 / (c.D * c.DFF)
                p.inc(v.tensor_scalar(scl[0:1, 2:3], scl[0:1, 2:3], n1, 1e-8, op0=ALU.mult, op1=ALU.max), "dve")
                p.wait(v, "v", p.ev("dve"))
                p.inc(v.tensor_scalar(scl[0:1, 3:4], scl[0:1, 3:4], n1, 1e-8, op0=ALU.mult, op1=ALU.max), "dve")
                p.wait(v, "v", p.ev("dve"))
                p.inc(v.reciprocal(scl[0:1, 2:4], scl[0:1, 2:4]), "dve")
            ev_rs = p.ev("dve")

            @block.tensor
            def _(t):
                p.waits(t, "t", [ev_rs, ev_const])
                p.inc(t.matmul(psBc[:, 0:2], ones_row[0:1, :], scl[0:1, 2:4], start=True, stop=True), "pe")
            ev_bc = p.ev("pe")

            @block.vector
            def _(v):
                p.wait(v, "v", ev_bc)
                p.inc(v.tensor_copy(rsc[:, 0:2], psBc[:, 0:2]), "dve")
            ev_rsc = p.ev("dve")


        stq2 = ExitStack()
        if True:
            qin2 = stq2.enter_context(nc.sbuf_tensor("qin2", [128, 2, c.QW], F32, side="right"))
            qout = stq2.enter_context(nc.sbuf_tensor("qout", [128, 2, c.QW], BF16, side="right"))
            fence1 = [p.ev("pe"), p.ev("dve"), p.ev("act"), ev_vdma[c.TT - 1], ev_vdma[c.TT - 2] if c.TT >= 2 else None]
            ev_qdma = {}
            ev_qv = {}
            n1t = len([1 for q in qtiles if q[0] == "w1"])
            for idx, (which, i, j, rows, w) in enumerate(qtiles):
                sl = idx % 2
                src = w1s if which == "w1" else w2s
                dst = w1q_own if which == "w1" else w2q_own
                rcol = 0 if which == "w1" else 1

                @block.sync
                def _(s, idx=idx, sl=sl, src=src, i=i, j=j, rows=rows, w=w):
                    if idx < 2:
                        p.waits(s, "s", fence1)
                    else:
                        p.wait(s, "s", ev_qv[idx - 2])
                    sem = "dQ0" if idx % 2 == 0 else "dQ1"
                    p.inc(s.dma_start(out=qin2[0:rows, sl, 0:w],
                                      in_=src[i * 128:i * 128 + rows, j * c.QW:j * c.QW + w]), sem, 16)
                    ev_qdma[idx] = p.ev(sem)

                @block.vector
                def _(v, idx=idx, sl=sl, rows=rows, w=w, rcol=rcol):
                    p.waits(v, "v", [ev_qdma[idx], ev_rsc])
                    a = qin2[0:rows, sl, 0:w]
                    p.inc(v.tensor_scalar(a, a, rsc[0:rows, rcol:rcol + 1], CLIP, op0=ALU.mult, op1=ALU.min), "dve")
                    p.wait(v, "v", p.ev("dve"))
                    p.inc(v.tensor_scalar(a, a, -CLIP, MAGIC, op0=ALU.max, op1=ALU.add), "dve")
                    p.wait(v, "v", p.ev("dve"))
                    if idx >= 2:
                        p.wait(v, "v", ev_qdma.get(("out", idx - 2)))
                    ins = v.tensor_scalar(qout[0:rows, sl, 0:w], a, MAGIC, None, op0=ALU.subtract, op1=ALU.bypass)
                    ev_qv[idx] = p.inc(ins, "dve")

                @block.gpsimd
                def _(g, idx=idx, sl=sl, dst=dst, i=i, j=j, rows=rows, w=w):
                    p.wait(g, "g", ev_qv[idx])
                    ins = g.dma_start(out=dst[i * 128:i * 128 + rows, j * c.QW:j * c.QW + w],
                                      in_=qout[0:rows, sl, 0:w])
                    ev_qdma[("out", idx)] = p.inc(ins, "dQst0" if idx % 2 == 0 else "dQst1", 16)

            @block.gpsimd
            def _(g):
                p.wait(g, "g", ev_qdma[("out", n1t - 1)])
                if n1t >= 2:
                    p.wait(g, "g", ev_qdma[("out", n1t - 2)])
                ins = g.collective_compute("AllGather", ALU.bypass, replica_groups=RG_ALL,
                                           ins=[w1q_own.opt()], outs=[w1q.opt()])
                p.inc(ins, "cc1")
                p.wait(g, "g", ev_qdma[("out", len(qtiles) - 1)])
                if len(qtiles) >= 2:
                    p.wait(g, "g", ev_qdma[("out", len(qtiles) - 2)])
                ins = g.collective_compute("AllGather", ALU.bypass, replica_groups=RG_ALL,
                                           ins=[w2q_own.opt()], outs=[w2q.opt()])
                p.inc(ins, "cc2")
        ev_cc1 = p.ev("cc1")
        ev_cc2 = p.ev("cc2")

        if c.stop_after == "p1":
            emit_dummy_out()
            stq2.close()
            stq.close()
            return nc

        st2 = ExitStack()
        with st2:
            e2 = st2.enter_context
            kbuf = e2(nc.sbuf_tensor("kbuf", [128, 2, c.G, c.SR], BF16))
            vbuf = e2(nc.sbuf_tensor("vbuf", [128, 2, c.ST, c.DH], BF16))
            Ebuf = e2(nc.sbuf_tensor("Ebuf", [128, 3, 2, c.NSQ, c.SQW], BF16))
            Zp = e2(nc.sbuf_tensor("Zp", [128, 2, c.NSQ, c.SQW], BF16))
            rzb = e2(nc.sbuf_tensor("rzb", [1, c.NSQ, c.SQW], F32))
            rzbh = e2(nc.sbuf_tensor("rzbh", [1, c.NSQ, c.SQW], BF16))
            rzbc = e2(nc.sbuf_tensor("rzbc", [128, c.NSQ, c.SQW], BF16))
            psL = e2(nc.psum_tensor("psL", [128, 2, c.NSQ, 512], F32))
            psC = e2(nc.psum_tensor("psC", [128, c.NSQ, 512], F32))
            psZ = e2(nc.psum_tensor("psZ", [1, c.NSQ, 512], F32))

            ev_kv = [None] * c.H
            ev_exp = {}
            ev_mmL = {}
            ev_mmC = {}
            ev_zadd = {}
            ev_comb = [None] * c.H
            ev_rzc = [None] * c.H
            ev_onemm = [None] * c.H
            ev_bcmm = [None] * c.H

            k_gr = k_g.rearrange("(r hc pp) t -> r hc pp t", r=c.G, pp=128)
            v_gr = v_g.rearrange("(r o pp) (hc dv) -> r hc pp o dv", r=c.G, pp=128, dv=c.DH)

            def emit_head_dma(h):
                sl = h % 2

                @block.sync
                def _(s, h=h, sl=sl):
                    p.waits(s, "s", [ev_ccK, ev_ccV])
                    if h >= 2:
                        p.wait(s, "s", ev_mmL[(h - 2, c.ST - 1)])
                        p.wait(s, "s", ev_mmC[(h - 2, c.ST - 1)])
                    sem = "dKV0" if h % 2 == 0 else "dKV1"
                    for r in range(c.G):
                        p.inc(s.dma_start(out=kbuf[:, sl, r, :], in_=k_gr[r, h]), sem, 16)
                        p.inc(s.dma_start(out=vbuf[:, sl, r * c.SRT:(r + 1) * c.SRT, :], in_=v_gr[r, h]), sem, 16)
                    ev_kv[h] = p.ev(sem)

            emit_head_dma(0)
            emit_head_dma(1)

            ev_zaddp = {}
            NG = c.ST // 2
            for h in range(c.H):
                sl = h % 2
                for g in range(NG):
                    gsl = (h * NG + g) % 3
                    ts_ = (2 * g, 2 * g + 1)
                    for t in ts_:
                        @block.tensor
                        def _(tns, h=h, sl=sl, t=t, g=g):
                            p.waits(tns, "t", [ev_kv[h], ev_qT])
                            if g >= 1:
                                p.wait(tns, "t", ev_exp[(h, g - 1)])
                            elif h >= 1:
                                p.wait(tns, "t", ev_exp[(h - 1, NG - 1)])
                                if t == 0:
                                    p.wait(tns, "t", ev_comb[h - 1])
                            r, col = t // c.SRT, (t % c.SRT) * 128
                            for sq in range(c.NSQ):
                                ins = tns.matmul(psL[:, t % 2, sq, 0:c.SQW], kbuf[:, sl, r, col:col + 128],
                                                 qT[:, h, sq * c.SQW:(sq + 1) * c.SQW], start=True, stop=True)
                            ev_mmL[(h, t)] = p.inc(ins, "pe")

                    @block.scalar
                    def _(a, h=h, g=g, gsl=gsl, ts_=ts_):
                        p.wait(a, "a", ev_mmL[(h, ts_[-1])])
                        gp3 = h * NG + g - 3
                        pg = (gp3 // NG, gp3 % NG) if gp3 >= 0 else None
                        if pg is not None:
                            pt1 = 2 * pg[1] + 1
                            p.waits(a, "a", [ev_mmC.get((pg[0], pt1)), ev_zadd.get((pg[0], pt1)),
                                             ev_zaddp.get((pg[0], pt1))])
                        ins = a.activation(Ebuf[:, gsl, :, :, :], psL[:, :, :, 0:c.SQW], AF.Exp, scale=c.scale)
                        ev_exp[(h, g)] = p.inc(ins, "act")

                    for t in ts_:
                        @block.tensor
                        def _(tns, h=h, sl=sl, t=t, g=g, gsl=gsl):
                            p.wait(tns, "t", ev_exp[(h, g)])
                            for sq in range(c.NSQ):
                                ins = tns.matmul(psC[:, sq, 0:c.SQW], vbuf[:, sl, t, :],
                                                 Ebuf[:, gsl, t % 2, sq, :],
                                                 start=(t == 0), stop=(t == c.ST - 1))
                            ev_mmC[(h, t)] = p.inc(ins, "pe")

                    @block.vector
                    def _(v, h=h, g=g, gsl=gsl, ts_=ts_):
                        p.wait(v, "v", ev_exp[(h, g)])
                        if g == 0 and h >= 1:
                            p.wait(v, "v", ev_onemm[h - 1])
                        t = ts_[0]
                        p.wait(v, "v", p.ev("dve"))
                        for sq in range(c.NSQ):
                            if sq:
                                p.wait(v, "v", p.ev("dve"))
                            if g == 0:
                                ins = v.tensor_copy(Zp[:, 0, sq, :], Ebuf[:, gsl, 0, sq, :])
                            else:
                                ins = v.tensor_tensor(Zp[:, 0, sq, :], Zp[:, 0, sq, :],
                                                      Ebuf[:, gsl, 0, sq, :], op=ALU.add)
                        ev_zadd[(h, t)] = p.inc(ins, "dve")
                        ev_zadd[(h, ts_[1])] = ev_zadd[(h, t)]

                    @block.gpsimd
                    def _(gp, h=h, g=g, gsl=gsl, ts_=ts_):
                        p.wait(gp, "g", ev_exp[(h, g)])
                        if g == 0 and h >= 1:
                            p.wait(gp, "g", ev_onemm[h - 1])
                        p.wait(gp, "g", p.ev("pool"))
                        for sq in range(c.NSQ):
                            if sq:
                                p.wait(gp, "g", p.ev("pool"))
                            if g == 0:
                                ins = gp.tensor_copy(Zp[:, 1, sq, :], Ebuf[:, gsl, 1, sq, :])
                            else:
                                ins = gp.tensor_tensor(Zp[:, 1, sq, :], Zp[:, 1, sq, :],
                                                       Ebuf[:, gsl, 1, sq, :], op=ALU.add)
                        ev_zaddp[(h, ts_[1])] = p.inc(ins, "pool")
                        ev_zaddp[(h, ts_[0])] = ev_zaddp[(h, ts_[1])]

                if h + 2 < c.H:
                    emit_head_dma(h + 2)

                @block.tensor
                def _(tns, h=h):
                    p.wait(tns, "t", ev_zadd[(h, c.ST - 1)])
                    if ev_zaddp.get((h, c.ST - 1)) is not None:
                        p.wait(tns, "t", ev_zaddp[(h, c.ST - 1)])
                    if h >= 1:
                        p.wait(tns, "t", ev_rzc[h - 1])
                    for sq in range(c.NSQ):
                        tns.matmul(psZ[0:1, sq, 0:c.SQW], onesb_col[:, :], Zp[:, 0, sq, :], start=True, stop=False)
                        ins = tns.matmul(psZ[0:1, sq, 0:c.SQW], onesb_col[:, :], Zp[:, 1, sq, :], start=False, stop=True)
                    ev_onemm[h] = p.inc(ins, "pe")

                @block.vector
                def _(v, h=h):
                    p.wait(v, "v", ev_onemm[h])
                    ins = v.reciprocal(rzb[0:1, :, :], psZ[0:1, :, 0:c.SQW])
                    p.inc(ins, "dve")
                    p.wait(v, "v", p.ev("dve"))
                    ins = v.tensor_copy(rzbh[0:1, :, :], rzb[0:1, :, :])
                    ev_rzc[h] = p.inc(ins, "dve")

                @block.tensor
                def _(tns, h=h):
                    p.waits(tns, "t", [ev_rzc[h], ev_exp[(h, c.ST // 2 - 1)]])
                    for sq in range(c.NSQ):
                        ins = tns.matmul(psL[:, 0, sq, 0:c.SQW], onesb_row[0:1, :], rzbh[0:1, sq, :], start=True, stop=True)
                    ev_bcmm[h] = p.inc(ins, "pe")

                @block.vector
                def _(v, h=h):
                    p.wait(v, "v", ev_bcmm[h])
                    p.inc(v.tensor_copy(rzbc[:, :, :], psL[:, 0, :, 0:c.SQW]), "dve")
                    p.wait(v, "v", p.ev("dve"))
                    p.wait(v, "v", ev_mmC[(h, c.ST - 1)])
                    for sq in range(c.NSQ):
                        if sq:
                            p.wait(v, "v", p.ev("dve"))
                        ins = v.tensor_tensor(combn[:, h, sq * c.SQW:(sq + 1) * c.SQW],
                                              psC[:, sq, 0:c.SQW], rzbc[:, sq, :], op=ALU.mult)
                    ev_comb[h] = p.inc(ins, "dve")

            ev_attn = ev_comb[c.H - 1]

        def emit_tr_add(sh, dc, sl, prev1, psIn, psT, ev_mm_d, ev_evo_d, ev_tr_d, ev_add_d,
                        ev_xload, bcol):
            NH = max(1, c.NOT // 2)
            for hf in range(NH):
                grp = (sh, dc, "g", hf)
                pgrp = (sh, dc, "g", hf - 1) if hf >= 1 else \
                       ((prev1[0], prev1[1], "g", NH - 1) if prev1 is not None else None)
                tiles = [i for i in range(2 * hf, min(2 * hf + 2, c.NOT))]
                w0, w1_ = tiles[0] * 128, (tiles[-1] + 1) * 128

                @block.vector
                def _(v, sh=sh, dc=dc, sl=sl, grp=grp, pgrp=pgrp, w0=w0, w1_=w1_, hf=hf):
                    p.wait(v, "v", ev_mm_d[(sh, dc)])
                    if bcol is not None:
                        p.wait(v, "v", ev_b2c)
                    if pgrp is not None and ("sth", pgrp) in ev_add_d:
                        p.wait(v, "v", ev_add_d[("sth", pgrp)])
                    if bcol is not None:
                        ins = v.tensor_scalar(sT[:, w0:w1_], psIn[:, sl, w0:w1_], bcol, None,
                                              op0=ALU.add, op1=ALU.bypass)
                    else:
                        ins = v.tensor_copy(sT[:, w0:w1_], psIn[:, sl, w0:w1_])
                    p.inc(ins, "dve")
                    ev_evo_d[grp] = p.ev("dve")
                if hf == NH - 1:
                    ev_evo_d[(sh, dc)] = ev_evo_d[grp]

                @block.vector
                def _(v, sh=sh, dc=dc, grp=grp, pgrp=pgrp, w0=w0, w1_=w1_):
                    p.wait(v, "v", ev_evo_d[grp])
                    if pgrp is not None and pgrp in ev_tr_d:
                        p.wait(v, "v", ev_tr_d[pgrp])
                    p.inc(v.tensor_copy(sTh[:, w0:w1_], sT[:, w0:w1_]), "dve")
                    ev_add_d[("sth", grp)] = p.ev("dve")

                @block.vector
                def _(v, sh=sh, dc=dc, grp=grp, pgrp=pgrp, w0=w0, w1_=w1_):
                    p.wait(v, "v", ev_add_d[("sth", grp)])
                    if pgrp is not None and pgrp in ev_tr_d:
                        p.wait(v, "v", ev_tr_d[pgrp])
                    p.inc(v.tensor_tensor(sTl[:, w0:w1_], sT[:, w0:w1_], sTh[:, w0:w1_], op=ALU.subtract), "dve")
                    ev_add_d[("stl", grp)] = p.ev("dve")

                @block.tensor
                def _(t, sh=sh, dc=dc, grp=grp, pgrp=pgrp, tiles=tiles):
                    p.waits(t, "t", [ev_add_d[("stl", grp)], ev_add_d[("sth", grp)]])
                    if pgrp is not None and ("ev", pgrp) in ev_add_d:
                        p.wait(t, "t", ev_add_d[("ev", pgrp)])
                    for i in tiles:
                        j = i % 2
                        ins = t.transpose(psT[:, j, 0, 0:128], sTh[:, i * 128:(i + 1) * 128], identb[:, :])
                        p.inc(ins, "pe")
                        ins = t.transpose(psT[:, j, 1, 0:128], sTl[:, i * 128:(i + 1) * 128], identb[:, :])
                        ev_tr_d[(sh, dc, i)] = p.inc(ins, "pe")
                    ev_tr_d[grp] = p.ev("pe")
                if hf == NH - 1:
                    ev_tr_d[(sh, dc)] = ev_tr_d[grp]

                @block.vector
                def _(v, sh=sh, dc=dc, grp=grp, pgrp=pgrp, tiles=tiles):
                    for i in tiles:
                        j = i % 2
                        p.wait(v, "v", ev_tr_d[(sh, dc, i)])
                        if pgrp is not None and (pgrp, j) in ev_add_d:
                            p.wait(v, "v", ev_add_d[(pgrp, j)])
                        v.tensor_copy(tr_sb[:, j, 0, :], psT[:, j, 0, 0:128]).then_inc(p.sem["dve"], 1)
                        p.cnt["dve"] += 1
                        ins = v.tensor_copy(tr_sb[:, j, 1, :], psT[:, j, 1, 0:128])
                        p.inc(ins, "dve")
                        ev_add_d[("evt", (sh, dc, i))] = p.ev("dve")
                    ev_add_d[("ev", grp)] = p.ev("dve")

                @block.vector
                def _(v, sh=sh, dc=dc, grp=grp, tiles=tiles):
                    p.wait(v, "v", ev_xload)
                    for i in tiles:
                        j = i % 2
                        p.wait(v, "v", ev_add_d[("evt", (sh, dc, i))])
                        a = A_stage[:, i, dc * 128:(dc + 1) * 128]
                        v.tensor_tensor(a, a, tr_sb[:, j, 0, :], op=ALU.add).then_inc(p.sem["dve"], 1)
                        p.cnt["dve"] += 1
                        p.wait(v, "v", p.ev("dve"))
                        ins = v.tensor_tensor(a, a, tr_sb[:, j, 1, :], op=ALU.add)
                        ev_add_d[(sh, dc, i)] = p.inc(ins, "dve")
                        ev_add_d[(grp, j)] = p.ev("dve")

        stq2.close()
        if c.stop_after == "p2":
            emit_dummy_out()
            stq.close()
            return nc
        stA = ExitStack()
        eA = stA.enter_context
        A_stage = eA(nc.sbuf_tensor("A_stage", [128, c.NOT, c.D], F32, side="right"))

        with (
            nc.sbuf_tensor("wo", [128, 2, c.DC, 128], BF16) as wo,
            nc.psum_tensor("psO", [128, 2, 512], F32) as psO,
            nc.psum_tensor("psT2", [128, 2, 2, 1024], BF16) as psT2,
        ):
            ev_wo = {}
            ev_xa = {}
            ev_mmo = {}
            ev_evo = {}
            ev_tr2 = {}
            ev_add = {}
            ev_adma = {}
            w_or = w_out.rearrange("(ec pp) (dc j) -> dc pp ec j", pp=128, j=128)

            ev_qst = [p.ev("dQst0"), p.ev("dQst1")]
            for sh in range(c.NSQ):
                @block.sync
                def _(s, sh=sh):
                    p.wait(s, "s", ev_attn)
                    p.waits(s, "s", ev_qst)
                    if sh >= 1:
                        p.wait(s, "s", ev_adma[sh - 1])
                    for i in range(c.NOT):
                        row = (sh * c.NOT + i) * 128
                        p.inc(s.dma_start(out=A_stage[:, i, :], in_=x_own[row:row + 128, :]), "dA", 16)
                    ev_xa[sh] = p.ev("dA")

                for dc in range(c.DC):
                    sl = dc % 2
                    prev2 = None
                    if dc >= 2:
                        prev2 = (sh, dc - 2)
                    elif sh >= 1:
                        prev2 = (sh - 1, c.DC - 2 + dc)
                    prev1 = (sh, dc - 1) if dc >= 1 else ((sh - 1, c.DC - 1) if sh >= 1 else None)

                    @block.gpsimd
                    def _(g, sh=sh, dc=dc, sl=sl, prev2=prev2):
                        p.wait(g, "g", ev_attn)
                        if prev2 in ev_mmo:
                            p.wait(g, "g", ev_mmo[prev2])
                        sem = "dWo0" if dc % 2 == 0 else "dWo1"
                        p.inc(g.dma_start(out=wo[:, sl, :, :], in_=w_or[dc]), sem, 16)
                        ev_wo[(sh, dc)] = p.ev(sem)

                    @block.tensor
                    def _(t, sh=sh, dc=dc, sl=sl, prev2=prev2):
                        p.waits(t, "t", [ev_wo[(sh, dc)], ev_attn])
                        if prev2 in ev_evo:
                            p.wait(t, "t", ev_evo[prev2])
                        for ec in range(c.DC):
                            ins = t.matmul(psO[:, sl, 0:c.SQW], wo[:, sl, ec, :],
                                           combn[:, ec, sh * c.SQW:(sh + 1) * c.SQW],
                                           start=(ec == 0), stop=(ec == c.DC - 1))
                        ev_mmo[(sh, dc)] = p.inc(ins, "pe")

                    emit_tr_add(sh, dc, sl, prev1, psO, psT2, ev_mmo, ev_evo, ev_tr2, ev_add,
                                ev_xa[sh], None)

                @block.gpsimd
                def _(g, sh=sh):
                    for i in range(c.NOT):
                        p.wait(g, "g", ev_add[(sh, c.DC - 1, i)])
                        row = (sh * c.NOT + i) * 128
                        p.inc(g.dma_start(out=a_d[row:row + 128, :], in_=A_stage[:, i, :]), "dAD", 16)
                    ev_adma[sh] = p.ev("dAD")

        if c.stop_after == "p25":
            emit_dummy_out()
            stA.close()
            stq.close()
            return nc
        stq.close()
        fence3 = [p.ev("pe"), p.ev("act"), p.ev("dve")]

        with (
            nc.sbuf_tensor("gbuf", [128, c.FC, c.SQW], BF16) as gbuf,
            nc.sbuf_tensor("hbuf", [128, 2, min(4, c.FC), c.SQW], BF16) as hbuf,
            nc.sbuf_tensor("w1st", [128, 2, c.DC, 128], BF16) as w1st,
            nc.sbuf_tensor("w2st", [128, 2, c.FC, 128], BF16) as w2st,
            nc.psum_tensor("psH", [128, 2, 512], F32) as psH,
            nc.psum_tensor("psO2", [128, 2, 512], F32) as psO2,
            nc.psum_tensor("psT3", [128, 2, 2, 1024], BF16) as psT3,
        ):
            w1r = w1q.rearrange("(ec pp) (fc j) -> fc pp ec j", pp=128, j=128)
            w2r = w2q.rearrange("(fc pp) (dc j) -> dc pp fc j", pp=128, j=128)
            ev_w1 = {}
            ev_w2 = {}
            ev_mmh = {}
            ev_g = {}
            ev_mm2 = {}
            ev_ev2 = {}
            ev_tr3 = {}
            ev_add3 = {}
            ev_odma = {}
            ev_xa3 = {}

            for sh in range(c.NSQ):
                @block.sync
                def _(s, sh=sh):
                    p.wait(s, "s", ev_adma[c.NSQ - 1])
                    if sh >= 1:
                        p.wait(s, "s", ev_odma[sh - 1])
                    for i in range(c.NOT):
                        row = (sh * c.NOT + i) * 128
                        p.inc(s.dma_start(out=A_stage[:, i, :], in_=a_d[row:row + 128, :]), "dA", 16)
                    ev_xa3[sh] = p.ev("dA")

                FCG = min(4, c.FC)
                NFG = c.FC // FCG
                ev_hb = {}
                for fc in range(c.FC):
                    sl = fc % 2
                    fg = fc // FCG
                    fgsl = fg % 2
                    prev2 = None
                    if fc >= 2:
                        prev2 = (sh, fc - 2)
                    elif sh >= 1:
                        prev2 = (sh - 1, c.FC - 2 + fc)

                    @block.sync
                    def _(s, sh=sh, fc=fc, sl=sl, prev2=prev2):
                        p.wait(s, "s", ev_cc1)
                        if sh == 0 and fc < 2:
                            p.waits(s, "s", fence3)
                        if prev2 in ev_mmh:
                            p.wait(s, "s", ev_mmh[prev2])
                        sem = "dF10" if fc % 2 == 0 else "dF11"
                        p.inc(s.dma_start(out=w1st[:, sl, :, :], in_=w1r[fc]), sem, 16)
                        ev_w1[(sh, fc)] = p.ev(sem)

                    @block.tensor
                    def _(t, sh=sh, fc=fc, sl=sl, prev2=prev2):
                        p.wait(t, "t", ev_w1[(sh, fc)])
                        if prev2 in ev_hb:
                            p.wait(t, "t", ev_hb[prev2])
                        for ec in range(c.DC):
                            ins = t.matmul(psH[:, sl, 0:c.SQW], w1st[:, sl, ec, :],
                                           xnT[:, ec, sh * c.SQW:(sh + 1) * c.SQW],
                                           start=(ec == 0), stop=(ec == c.DC - 1))
                        ev_mmh[(sh, fc)] = p.inc(ins, "pe")

                    @block.vector
                    def _(v, sh=sh, fc=fc, sl=sl, fg=fg, fgsl=fgsl):
                        p.waits(v, "v", [ev_mmh[(sh, fc)], ev_b1c])
                        pfg = (sh, fg - 2) if fg >= 2 else ((sh - 1, NFG - 2 + fg) if sh >= 1 else None)
                        if pfg in ev_g:
                            p.wait(v, "v", ev_g[pfg])
                        ins = v.tensor_scalar(hbuf[:, fgsl, fc % FCG, :], psH[:, sl, 0:c.SQW],
                                              b1c[:, fc:fc + 1], None, op0=ALU.add, op1=ALU.bypass)
                        ev_hb[(sh, fc)] = p.inc(ins, "dve")

                    if fc % FCG == FCG - 1:
                        @block.scalar
                        def _(a, sh=sh, fc=fc, fg=fg, fgsl=fgsl):
                            p.wait(a, "a", ev_hb[(sh, fc)])
                            if sh >= 1 and fg == 0:
                                p.wait(a, "a", ev_mm2[(sh - 1, c.DC - 1)])
                            ins = a.activation(gbuf[:, fg * FCG:(fg + 1) * FCG, :],
                                               hbuf[:, fgsl, :, :], AF.Gelu)
                            ev_g[(sh, fg)] = p.inc(ins, "act")

                for dc in range(c.DC):
                    sl = dc % 2
                    prev2 = None
                    if dc >= 2:
                        prev2 = (sh, dc - 2)
                    elif sh >= 1:
                        prev2 = (sh - 1, c.DC - 2 + dc)
                    prev1 = (sh, dc - 1) if dc >= 1 else ((sh - 1, c.DC - 1) if sh >= 1 else None)

                    @block.sync
                    def _(s, sh=sh, dc=dc, sl=sl, prev2=prev2):
                        p.wait(s, "s", ev_cc2)
                        if sh == 0 and dc < 2:
                            p.waits(s, "s", fence3)
                        if prev2 in ev_mm2:
                            p.wait(s, "s", ev_mm2[prev2])
                        sem = "dF20" if dc % 2 == 0 else "dF21"
                        for f0 in range(0, c.FC, 16):
                            f1 = min(f0 + 16, c.FC)
                            p.inc(s.dma_start(out=w2st[:, sl, f0:f1, :], in_=w2r[dc][:, f0:f1, :]), sem, 16)
                        ev_w2[(sh, dc)] = p.ev(sem)

                    @block.tensor
                    def _(t, sh=sh, dc=dc, sl=sl, prev2=prev2):
                        p.waits(t, "t", [ev_w2[(sh, dc)], ev_g[(sh, c.FC // min(4, c.FC) - 1)]])
                        if prev2 in ev_ev2:
                            p.wait(t, "t", ev_ev2[prev2])
                        for fc in range(c.FC):
                            ins = t.matmul(psO2[:, sl, 0:c.SQW], w2st[:, sl, fc, :], gbuf[:, fc, :],
                                           start=(fc == 0), stop=(fc == c.FC - 1))
                        ev_mm2[(sh, dc)] = p.inc(ins, "pe")

                    emit_tr_add(sh, dc, sl, prev1, psO2, psT3, ev_mm2, ev_ev2, ev_tr3, ev_add3,
                                ev_xa3[sh], b2c[:, dc:dc + 1])

                @block.gpsimd
                def _(g, sh=sh):
                    for i in range(c.NOT):
                        p.wait(g, "g", ev_add3[(sh, c.DC - 1, i)])
                        row = (sh * c.NOT + i) * 128
                        p.inc(g.dma_start(out=out_d[row:row + 128, :], in_=A_stage[:, i, :]), "dO", 16)
                    ev_odma[sh] = p.ev("dO")

            @block.gpsimd
            def _(g):
                if c.taps:
                    p.inc(g.dma_start(out=tap_g.rearrange("(fc pp) w -> pp fc w", pp=128), in_=gbuf[:, :, :]), "dO", 16)
                    p.inc(g.dma_start(out=tap_ad[:, :], in_=a_d[:, :]), "dO", 16)
                    p.inc(g.dma_start(out=tap_sth[:, :], in_=sTh[:, :]), "dO", 16)
                    p.inc(g.dma_start(out=tap_stl[:, :], in_=sTl[:, :]), "dO", 16)
                    p.inc(g.dma_start(out=tap_trsb[:, :], in_=tr_sb.rearrange("p a b j -> p (a b j)")), "dO", 16)
                    p.inc(g.dma_start(out=tap_kg[:, :], in_=k_g[:, :]), "dO", 16)
                    p.inc(g.dma_start(out=tap_vg[:, :], in_=v_g[:, :]), "dO", 16)
                    p.inc(g.dma_start(out=tap_w1q[:, :], in_=w1q[:, :]), "dO", 16)
                    p.inc(g.dma_start(out=tap_w2q[:, :], in_=w2q[:, :]), "dO", 16)
                g.wait_ge(p.sem["dO"], p.cnt["dO"])

        stA.close()

    return nc



def kernel(L0_attn_x, L0_attn_w_qkv, L0_attn_w_out, L0_ffn_w1, L0_ffn_b1, L0_ffn_w2, L0_ffn_b2):
    c = Cfg()
    x = np.ascontiguousarray(np.asarray(L0_attn_x, dtype=np.float32).reshape(c.NTOK, c.D))
    w_qkv = np.ascontiguousarray(np.asarray(L0_attn_w_qkv, dtype=np.float32))
    w_out = np.ascontiguousarray(np.asarray(L0_attn_w_out, dtype=np.float32))
    w1T = np.ascontiguousarray(np.asarray(L0_ffn_w1, dtype=np.float32).T)
    w2T = np.ascontiguousarray(np.asarray(L0_ffn_w2, dtype=np.float32).T)
    b1 = np.ascontiguousarray(np.asarray(L0_ffn_b1, dtype=np.float32))
    b2 = np.ascontiguousarray(np.asarray(L0_ffn_b2, dtype=np.float32))
    in_maps = []
    for i in range(c.NC):
        in_maps.append({
            "x_own": np.ascontiguousarray(x[i * c.T:(i + 1) * c.T]),
            "w_qkv": w_qkv,
            "w_out": w_out,
            "w1s": np.ascontiguousarray(w1T[i * c.W1R:(i + 1) * c.W1R]),
            "w2s": np.ascontiguousarray(w2T[i * c.W2R:(i + 1) * c.W2R]),
            "b1": b1,
            "b2": b2,
        })
    nc = build_program(c)
    res = run_bass_kernel_spmd(nc, in_maps, core_ids=list(range(c.NC)))
    out = np.concatenate([np.asarray(r["out"]) for r in res.results], axis=0)
    return out.reshape(c.B, c.S, c.D)


# version 12
# speedup vs baseline: 1.2404x; 1.0836x over previous
import math
import numpy as np
from contextlib import ExitStack

import concourse.bass as bass
import concourse.mybir as mybir
from concourse.bass_utils import run_bass_kernel_spmd

F32 = mybir.dt.float32
BF16 = mybir.dt.bfloat16
AF = mybir.ActivationFunctionType
ALU = mybir.AluOpType
AX = mybir.AxisListType

MAGIC = float(1.5 * 2 ** 23)
CLIP = 1.25


class Cfg:
    def __init__(self, B=2, S=4096, D=2048, H=16, DH=128, DFF=8192, NCORES=8, sqw_cap=512, stop_after=None, p25_stub=0, taps=False):
        self.B, self.S, self.D, self.H, self.DH, self.DFF, self.NC = B, S, D, H, DH, DFF, NCORES
        self.NTOK = B * S
        self.T = self.NTOK // NCORES
        self.TT = self.T // 128
        self.DC = D // 128
        self.FC = DFF // 128
        self.G = NCORES // B
        self.SR = S // self.G
        assert self.SR == self.T
        self.ST = S // 128
        self.SRT = self.SR // 128
        self.SQW = min(sqw_cap, self.T)
        self.NSQ = self.T // self.SQW
        self.NOT = self.T // self.NSQ // 128
        self.W1R = D // NCORES
        self.W2R = DFF // NCORES
        self.QW = min(2048, max(DFF, D))
        self.VW = min(512, H * DH)
        self.scale = 1.0 / math.sqrt(DH)
        self.stop_after = stop_after
        self.p25_stub = p25_stub
        self.taps = taps


class Prog:
    def __init__(self, nc):
        self.nc = nc
        self.cnt = {}
        self.sem = {}
        self.lastw = {}

    def mksems(self, st, names):
        for n in names:
            self.sem[n] = st.enter_context(self.nc.semaphore(n))
            self.cnt[n] = 0
            self.lastw[n] = {}

    def inc(self, instr, s, n=1):
        self.cnt[s] += n
        instr.then_inc(self.sem[s], n)
        return (s, self.cnt[s])

    def ev(self, s):
        return (s, self.cnt[s])

    def wait(self, eng, engname, ev):
        if ev is None:
            return
        s, v = ev
        if v <= 0:
            return
        if self.lastw[s].get(engname, 0) < v:
            eng.wait_ge(self.sem[s], v)
            self.lastw[s][engname] = v

    def waits(self, eng, engname, evs):
        for ev in evs:
            if ev is not None:
                self.wait(eng, engname, ev)


def build_program(cfg: Cfg):
    nc = bass.Bass("TRN2", target_bir_lowering=False, debug=False)
    c = cfg
    RG_BATCH = [list(range(g * c.G, (g + 1) * c.G)) for g in range(c.B)]
    RG_ALL = [list(range(c.NC))]

    x_own = nc.dram_tensor("x_own", [c.T, c.D], F32, kind="ExternalInput").ap()
    w_qkv = nc.dram_tensor("w_qkv", [c.D, 3 * c.H * c.DH], F32, kind="ExternalInput").ap()
    w_out = nc.dram_tensor("w_out", [c.H * c.DH, c.D], F32, kind="ExternalInput").ap()
    w1s = nc.dram_tensor("w1s", [c.W1R, c.DFF], F32, kind="ExternalInput").ap()
    w2s = nc.dram_tensor("w2s", [c.W2R, c.D], F32, kind="ExternalInput").ap()
    b1 = nc.dram_tensor("b1", [c.DFF], F32, kind="ExternalInput").ap()
    b2 = nc.dram_tensor("b2", [c.D], F32, kind="ExternalInput").ap()
    out_d = nc.dram_tensor("out", [c.T, c.D], F32, kind="ExternalOutput").ap()

    k_own = nc.dram_tensor("k_own", [c.H * c.DH, c.T], BF16).ap()
    v_own = nc.dram_tensor("v_own", [c.T, c.H * c.DH], BF16).ap()
    k_g = nc.dram_tensor("k_g", [c.G * c.H * c.DH, c.T], BF16).ap()
    v_g = nc.dram_tensor("v_g", [c.G * c.T, c.H * c.DH], BF16).ap()
    p_own = nc.dram_tensor("p_own", [1, 2], F32).ap()
    p_g = nc.dram_tensor("p_g", [1, 2], F32).ap()
    w1q_own = nc.dram_tensor("w1q_own", [c.W1R, c.DFF], BF16).ap()
    w2q_own = nc.dram_tensor("w2q_own", [c.W2R, c.D], BF16).ap()
    w1q = nc.dram_tensor("w1q", [c.D, c.DFF], BF16).ap()
    w2q = nc.dram_tensor("w2q", [c.DFF, c.D], BF16).ap()
    a_d = nc.dram_tensor("a_d", [c.T, c.D], F32).ap()
    if c.taps:
        tap_ad = nc.dram_tensor("tap_ad", [c.T, c.D], F32, kind="ExternalOutput").ap()
        tap_kg = nc.dram_tensor("tap_kg", [c.G * c.H * c.DH, c.T], BF16, kind="ExternalOutput").ap()
        tap_vg = nc.dram_tensor("tap_vg", [c.G * c.T, c.H * c.DH], BF16, kind="ExternalOutput").ap()
        tap_w1q = nc.dram_tensor("tap_w1q", [c.D, c.DFF], BF16, kind="ExternalOutput").ap()
        tap_w2q = nc.dram_tensor("tap_w2q", [c.DFF, c.D], BF16, kind="ExternalOutput").ap()
        tap_g = nc.dram_tensor("tap_g", [c.FC * 128, c.SQW], BF16, kind="ExternalOutput").ap()
        tap_sth = nc.dram_tensor("tap_sth", [128, c.SQW], BF16, kind="ExternalOutput").ap()
        tap_stl = nc.dram_tensor("tap_stl", [128, c.SQW], BF16, kind="ExternalOutput").ap()
        tap_trsb = nc.dram_tensor("tap_trsb", [128, 2 * 2 * 128], F32, kind="ExternalOutput").ap()

    p = Prog(nc)

    st = ExitStack()
    e = st.enter_context
    with st:
        p.mksems(st, ["dB", "dX0", "dX1", "dQ0", "dQ1", "dW0", "dW1", "dWV0", "dWV1",
                      "dKst0", "dKst1", "dVst0", "dVst1", "dP", "dQst0", "dQst1",
                      "dKV0", "dKV1", "dWo0", "dWo1", "dA", "dAD", "dF10", "dF11",
                      "dF20", "dF21", "pe", "act", "dve", "pool",
                      "ccS", "ccK", "ccV", "cc1", "cc2", "dO"])

        xnT = e(nc.sbuf_tensor("xnT", [128, c.DC, c.T], BF16))
        identb = e(nc.sbuf_tensor("identb", [128, 128], BF16))
        identf = e(nc.sbuf_tensor("identf", [128, 128], F32))
        onesb_row = e(nc.sbuf_tensor("onesb_row", [1, 128], BF16))
        onesb_col = e(nc.sbuf_tensor("onesb_col", [128, 1], BF16))
        ones_row = e(nc.sbuf_tensor("ones_row", [1, 128], F32))
        ones_col = e(nc.sbuf_tensor("ones_col", [128, 1], F32))
        b1c = e(nc.sbuf_tensor("b1c", [128, c.FC], F32))
        b2c = e(nc.sbuf_tensor("b2c", [128, c.DC], F32))
        brow = e(nc.sbuf_tensor("brow", [max(c.FC, c.DC), 128], F32))
        rsc = e(nc.sbuf_tensor("rsc", [128, 2], F32))
        scl = e(nc.sbuf_tensor("scl", [1, 4], F32))
        sT = e(nc.sbuf_tensor("sT", [128, c.SQW], F32))
        sTh = e(nc.sbuf_tensor("sTh", [128, c.SQW], BF16))
        sTl = e(nc.sbuf_tensor("sTl", [128, c.SQW], BF16))
        tr_sb = e(nc.sbuf_tensor("tr_sb", [128, 2, 2, 128], F32))

        block = e(nc.Block())

        @block.gpsimd
        def _(g):
            for ident in (identb, identf):
                g.memset(ident[:, :], 0.0).then_inc(p.sem["pool"], 1)
                p.cnt["pool"] += 1
                g.wait_ge(p.sem["pool"], p.cnt["pool"])
                ins = g.affine_select(out=ident[:, :], in_=ident[:, :],
                                      compare_op=ALU.not_equal, fill=1.0,
                                      base=0, pattern=[[-1, 128]], channel_multiplier=1)
                p.inc(ins, "pool")
            p.inc(g.memset(onesb_row[:, :], 1.0), "pool")
            p.inc(g.memset(onesb_col[:, :], 1.0), "pool")
            p.inc(g.memset(ones_row[:, :], 1.0), "pool")
            p.inc(g.memset(ones_col[:, :], 1.0), "pool")
        ev_const = p.ev("pool")

        with nc.psum_tensor("psB", [128, max(c.FC, c.DC)], F32) as psB:
            @block.sync
            def _(s):
                p.inc(s.dma_start(out=brow[0:c.FC, :], in_=b1.rearrange("(a p) -> a p", p=128)), "dB", 16)
            ev_b1row = p.ev("dB")

            @block.tensor
            def _(t):
                p.waits(t, "t", [ev_b1row, ev_const])
                p.inc(t.transpose(psB[:, 0:c.FC], brow[0:c.FC, :], identf[0:c.FC, 0:c.FC]), "pe")
            ev_tb1 = p.ev("pe")

            @block.vector
            def _(v):
                p.wait(v, "v", ev_tb1)
                p.inc(v.tensor_copy(b1c[:, :], psB[:, 0:c.FC]), "dve")
            ev_b1c = p.ev("dve")

            @block.sync
            def _(s):
                p.wait(s, "s", ev_tb1)
                p.inc(s.dma_start(out=brow[0:c.DC, :], in_=b2.rearrange("(a p) -> a p", p=128)), "dB", 16)
            ev_b2row = p.ev("dB")

            @block.tensor
            def _(t):
                p.waits(t, "t", [ev_b2row, ev_b1c])
                p.inc(t.transpose(psB[:, 0:c.DC], brow[0:c.DC, :], identf[0:c.DC, 0:c.DC]), "pe")
            ev_tb2 = p.ev("pe")

            @block.vector
            def _(v):
                p.wait(v, "v", ev_tb2)
                p.inc(v.tensor_copy(b2c[:, :], psB[:, 0:c.DC]), "dve")
            ev_b2c = p.ev("dve")

        st0 = ExitStack()
        with st0:
            e0 = st0.enter_context
            xstage = e0(nc.sbuf_tensor("xstage", [128, 2, c.D], F32))
            xn_tok = e0(nc.sbuf_tensor("xn_tok", [128, 2, c.D], BF16))
            lncols = e0(nc.sbuf_tensor("lncols", [128, 8], F32))
            ln_scr = e0(nc.sbuf_tensor("ln_scr", [128, c.QW], F32))
            psTr = e0(nc.psum_tensor("psTr", [128, 2, 1024], BF16))

            ev_x = [None] * c.TT
            ev_sq = [None] * c.TT
            ev_xn = [None] * c.TT
            ev_tr = {}
            ev_evac = {}

            for tt in range(c.TT):
                sl = tt % 2

                @block.sync
                def _(s, tt=tt, sl=sl):
                    if tt >= 2:
                        p.wait(s, "s", ev_xn[tt - 2])
                    sem = "dX0" if tt % 2 == 0 else "dX1"
                    p.inc(s.dma_start(out=xstage[:, sl, :], in_=x_own[tt * 128:(tt + 1) * 128, :]), sem, 16)
                    ev_x[tt] = p.ev(sem)

                @block.vector
                def _(v, tt=tt, sl=sl):
                    p.wait(v, "v", ev_x[tt])
                    p.inc(v.tensor_reduce(lncols[:, 0:1], xstage[:, sl, :], axis=AX.X, op=ALU.add), "dve")
                ev_s1 = p.ev("dve")

                @block.vector
                def _(v, tt=tt, sl=sl):
                    p.wait(v, "v", ev_x[tt])
                    if tt >= 1:
                        p.wait(v, "v", ev_sq[tt - 1])
                    p.inc(v.tensor_tensor(ln_scr[:, 0:c.D], xstage[:, sl, :], xstage[:, sl, :], op=ALU.mult), "dve")
                    p.wait(v, "v", p.ev("dve"))
                    ins = v.tensor_reduce(lncols[:, 1:2], ln_scr[:, 0:c.D], axis=AX.X, op=ALU.add)
                    ev_sq[tt] = p.inc(ins, "dve")

                @block.vector
                def _(v, tt=tt, ev_s1=ev_s1):
                    p.waits(v, "v", [ev_sq[tt], ev_s1])
                    inv_d = 1.0 / c.D
                    p.inc(v.tensor_scalar(lncols[:, 2:3], lncols[:, 0:1], inv_d, None, op0=ALU.mult, op1=ALU.bypass), "dve")
                    p.wait(v, "v", p.ev("dve"))
                    p.inc(v.tensor_scalar(lncols[:, 3:4], lncols[:, 1:2], inv_d, 1e-5, op0=ALU.mult, op1=ALU.add), "dve")
                    p.wait(v, "v", p.ev("dve"))
                    p.inc(v.tensor_tensor(lncols[:, 4:5], lncols[:, 2:3], lncols[:, 2:3], op=ALU.mult), "dve")
                    p.wait(v, "v", p.ev("dve"))
                    p.inc(v.tensor_tensor(lncols[:, 3:4], lncols[:, 3:4], lncols[:, 4:5], op=ALU.subtract), "dve")
                ev_var = p.ev("dve")

                @block.scalar
                def _(a, tt=tt):
                    p.wait(a, "a", ev_var)
                    p.inc(a.activation(lncols[:, 5:6], lncols[:, 3:4], AF.Ln), "act")
                    p.wait(a, "a", p.ev("act"))
                    p.inc(a.activation(lncols[:, 5:6], lncols[:, 5:6], AF.Exp, scale=-0.5), "act")
                ev_rstd = p.ev("act")

                @block.vector
                def _(v, tt=tt):
                    p.wait(v, "v", ev_rstd)
                    p.inc(v.tensor_tensor(lncols[:, 6:7], lncols[:, 2:3], lncols[:, 5:6], op=ALU.mult), "dve")
                    p.wait(v, "v", p.ev("dve"))
                    p.inc(v.tensor_scalar(lncols[:, 6:7], lncols[:, 6:7], -1.0, None, op0=ALU.mult, op1=ALU.bypass), "dve")
                ev_stats = p.ev("dve")

                @block.vector
                def _(v, tt=tt, sl=sl):
                    p.wait(v, "v", ev_stats)
                    if tt >= 2:
                        p.wait(v, "v", ev_tr[(tt - 2, c.DC - 1)])
                    ins = v.tensor_scalar(xn_tok[:, sl, :], xstage[:, sl, :],
                                          lncols[:, 5:6], lncols[:, 6:7], op0=ALU.mult, op1=ALU.add)
                    ev_xn[tt] = p.inc(ins, "dve")

                for ec in range(c.DC):
                    pslot = ec % 2

                    @block.tensor
                    def _(t, tt=tt, sl=sl, ec=ec, pslot=pslot):
                        p.waits(t, "t", [ev_xn[tt], ev_const])
                        prev = None
                        if ec >= 2:
                            prev = (tt, ec - 2)
                        elif tt >= 1:
                            prev = (tt - 1, c.DC - 2 + ec)
                        if prev in ev_evac:
                            p.wait(t, "t", ev_evac[prev])
                        ins = t.transpose(psTr[:, pslot, 0:128], xn_tok[:, sl, ec * 128:(ec + 1) * 128], identb[:, :])
                        ev_tr[(tt, ec)] = p.inc(ins, "pe")

                    @block.vector
                    def _(v, tt=tt, ec=ec, pslot=pslot):
                        p.wait(v, "v", ev_tr[(tt, ec)])
                        ins = v.tensor_copy(xnT[:, ec, tt * 128:(tt + 1) * 128], psTr[:, pslot, 0:128])
                        ev_evac[(tt, ec)] = p.inc(ins, "dve")

            ev_xnT = p.ev("dve")

            qtiles = []
            for which, R, W in (("w1", c.W1R, c.DFF), ("w2", c.W2R, c.D)):
                for i in range((R + 127) // 128):
                    for j in range((W + c.QW - 1) // c.QW):
                        qtiles.append((which, i, j, min(128, R - i * 128), min(c.QW, W - j * c.QW)))

            with (
                nc.sbuf_tensor("qin", [128, 2, c.QW], F32) as qin,
                nc.sbuf_tensor("acol", [128, 3], F32) as acol,
                nc.psum_tensor("psS", [1, 2], F32) as psS,
            ):
                ev_qin = {}
                ev_abs = {}

                @block.vector
                def _(v):
                    p.inc(v.memset(acol[:, 1:3], 0.0), "dve")
                ev_az = p.ev("dve")

                for idx, (which, i, j, rows, w) in enumerate(qtiles):
                    sl = idx % 2
                    src = w1s if which == "w1" else w2s

                    @block.sync
                    def _(s, idx=idx, sl=sl, src=src, i=i, j=j, rows=rows, w=w):
                        if idx >= 2:
                            p.wait(s, "s", ev_abs[idx - 2])
                        sem = "dQ0" if idx % 2 == 0 else "dQ1"
                        p.inc(s.dma_start(out=qin[0:rows, sl, 0:w],
                                          in_=src[i * 128:i * 128 + rows, j * c.QW:j * c.QW + w]), sem, 16)
                        ev_qin[idx] = p.ev(sem)

                    @block.vector
                    def _(v, idx=idx, sl=sl, rows=rows, w=w):
                        p.wait(v, "v", ev_qin[idx])
                        if idx >= 1:
                            p.wait(v, "v", ev_abs[idx - 1])
                            p.wait(v, "v", p.ev("dve"))
                        ins = v.tensor_reduce(acol[0:rows, 0:1], qin[0:rows, sl, 0:w], axis=AX.X,
                                              op=ALU.add, apply_absolute_value=True)
                        ev_abs[idx] = p.inc(ins, "dve")

                    @block.vector
                    def _(v, idx=idx, which=which, rows=rows):
                        p.waits(v, "v", [ev_abs[idx], ev_az])
                        col = 1 if which == "w1" else 2
                        p.wait(v, "v", p.ev("dve"))
                        ins = v.tensor_tensor(acol[0:rows, col:col + 1], acol[0:rows, col:col + 1],
                                              acol[0:rows, 0:1], op=ALU.add)
                        p.inc(ins, "dve")
                ev_acc = p.ev("dve")

                @block.tensor
                def _(t):
                    p.waits(t, "t", [ev_acc, ev_const])
                    p.inc(t.matmul(psS[0:1, 0:1], ones_col[:, :], acol[:, 1:2], start=True, stop=True), "pe")
                    p.inc(t.matmul(psS[0:1, 1:2], ones_col[:, :], acol[:, 2:3], start=True, stop=True), "pe")
                ev_psS = p.ev("pe")

                @block.vector
                def _(v):
                    p.wait(v, "v", ev_psS)
                    p.inc(v.tensor_copy(scl[0:1, 0:2], psS[0:1, 0:2]), "dve")
                ev_partial = p.ev("dve")

        def emit_dummy_out():
            dummy = nc.dram_tensor("dummy_src", [c.T, c.D], F32).ap()
            @block.gpsimd
            def _(g):
                p.waits(g, "g", [p.ev("pe"), p.ev("act"), p.ev("dve")])
                p.inc(g.dma_start(out=out_d[:, :], in_=dummy[:, :]), "dO", 16)
                g.wait_ge(p.sem["dO"], p.cnt["dO"])

        if c.stop_after == "p0":
            emit_dummy_out()
            return nc

        stq = ExitStack()
        eq = stq.enter_context
        qT = eq(nc.sbuf_tensor("qT", [128, c.H, c.T], BF16))
        combn = eq(nc.sbuf_tensor("combn", [128, c.H, c.T], BF16))

        fence0 = [p.ev("pe"), p.ev("act"), p.ev("dve")]
        st1 = ExitStack()
        with st1:
            e1 = st1.enter_context
            wst = e1(nc.sbuf_tensor("wst", [128, 2, c.DC, 128], BF16))
            wvm = e1(nc.sbuf_tensor("wvm", [128, 2, c.DC, c.VW], BF16))
            kstage = e1(nc.sbuf_tensor("kstage", [128, 2, c.T], BF16))
            vstage = e1(nc.sbuf_tensor("vstage", [128, 2, c.VW], BF16))
            psQK = e1(nc.psum_tensor("psQK", [128, 2, c.NSQ, 512], F32))
            psV = e1(nc.psum_tensor("psV", [128, 2, 512], F32))

            NQK = 2 * c.H
            ev_w = [None] * NQK
            ev_mmqk = [None] * NQK
            ev_evqk = [None] * NQK
            ev_kdma = [None] * c.H
            w_qkv_r = w_qkv.rearrange("(ec pp) (hc j) -> hc pp ec j", pp=128, j=128)

            oc_order = list(range(c.H, 2 * c.H)) + list(range(c.H))
            for pos, oc in enumerate(oc_order):
                sl = pos % 2
                prev_oc = oc_order[pos - 2] if pos >= 2 else None

                @block.gpsimd
                def _(g, oc=oc, sl=sl, pos=pos, prev_oc=prev_oc):
                    if pos < 2:
                        p.waits(g, "g", fence0)
                    else:
                        p.wait(g, "g", ev_mmqk[prev_oc])
                    sem = "dW0" if pos % 2 == 0 else "dW1"
                    p.inc(g.dma_start(out=wst[:, sl, :, :], in_=w_qkv_r[oc]), sem, 16)
                    ev_w[oc] = p.ev(sem)

                @block.tensor
                def _(t, oc=oc, sl=sl, pos=pos, prev_oc=prev_oc):
                    p.waits(t, "t", [ev_w[oc], ev_xnT])
                    if pos >= 2:
                        p.wait(t, "t", ev_evqk[prev_oc])
                    for sq in range(c.NSQ):
                        for ec in range(c.DC):
                            ins = t.matmul(psQK[:, sl, sq, 0:c.SQW],
                                           wst[:, sl, ec, :],
                                           xnT[:, ec, sq * c.SQW:(sq + 1) * c.SQW],
                                           start=(ec == 0), stop=(ec == c.DC - 1))
                    ev_mmqk[oc] = p.inc(ins, "pe")

                if oc < c.H:
                    @block.vector
                    def _(v, oc=oc, sl=sl):
                        p.wait(v, "v", ev_mmqk[oc])
                        ins = v.tensor_copy(qT[:, oc, :], psQK[:, sl, :, 0:c.SQW])
                        ev_evqk[oc] = p.inc(ins, "dve")
                else:
                    h = oc - c.H

                    @block.vector
                    def _(v, oc=oc, sl=sl, h=h):
                        p.wait(v, "v", ev_mmqk[oc])
                        if h >= 2:
                            p.wait(v, "v", ev_kdma[h - 2])
                        ins = v.tensor_copy(kstage[:, h % 2, :], psQK[:, sl, :, 0:c.SQW])
                        ev_evqk[oc] = p.inc(ins, "dve")

                    @block.gpsimd
                    def _(g, h=h):
                        p.wait(g, "g", ev_evqk[c.H + h])
                        ins = g.dma_start(out=k_own[h * 128:(h + 1) * 128, :], in_=kstage[:, h % 2, :])
                        ev_kdma[h] = p.inc(ins, "dKst0" if h % 2 == 0 else "dKst1", 16)

                    if h == c.H - 1:
                        @block.gpsimd
                        def _(g):
                            p.wait(g, "g", ev_kdma[c.H - 1])
                            if c.H >= 2:
                                p.wait(g, "g", ev_kdma[c.H - 2])
                            ins = g.collective_compute("AllGather", ALU.bypass, replica_groups=RG_BATCH,
                                                       ins=[k_own.opt()], outs=[k_g.opt()])
                            p.inc(ins, "ccK")

            ev_qT = ev_evqk[c.H - 1]

            NVG = (c.H * c.DH) // c.VW
            ev_wv = [None] * NVG
            ev_mmv = {}
            ev_evv = {}
            ev_vdma = [None] * c.TT
            w_qkv_v = w_qkv.rearrange("(ec pp) j -> pp ec j", pp=128)

            ev_vst = {}
            for vg in range(NVG):
                sl = vg % 2

                @block.gpsimd
                def _(g, vg=vg, sl=sl):
                    if vg < 2:
                        p.waits(g, "g", fence0)
                    else:
                        p.wait(g, "g", ev_mmv[(vg - 2, c.TT - 1)])
                    base = 2 * c.H * c.DH
                    sem = "dWV0" if vg % 2 == 0 else "dWV1"
                    p.inc(g.dma_start(out=wvm[:, sl, :, :],
                                      in_=w_qkv_v[:, :, base + vg * c.VW: base + (vg + 1) * c.VW]), sem, 16)
                    ev_wv[vg] = p.ev(sem)

                for tt in range(c.TT):
                    k = vg * c.TT + tt
                    ksl = k % 2

                    @block.tensor
                    def _(t, vg=vg, sl=sl, tt=tt):
                        p.waits(t, "t", [ev_wv[vg], ev_xnT])
                        prev = None
                        if tt >= 2:
                            prev = (vg, tt - 2)
                        elif vg >= 1:
                            prev = (vg - 1, c.TT - 2 + tt) if c.TT >= 2 else (vg - 1, tt)
                        if prev in ev_evv:
                            p.wait(t, "t", ev_evv[prev])
                        for ec in range(c.DC):
                            ins = t.matmul(psV[:, tt % 2, 0:c.VW], xnT[:, ec, tt * 128:(tt + 1) * 128],
                                           wvm[:, sl, ec, :], start=(ec == 0), stop=(ec == c.DC - 1))
                        ev_mmv[(vg, tt)] = p.inc(ins, "pe")

                    @block.vector
                    def _(v, vg=vg, tt=tt, k=k, ksl=ksl):
                        p.wait(v, "v", ev_mmv[(vg, tt)])
                        if k >= 2:
                            p.wait(v, "v", ev_vst[k - 2])
                        ins = v.tensor_copy(vstage[:, ksl, 0:c.VW], psV[:, tt % 2, 0:c.VW])
                        ev_evv[(vg, tt)] = p.inc(ins, "dve")

                    @block.gpsimd
                    def _(g, vg=vg, tt=tt, k=k, ksl=ksl):
                        p.wait(g, "g", ev_evv[(vg, tt)])
                        ins = g.dma_start(out=v_own[tt * 128:(tt + 1) * 128, vg * c.VW:(vg + 1) * c.VW],
                                          in_=vstage[:, ksl, 0:c.VW])
                        ev_vst[k] = p.inc(ins, "dVst0" if ksl == 0 else "dVst1", 16)

            for tt in range(c.TT):
                ev_vdma[tt] = ev_vst[(NVG - 1) * c.TT + tt]

            @block.gpsimd
            def _(g):
                p.wait(g, "g", ev_vdma[c.TT - 1])
                if c.TT >= 2:
                    p.wait(g, "g", ev_vdma[c.TT - 2])
                ins = g.collective_compute("AllGather", ALU.bypass, replica_groups=RG_BATCH,
                                           ins=[v_own.opt()], outs=[v_g.opt()])
                p.inc(ins, "ccV")
            ev_ccK = p.ev("ccK")
            ev_ccV = p.ev("ccV")

        @block.gpsimd
        def _(g):
            p.wait(g, "g", ev_partial)
            p.inc(g.dma_start(out=p_own[:, :], in_=scl[0:1, 0:2]), "dP", 16)
            p.wait(g, "g", p.ev("dP"))
            ins = g.collective_compute("AllReduce", ALU.add, replica_groups=RG_ALL,
                                       ins=[p_own.opt()], outs=[p_g.opt()])
            p.inc(ins, "ccS")
            p.wait(g, "g", p.ev("ccS"))
            p.inc(g.dma_start(out=scl[0:1, 2:4], in_=p_g[:, :]), "dP", 16)
        ev_pg = p.ev("dP")

        with nc.psum_tensor("psBc", [128, 2], F32) as psBc:
            @block.vector
            def _(v):
                p.wait(v, "v", ev_pg)
                n1 = 1.0 / (c.D * c.DFF)
                p.inc(v.tensor_scalar(scl[0:1, 2:3], scl[0:1, 2:3], n1, 1e-8, op0=ALU.mult, op1=ALU.max), "dve")
                p.wait(v, "v", p.ev("dve"))
                p.inc(v.tensor_scalar(scl[0:1, 3:4], scl[0:1, 3:4], n1, 1e-8, op0=ALU.mult, op1=ALU.max), "dve")
                p.wait(v, "v", p.ev("dve"))
                p.inc(v.reciprocal(scl[0:1, 2:4], scl[0:1, 2:4]), "dve")
            ev_rs = p.ev("dve")

            @block.tensor
            def _(t):
                p.waits(t, "t", [ev_rs, ev_const])
                p.inc(t.matmul(psBc[:, 0:2], ones_row[0:1, :], scl[0:1, 2:4], start=True, stop=True), "pe")
            ev_bc = p.ev("pe")

            @block.vector
            def _(v):
                p.wait(v, "v", ev_bc)
                p.inc(v.tensor_copy(rsc[:, 0:2], psBc[:, 0:2]), "dve")
            ev_rsc = p.ev("dve")


        stq2 = ExitStack()
        if True:
            qin2 = stq2.enter_context(nc.sbuf_tensor("qin2", [128, 2, c.QW], F32, side="right"))
            qout = stq2.enter_context(nc.sbuf_tensor("qout", [128, 2, c.QW], BF16, side="right"))
            fence1 = [p.ev("pe"), p.ev("dve"), p.ev("act"), ev_vdma[c.TT - 1], ev_vdma[c.TT - 2] if c.TT >= 2 else None]
            ev_qdma = {}
            ev_qv = {}
            n1t = len([1 for q in qtiles if q[0] == "w1"])
            for idx, (which, i, j, rows, w) in enumerate(qtiles):
                sl = idx % 2
                src = w1s if which == "w1" else w2s
                dst = w1q_own if which == "w1" else w2q_own
                rcol = 0 if which == "w1" else 1

                @block.sync
                def _(s, idx=idx, sl=sl, src=src, i=i, j=j, rows=rows, w=w):
                    if idx < 2:
                        p.waits(s, "s", fence1)
                    else:
                        p.wait(s, "s", ev_qv[idx - 2])
                    sem = "dQ0" if idx % 2 == 0 else "dQ1"
                    p.inc(s.dma_start(out=qin2[0:rows, sl, 0:w],
                                      in_=src[i * 128:i * 128 + rows, j * c.QW:j * c.QW + w]), sem, 16)
                    ev_qdma[idx] = p.ev(sem)

                @block.vector
                def _(v, idx=idx, sl=sl, rows=rows, w=w, rcol=rcol):
                    p.waits(v, "v", [ev_qdma[idx], ev_rsc])
                    a = qin2[0:rows, sl, 0:w]
                    p.inc(v.tensor_scalar(a, a, rsc[0:rows, rcol:rcol + 1], CLIP, op0=ALU.mult, op1=ALU.min), "dve")
                    p.wait(v, "v", p.ev("dve"))
                    p.inc(v.tensor_scalar(a, a, -CLIP, MAGIC, op0=ALU.max, op1=ALU.add), "dve")
                    p.wait(v, "v", p.ev("dve"))
                    if idx >= 2:
                        p.wait(v, "v", ev_qdma.get(("out", idx - 2)))
                    ins = v.tensor_scalar(qout[0:rows, sl, 0:w], a, MAGIC, None, op0=ALU.subtract, op1=ALU.bypass)
                    ev_qv[idx] = p.inc(ins, "dve")

                @block.gpsimd
                def _(g, idx=idx, sl=sl, dst=dst, i=i, j=j, rows=rows, w=w):
                    p.wait(g, "g", ev_qv[idx])
                    ins = g.dma_start(out=dst[i * 128:i * 128 + rows, j * c.QW:j * c.QW + w],
                                      in_=qout[0:rows, sl, 0:w])
                    ev_qdma[("out", idx)] = p.inc(ins, "dQst0" if idx % 2 == 0 else "dQst1", 16)

            @block.gpsimd
            def _(g):
                p.wait(g, "g", ev_qdma[("out", n1t - 1)])
                if n1t >= 2:
                    p.wait(g, "g", ev_qdma[("out", n1t - 2)])
                ins = g.collective_compute("AllGather", ALU.bypass, replica_groups=RG_ALL,
                                           ins=[w1q_own.opt()], outs=[w1q.opt()])
                p.inc(ins, "cc1")
                p.wait(g, "g", ev_qdma[("out", len(qtiles) - 1)])
                if len(qtiles) >= 2:
                    p.wait(g, "g", ev_qdma[("out", len(qtiles) - 2)])
                ins = g.collective_compute("AllGather", ALU.bypass, replica_groups=RG_ALL,
                                           ins=[w2q_own.opt()], outs=[w2q.opt()])
                p.inc(ins, "cc2")
        ev_cc1 = p.ev("cc1")
        ev_cc2 = p.ev("cc2")

        if c.stop_after == "p1":
            emit_dummy_out()
            stq2.close()
            stq.close()
            return nc

        st2 = ExitStack()
        with st2:
            e2 = st2.enter_context
            kbuf = e2(nc.sbuf_tensor("kbuf", [128, 2, c.G, c.SR], BF16))
            vbuf = e2(nc.sbuf_tensor("vbuf", [128, 2, c.ST, c.DH], BF16))
            Ebuf = e2(nc.sbuf_tensor("Ebuf", [128, 3, 2, c.NSQ, c.SQW], BF16))
            Zp = e2(nc.sbuf_tensor("Zp", [128, 2, c.NSQ, c.SQW], BF16))
            rzb = e2(nc.sbuf_tensor("rzb", [1, c.NSQ, c.SQW], F32))
            rzbh = e2(nc.sbuf_tensor("rzbh", [1, c.NSQ, c.SQW], BF16))
            rzbc = e2(nc.sbuf_tensor("rzbc", [128, c.NSQ, c.SQW], BF16))
            psL = e2(nc.psum_tensor("psL", [128, 2, c.NSQ, 512], F32))
            psC = e2(nc.psum_tensor("psC", [128, c.NSQ, 512], F32))
            psZ = e2(nc.psum_tensor("psZ", [1, c.NSQ, 512], F32))

            ev_kv = [None] * c.H
            ev_exp = {}
            ev_mmL = {}
            ev_mmC = {}
            ev_zadd = {}
            ev_comb = [None] * c.H
            ev_rzc = [None] * c.H
            ev_onemm = [None] * c.H
            ev_bcmm = [None] * c.H

            k_gr = k_g.rearrange("(r hc pp) t -> r hc pp t", r=c.G, pp=128)
            v_gr = v_g.rearrange("(r o pp) (hc dv) -> r hc pp o dv", r=c.G, pp=128, dv=c.DH)

            def emit_head_dma(h):
                sl = h % 2

                @block.sync
                def _(s, h=h, sl=sl):
                    p.waits(s, "s", [ev_ccK, ev_ccV])
                    if h >= 2:
                        p.wait(s, "s", ev_mmL[(h - 2, c.ST - 1)])
                        p.wait(s, "s", ev_mmC[(h - 2, c.ST - 1)])
                    sem = "dKV0" if h % 2 == 0 else "dKV1"
                    for r in range(c.G):
                        p.inc(s.dma_start(out=kbuf[:, sl, r, :], in_=k_gr[r, h]), sem, 16)
                        p.inc(s.dma_start(out=vbuf[:, sl, r * c.SRT:(r + 1) * c.SRT, :], in_=v_gr[r, h]), sem, 16)
                    ev_kv[h] = p.ev(sem)

            emit_head_dma(0)
            emit_head_dma(1)

            ev_zaddp = {}
            NG = c.ST // 2

            def emit_mmL(h, g):
                sl = h % 2
                for t in (2 * g, 2 * g + 1):
                    @block.tensor
                    def _(tns, h=h, sl=sl, t=t, g=g):
                        p.waits(tns, "t", [ev_kv[h], ev_qT])
                        if g >= 1:
                            p.wait(tns, "t", ev_exp[(h, g - 1)])
                        elif h >= 1:
                            p.wait(tns, "t", ev_exp[(h - 1, NG - 1)])
                            if t == 0:
                                p.wait(tns, "t", ev_comb[h - 1])
                        r, col = t // c.SRT, (t % c.SRT) * 128
                        for sq in range(c.NSQ):
                            ins = tns.matmul(psL[:, t % 2, sq, 0:c.SQW], kbuf[:, sl, r, col:col + 128],
                                             qT[:, h, sq * c.SQW:(sq + 1) * c.SQW], start=True, stop=True)
                        ev_mmL[(h, t)] = p.inc(ins, "pe")

            for h in range(c.H):
                sl = h % 2
                emit_mmL(h, 0)
                for g in range(NG):
                    gsl = (h * NG + g) % 3
                    ts_ = (2 * g, 2 * g + 1)

                    @block.scalar
                    def _(a, h=h, g=g, gsl=gsl, ts_=ts_):
                        p.wait(a, "a", ev_mmL[(h, ts_[-1])])
                        gp3 = h * NG + g - 3
                        pg = (gp3 // NG, gp3 % NG) if gp3 >= 0 else None
                        if pg is not None:
                            pt1 = 2 * pg[1] + 1
                            p.waits(a, "a", [ev_mmC.get((pg[0], pt1)), ev_zadd.get((pg[0], pt1)),
                                             ev_zaddp.get((pg[0], pt1))])
                        ins = a.activation(Ebuf[:, gsl, :, :, :], psL[:, :, :, 0:c.SQW], AF.Exp, scale=c.scale)
                        ev_exp[(h, g)] = p.inc(ins, "act")

                    if g + 1 < NG:
                        emit_mmL(h, g + 1)

                    for t in ts_:
                        @block.tensor
                        def _(tns, h=h, sl=sl, t=t, g=g, gsl=gsl):
                            p.wait(tns, "t", ev_exp[(h, g)])
                            for sq in range(c.NSQ):
                                ins = tns.matmul(psC[:, sq, 0:c.SQW], vbuf[:, sl, t, :],
                                                 Ebuf[:, gsl, t % 2, sq, :],
                                                 start=(t == 0), stop=(t == c.ST - 1))
                            ev_mmC[(h, t)] = p.inc(ins, "pe")

                    @block.vector
                    def _(v, h=h, g=g, gsl=gsl, ts_=ts_):
                        p.wait(v, "v", ev_exp[(h, g)])
                        if g == 0 and h >= 1:
                            p.wait(v, "v", ev_onemm[h - 1])
                        t = ts_[0]
                        p.wait(v, "v", p.ev("dve"))
                        for sq in range(c.NSQ):
                            if sq:
                                p.wait(v, "v", p.ev("dve"))
                            if g == 0:
                                ins = v.tensor_copy(Zp[:, 0, sq, :], Ebuf[:, gsl, 0, sq, :])
                            else:
                                ins = v.tensor_tensor(Zp[:, 0, sq, :], Zp[:, 0, sq, :],
                                                      Ebuf[:, gsl, 0, sq, :], op=ALU.add)
                        ev_zadd[(h, t)] = p.inc(ins, "dve")
                        ev_zadd[(h, ts_[1])] = ev_zadd[(h, t)]

                    @block.gpsimd
                    def _(gp, h=h, g=g, gsl=gsl, ts_=ts_):
                        p.wait(gp, "g", ev_exp[(h, g)])
                        if g == 0 and h >= 1:
                            p.wait(gp, "g", ev_onemm[h - 1])
                        p.wait(gp, "g", p.ev("pool"))
                        for sq in range(c.NSQ):
                            if sq:
                                p.wait(gp, "g", p.ev("pool"))
                            if g == 0:
                                ins = gp.tensor_copy(Zp[:, 1, sq, :], Ebuf[:, gsl, 1, sq, :])
                            else:
                                ins = gp.tensor_tensor(Zp[:, 1, sq, :], Zp[:, 1, sq, :],
                                                       Ebuf[:, gsl, 1, sq, :], op=ALU.add)
                        ev_zaddp[(h, ts_[1])] = p.inc(ins, "pool")
                        ev_zaddp[(h, ts_[0])] = ev_zaddp[(h, ts_[1])]

                if h + 2 < c.H:
                    emit_head_dma(h + 2)

                @block.tensor
                def _(tns, h=h):
                    p.wait(tns, "t", ev_zadd[(h, c.ST - 1)])
                    if ev_zaddp.get((h, c.ST - 1)) is not None:
                        p.wait(tns, "t", ev_zaddp[(h, c.ST - 1)])
                    if h >= 1:
                        p.wait(tns, "t", ev_rzc[h - 1])
                    for sq in range(c.NSQ):
                        tns.matmul(psZ[0:1, sq, 0:c.SQW], onesb_col[:, :], Zp[:, 0, sq, :], start=True, stop=False)
                        ins = tns.matmul(psZ[0:1, sq, 0:c.SQW], onesb_col[:, :], Zp[:, 1, sq, :], start=False, stop=True)
                    ev_onemm[h] = p.inc(ins, "pe")

                @block.vector
                def _(v, h=h):
                    p.wait(v, "v", ev_onemm[h])
                    ins = v.reciprocal(rzb[0:1, :, :], psZ[0:1, :, 0:c.SQW])
                    p.inc(ins, "dve")
                    p.wait(v, "v", p.ev("dve"))
                    ins = v.tensor_copy(rzbh[0:1, :, :], rzb[0:1, :, :])
                    ev_rzc[h] = p.inc(ins, "dve")

                @block.tensor
                def _(tns, h=h):
                    p.waits(tns, "t", [ev_rzc[h], ev_exp[(h, c.ST // 2 - 1)]])
                    for sq in range(c.NSQ):
                        ins = tns.matmul(psL[:, 0, sq, 0:c.SQW], onesb_row[0:1, :], rzbh[0:1, sq, :], start=True, stop=True)
                    ev_bcmm[h] = p.inc(ins, "pe")

                @block.vector
                def _(v, h=h):
                    p.wait(v, "v", ev_bcmm[h])
                    p.inc(v.tensor_copy(rzbc[:, :, :], psL[:, 0, :, 0:c.SQW]), "dve")
                    p.wait(v, "v", p.ev("dve"))
                    p.wait(v, "v", ev_mmC[(h, c.ST - 1)])
                    for sq in range(c.NSQ):
                        if sq:
                            p.wait(v, "v", p.ev("dve"))
                        ins = v.tensor_tensor(combn[:, h, sq * c.SQW:(sq + 1) * c.SQW],
                                              psC[:, sq, 0:c.SQW], rzbc[:, sq, :], op=ALU.mult)
                    ev_comb[h] = p.inc(ins, "dve")

            ev_attn = ev_comb[c.H - 1]

        def emit_tr_add(sh, dc, sl, prev1, psIn, psT, ev_mm_d, ev_evo_d, ev_tr_d, ev_add_d,
                        ev_xload, bcol):
            NH = max(1, c.NOT // 2)
            for hf in range(NH):
                grp = (sh, dc, "g", hf)
                pgrp = (sh, dc, "g", hf - 1) if hf >= 1 else \
                       ((prev1[0], prev1[1], "g", NH - 1) if prev1 is not None else None)
                tiles = [i for i in range(2 * hf, min(2 * hf + 2, c.NOT))]
                w0, w1_ = tiles[0] * 128, (tiles[-1] + 1) * 128

                @block.vector
                def _(v, sh=sh, dc=dc, sl=sl, grp=grp, pgrp=pgrp, w0=w0, w1_=w1_, hf=hf):
                    p.wait(v, "v", ev_mm_d[(sh, dc)])
                    if bcol is not None:
                        p.wait(v, "v", ev_b2c)
                    if pgrp is not None and ("sth", pgrp) in ev_add_d:
                        p.wait(v, "v", ev_add_d[("sth", pgrp)])
                    if bcol is not None:
                        ins = v.tensor_scalar(sT[:, w0:w1_], psIn[:, sl, w0:w1_], bcol, None,
                                              op0=ALU.add, op1=ALU.bypass)
                    else:
                        ins = v.tensor_copy(sT[:, w0:w1_], psIn[:, sl, w0:w1_])
                    p.inc(ins, "dve")
                    ev_evo_d[grp] = p.ev("dve")
                if hf == NH - 1:
                    ev_evo_d[(sh, dc)] = ev_evo_d[grp]

                @block.vector
                def _(v, sh=sh, dc=dc, grp=grp, pgrp=pgrp, w0=w0, w1_=w1_):
                    p.wait(v, "v", ev_evo_d[grp])
                    if pgrp is not None and pgrp in ev_tr_d:
                        p.wait(v, "v", ev_tr_d[pgrp])
                    p.inc(v.tensor_copy(sTh[:, w0:w1_], sT[:, w0:w1_]), "dve")
                    ev_add_d[("sth", grp)] = p.ev("dve")

                @block.vector
                def _(v, sh=sh, dc=dc, grp=grp, pgrp=pgrp, w0=w0, w1_=w1_):
                    p.wait(v, "v", ev_add_d[("sth", grp)])
                    if pgrp is not None and pgrp in ev_tr_d:
                        p.wait(v, "v", ev_tr_d[pgrp])
                    p.inc(v.tensor_tensor(sTl[:, w0:w1_], sT[:, w0:w1_], sTh[:, w0:w1_], op=ALU.subtract), "dve")
                    ev_add_d[("stl", grp)] = p.ev("dve")

                @block.tensor
                def _(t, sh=sh, dc=dc, grp=grp, pgrp=pgrp, tiles=tiles):
                    p.waits(t, "t", [ev_add_d[("stl", grp)], ev_add_d[("sth", grp)]])
                    if pgrp is not None and ("ev", pgrp) in ev_add_d:
                        p.wait(t, "t", ev_add_d[("ev", pgrp)])
                    for i in tiles:
                        j = i % 2
                        ins = t.transpose(psT[:, j, 0, 0:128], sTh[:, i * 128:(i + 1) * 128], identb[:, :])
                        p.inc(ins, "pe")
                        ins = t.transpose(psT[:, j, 1, 0:128], sTl[:, i * 128:(i + 1) * 128], identb[:, :])
                        ev_tr_d[(sh, dc, i)] = p.inc(ins, "pe")
                    ev_tr_d[grp] = p.ev("pe")
                if hf == NH - 1:
                    ev_tr_d[(sh, dc)] = ev_tr_d[grp]

                @block.vector
                def _(v, sh=sh, dc=dc, grp=grp, pgrp=pgrp, tiles=tiles):
                    for i in tiles:
                        j = i % 2
                        p.wait(v, "v", ev_tr_d[(sh, dc, i)])
                        if pgrp is not None and (pgrp, j) in ev_add_d:
                            p.wait(v, "v", ev_add_d[(pgrp, j)])
                        v.tensor_copy(tr_sb[:, j, 0, :], psT[:, j, 0, 0:128]).then_inc(p.sem["dve"], 1)
                        p.cnt["dve"] += 1
                        ins = v.tensor_copy(tr_sb[:, j, 1, :], psT[:, j, 1, 0:128])
                        p.inc(ins, "dve")
                        ev_add_d[("evt", (sh, dc, i))] = p.ev("dve")
                    ev_add_d[("ev", grp)] = p.ev("dve")

                @block.vector
                def _(v, sh=sh, dc=dc, grp=grp, tiles=tiles):
                    p.wait(v, "v", ev_xload)
                    for i in tiles:
                        j = i % 2
                        p.wait(v, "v", ev_add_d[("evt", (sh, dc, i))])
                        a = A_stage[:, i, dc * 128:(dc + 1) * 128]
                        v.tensor_tensor(a, a, tr_sb[:, j, 0, :], op=ALU.add).then_inc(p.sem["dve"], 1)
                        p.cnt["dve"] += 1
                        p.wait(v, "v", p.ev("dve"))
                        ins = v.tensor_tensor(a, a, tr_sb[:, j, 1, :], op=ALU.add)
                        ev_add_d[(sh, dc, i)] = p.inc(ins, "dve")
                        ev_add_d[(grp, j)] = p.ev("dve")

        stq2.close()
        if c.stop_after == "p2":
            emit_dummy_out()
            stq.close()
            return nc
        stA = ExitStack()
        eA = stA.enter_context
        A_stage = eA(nc.sbuf_tensor("A_stage", [128, c.NOT, c.D], F32, side="right"))

        with (
            nc.sbuf_tensor("wo", [128, 2, c.DC, 128], BF16) as wo,
            nc.psum_tensor("psO", [128, 2, 512], F32) as psO,
            nc.psum_tensor("psT2", [128, 2, 2, 1024], BF16) as psT2,
        ):
            ev_wo = {}
            ev_xa = {}
            ev_mmo = {}
            ev_evo = {}
            ev_tr2 = {}
            ev_add = {}
            ev_adma = {}
            w_or = w_out.rearrange("(ec pp) (dc j) -> dc pp ec j", pp=128, j=128)

            ev_qst = [p.ev("dQst0"), p.ev("dQst1")]
            for sh in range(c.NSQ):
                @block.sync
                def _(s, sh=sh):
                    p.wait(s, "s", ev_attn)
                    p.waits(s, "s", ev_qst)
                    if sh >= 1:
                        p.wait(s, "s", ev_adma[sh - 1])
                    for i in range(c.NOT):
                        row = (sh * c.NOT + i) * 128
                        p.inc(s.dma_start(out=A_stage[:, i, :], in_=x_own[row:row + 128, :]), "dA", 16)
                    ev_xa[sh] = p.ev("dA")

                for dc in range(c.DC):
                    sl = dc % 2
                    prev2 = None
                    if dc >= 2:
                        prev2 = (sh, dc - 2)
                    elif sh >= 1:
                        prev2 = (sh - 1, c.DC - 2 + dc)
                    prev1 = (sh, dc - 1) if dc >= 1 else ((sh - 1, c.DC - 1) if sh >= 1 else None)

                    @block.gpsimd
                    def _(g, sh=sh, dc=dc, sl=sl, prev2=prev2):
                        p.wait(g, "g", ev_attn)
                        if prev2 in ev_mmo:
                            p.wait(g, "g", ev_mmo[prev2])
                        sem = "dWo0" if dc % 2 == 0 else "dWo1"
                        p.inc(g.dma_start(out=wo[:, sl, :, :], in_=w_or[dc]), sem, 16)
                        ev_wo[(sh, dc)] = p.ev(sem)

                    @block.tensor
                    def _(t, sh=sh, dc=dc, sl=sl, prev2=prev2):
                        p.waits(t, "t", [ev_wo[(sh, dc)], ev_attn])
                        if prev2 in ev_evo:
                            p.wait(t, "t", ev_evo[prev2])
                        for ec in range(c.DC):
                            ins = t.matmul(psO[:, sl, 0:c.SQW], wo[:, sl, ec, :],
                                           combn[:, ec, sh * c.SQW:(sh + 1) * c.SQW],
                                           start=(ec == 0), stop=(ec == c.DC - 1))
                        ev_mmo[(sh, dc)] = p.inc(ins, "pe")

                    emit_tr_add(sh, dc, sl, prev1, psO, psT2, ev_mmo, ev_evo, ev_tr2, ev_add,
                                ev_xa[sh], None)

                @block.gpsimd
                def _(g, sh=sh):
                    for i in range(c.NOT):
                        p.wait(g, "g", ev_add[(sh, c.DC - 1, i)])
                        row = (sh * c.NOT + i) * 128
                        p.inc(g.dma_start(out=a_d[row:row + 128, :], in_=A_stage[:, i, :]), "dAD", 16)
                    ev_adma[sh] = p.ev("dAD")

        if c.stop_after == "p25":
            emit_dummy_out()
            stA.close()
            stq.close()
            return nc
        stq.close()
        fence3 = [p.ev("pe"), p.ev("act"), p.ev("dve")]

        with (
            nc.sbuf_tensor("gbuf", [128, c.FC, c.SQW], BF16) as gbuf,
            nc.sbuf_tensor("hbuf", [128, 2, min(4, c.FC), c.SQW], BF16) as hbuf,
            nc.sbuf_tensor("w1st", [128, 2, c.DC, 128], BF16) as w1st,
            nc.sbuf_tensor("w2st", [128, 2, c.FC, 128], BF16) as w2st,
            nc.psum_tensor("psH", [128, 2, 512], F32) as psH,
            nc.psum_tensor("psO2", [128, 2, 512], F32) as psO2,
            nc.psum_tensor("psT3", [128, 2, 2, 1024], BF16) as psT3,
        ):
            w1r = w1q.rearrange("(ec pp) (fc j) -> fc pp ec j", pp=128, j=128)
            w2r = w2q.rearrange("(fc pp) (dc j) -> dc pp fc j", pp=128, j=128)
            ev_w1 = {}
            ev_w2 = {}
            ev_mmh = {}
            ev_g = {}
            ev_mm2 = {}
            ev_ev2 = {}
            ev_tr3 = {}
            ev_add3 = {}
            ev_odma = {}
            ev_xa3 = {}

            for sh in range(c.NSQ):
                @block.sync
                def _(s, sh=sh):
                    p.wait(s, "s", ev_adma[c.NSQ - 1])
                    if sh >= 1:
                        p.wait(s, "s", ev_odma[sh - 1])
                    for i in range(c.NOT):
                        row = (sh * c.NOT + i) * 128
                        p.inc(s.dma_start(out=A_stage[:, i, :], in_=a_d[row:row + 128, :]), "dA", 16)
                    ev_xa3[sh] = p.ev("dA")

                FCG = min(4, c.FC)
                NFG = c.FC // FCG
                ev_hb = {}
                for fc in range(c.FC):
                    sl = fc % 2
                    fg = fc // FCG
                    fgsl = fg % 2
                    prev2 = None
                    if fc >= 2:
                        prev2 = (sh, fc - 2)
                    elif sh >= 1:
                        prev2 = (sh - 1, c.FC - 2 + fc)

                    @block.sync
                    def _(s, sh=sh, fc=fc, sl=sl, prev2=prev2):
                        p.wait(s, "s", ev_cc1)
                        if sh == 0 and fc < 2:
                            p.waits(s, "s", fence3)
                        if prev2 in ev_mmh:
                            p.wait(s, "s", ev_mmh[prev2])
                        sem = "dF10" if fc % 2 == 0 else "dF11"
                        p.inc(s.dma_start(out=w1st[:, sl, :, :], in_=w1r[fc]), sem, 16)
                        ev_w1[(sh, fc)] = p.ev(sem)

                    @block.tensor
                    def _(t, sh=sh, fc=fc, sl=sl, prev2=prev2):
                        p.wait(t, "t", ev_w1[(sh, fc)])
                        if prev2 in ev_hb:
                            p.wait(t, "t", ev_hb[prev2])
                        for ec in range(c.DC):
                            ins = t.matmul(psH[:, sl, 0:c.SQW], w1st[:, sl, ec, :],
                                           xnT[:, ec, sh * c.SQW:(sh + 1) * c.SQW],
                                           start=(ec == 0), stop=(ec == c.DC - 1))
                        ev_mmh[(sh, fc)] = p.inc(ins, "pe")

                    @block.vector
                    def _(v, sh=sh, fc=fc, sl=sl, fg=fg, fgsl=fgsl):
                        p.waits(v, "v", [ev_mmh[(sh, fc)], ev_b1c])
                        pfg = (sh, fg - 2) if fg >= 2 else ((sh - 1, NFG - 2 + fg) if sh >= 1 else None)
                        if pfg in ev_g:
                            p.wait(v, "v", ev_g[pfg])
                        ins = v.tensor_scalar(hbuf[:, fgsl, fc % FCG, :], psH[:, sl, 0:c.SQW],
                                              b1c[:, fc:fc + 1], None, op0=ALU.add, op1=ALU.bypass)
                        ev_hb[(sh, fc)] = p.inc(ins, "dve")

                    if fc % FCG == FCG - 1:
                        @block.scalar
                        def _(a, sh=sh, fc=fc, fg=fg, fgsl=fgsl):
                            p.wait(a, "a", ev_hb[(sh, fc)])
                            if sh >= 1 and fg == 0:
                                p.wait(a, "a", ev_mm2[(sh - 1, c.DC - 1)])
                            ins = a.activation(gbuf[:, fg * FCG:(fg + 1) * FCG, :],
                                               hbuf[:, fgsl, :, :], AF.Gelu)
                            ev_g[(sh, fg)] = p.inc(ins, "act")

                for dc in range(c.DC):
                    sl = dc % 2
                    prev2 = None
                    if dc >= 2:
                        prev2 = (sh, dc - 2)
                    elif sh >= 1:
                        prev2 = (sh - 1, c.DC - 2 + dc)
                    prev1 = (sh, dc - 1) if dc >= 1 else ((sh - 1, c.DC - 1) if sh >= 1 else None)

                    @block.sync
                    def _(s, sh=sh, dc=dc, sl=sl, prev2=prev2):
                        p.wait(s, "s", ev_cc2)
                        if sh == 0 and dc < 2:
                            p.waits(s, "s", fence3)
                        if prev2 in ev_mm2:
                            p.wait(s, "s", ev_mm2[prev2])
                        sem = "dF20" if dc % 2 == 0 else "dF21"
                        for f0 in range(0, c.FC, 16):
                            f1 = min(f0 + 16, c.FC)
                            p.inc(s.dma_start(out=w2st[:, sl, f0:f1, :], in_=w2r[dc][:, f0:f1, :]), sem, 16)
                        ev_w2[(sh, dc)] = p.ev(sem)

                    @block.tensor
                    def _(t, sh=sh, dc=dc, sl=sl, prev2=prev2):
                        p.waits(t, "t", [ev_w2[(sh, dc)], ev_g[(sh, c.FC // min(4, c.FC) - 1)]])
                        if prev2 in ev_ev2:
                            p.wait(t, "t", ev_ev2[prev2])
                        for fc in range(c.FC):
                            ins = t.matmul(psO2[:, sl, 0:c.SQW], w2st[:, sl, fc, :], gbuf[:, fc, :],
                                           start=(fc == 0), stop=(fc == c.FC - 1))
                        ev_mm2[(sh, dc)] = p.inc(ins, "pe")

                    emit_tr_add(sh, dc, sl, prev1, psO2, psT3, ev_mm2, ev_ev2, ev_tr3, ev_add3,
                                ev_xa3[sh], b2c[:, dc:dc + 1])

                @block.gpsimd
                def _(g, sh=sh):
                    for i in range(c.NOT):
                        p.wait(g, "g", ev_add3[(sh, c.DC - 1, i)])
                        row = (sh * c.NOT + i) * 128
                        p.inc(g.dma_start(out=out_d[row:row + 128, :], in_=A_stage[:, i, :]), "dO", 16)
                    ev_odma[sh] = p.ev("dO")

            @block.gpsimd
            def _(g):
                if c.taps:
                    p.inc(g.dma_start(out=tap_g.rearrange("(fc pp) w -> pp fc w", pp=128), in_=gbuf[:, :, :]), "dO", 16)
                    p.inc(g.dma_start(out=tap_ad[:, :], in_=a_d[:, :]), "dO", 16)
                    p.inc(g.dma_start(out=tap_sth[:, :], in_=sTh[:, :]), "dO", 16)
                    p.inc(g.dma_start(out=tap_stl[:, :], in_=sTl[:, :]), "dO", 16)
                    p.inc(g.dma_start(out=tap_trsb[:, :], in_=tr_sb.rearrange("p a b j -> p (a b j)")), "dO", 16)
                    p.inc(g.dma_start(out=tap_kg[:, :], in_=k_g[:, :]), "dO", 16)
                    p.inc(g.dma_start(out=tap_vg[:, :], in_=v_g[:, :]), "dO", 16)
                    p.inc(g.dma_start(out=tap_w1q[:, :], in_=w1q[:, :]), "dO", 16)
                    p.inc(g.dma_start(out=tap_w2q[:, :], in_=w2q[:, :]), "dO", 16)
                g.wait_ge(p.sem["dO"], p.cnt["dO"])

        stA.close()

    return nc



def kernel(L0_attn_x, L0_attn_w_qkv, L0_attn_w_out, L0_ffn_w1, L0_ffn_b1, L0_ffn_w2, L0_ffn_b2):
    c = Cfg()
    x = np.ascontiguousarray(np.asarray(L0_attn_x, dtype=np.float32).reshape(c.NTOK, c.D))
    w_qkv = np.ascontiguousarray(np.asarray(L0_attn_w_qkv, dtype=np.float32))
    w_out = np.ascontiguousarray(np.asarray(L0_attn_w_out, dtype=np.float32))
    w1T = np.ascontiguousarray(np.asarray(L0_ffn_w1, dtype=np.float32).T)
    w2T = np.ascontiguousarray(np.asarray(L0_ffn_w2, dtype=np.float32).T)
    b1 = np.ascontiguousarray(np.asarray(L0_ffn_b1, dtype=np.float32))
    b2 = np.ascontiguousarray(np.asarray(L0_ffn_b2, dtype=np.float32))
    in_maps = []
    for i in range(c.NC):
        in_maps.append({
            "x_own": np.ascontiguousarray(x[i * c.T:(i + 1) * c.T]),
            "w_qkv": w_qkv,
            "w_out": w_out,
            "w1s": np.ascontiguousarray(w1T[i * c.W1R:(i + 1) * c.W1R]),
            "w2s": np.ascontiguousarray(w2T[i * c.W2R:(i + 1) * c.W2R]),
            "b1": b1,
            "b2": b2,
        })
    nc = build_program(c)
    res = run_bass_kernel_spmd(nc, in_maps, core_ids=list(range(c.NC)))
    out = np.concatenate([np.asarray(r["out"]) for r in res.results], axis=0)
    return out.reshape(c.B, c.S, c.D)


# version 14
# speedup vs baseline: 1.2635x; 1.0186x over previous
import math
import numpy as np
from contextlib import ExitStack

import concourse.bass as bass
import concourse.mybir as mybir
from concourse.bass_utils import run_bass_kernel_spmd

F32 = mybir.dt.float32
BF16 = mybir.dt.bfloat16
AF = mybir.ActivationFunctionType
ALU = mybir.AluOpType
AX = mybir.AxisListType

MAGIC = float(1.5 * 2 ** 23)
CLIP = 1.25


class Cfg:
    def __init__(self, B=2, S=4096, D=2048, H=16, DH=128, DFF=8192, NCORES=8, sqw_cap=512, stop_after=None, p25_stub=0, taps=False):
        self.B, self.S, self.D, self.H, self.DH, self.DFF, self.NC = B, S, D, H, DH, DFF, NCORES
        self.NTOK = B * S
        self.T = self.NTOK // NCORES
        self.TT = self.T // 128
        self.DC = D // 128
        self.FC = DFF // 128
        self.G = NCORES // B
        self.SR = S // self.G
        assert self.SR == self.T
        self.ST = S // 128
        self.SRT = self.SR // 128
        self.SQW = min(sqw_cap, self.T)
        self.NSQ = self.T // self.SQW
        self.NOT = self.T // self.NSQ // 128
        self.W1R = D // NCORES
        self.W2R = DFF // NCORES
        self.QW = min(2048, max(DFF, D))
        self.VW = min(512, H * DH)
        self.scale = 1.0 / math.sqrt(DH)
        self.stop_after = stop_after
        self.p25_stub = p25_stub
        self.taps = taps


class Prog:
    def __init__(self, nc):
        self.nc = nc
        self.cnt = {}
        self.sem = {}
        self.lastw = {}

    def mksems(self, st, names):
        for n in names:
            self.sem[n] = st.enter_context(self.nc.semaphore(n))
            self.cnt[n] = 0
            self.lastw[n] = {}

    def inc(self, instr, s, n=1):
        self.cnt[s] += n
        instr.then_inc(self.sem[s], n)
        return (s, self.cnt[s])

    def ev(self, s):
        return (s, self.cnt[s])

    def wait(self, eng, engname, ev):
        if ev is None:
            return
        s, v = ev
        if v <= 0:
            return
        if self.lastw[s].get(engname, 0) < v:
            eng.wait_ge(self.sem[s], v)
            self.lastw[s][engname] = v

    def waits(self, eng, engname, evs):
        for ev in evs:
            if ev is not None:
                self.wait(eng, engname, ev)


def build_program(cfg: Cfg):
    nc = bass.Bass("TRN2", target_bir_lowering=False, debug=False)
    c = cfg
    RG_BATCH = [list(range(g * c.G, (g + 1) * c.G)) for g in range(c.B)]
    RG_ALL = [list(range(c.NC))]

    x_own = nc.dram_tensor("x_own", [c.T, c.D], F32, kind="ExternalInput").ap()
    w_qkv = nc.dram_tensor("w_qkv", [c.D, 3 * c.H * c.DH], F32, kind="ExternalInput").ap()
    w_out = nc.dram_tensor("w_out", [c.H * c.DH, c.D], F32, kind="ExternalInput").ap()
    w1s = nc.dram_tensor("w1s", [c.W1R, c.DFF], F32, kind="ExternalInput").ap()
    w2s = nc.dram_tensor("w2s", [c.W2R, c.D], F32, kind="ExternalInput").ap()
    b1 = nc.dram_tensor("b1", [c.DFF], F32, kind="ExternalInput").ap()
    b2 = nc.dram_tensor("b2", [c.D], F32, kind="ExternalInput").ap()
    out_d = nc.dram_tensor("out", [c.T, c.D], F32, kind="ExternalOutput").ap()

    k_own = nc.dram_tensor("k_own", [c.H * c.DH, c.T], BF16).ap()
    v_own = nc.dram_tensor("v_own", [c.T, c.H * c.DH], BF16).ap()
    k_g = nc.dram_tensor("k_g", [c.G * c.H * c.DH, c.T], BF16).ap()
    v_g = nc.dram_tensor("v_g", [c.G * c.T, c.H * c.DH], BF16).ap()
    p_own = nc.dram_tensor("p_own", [1, 2], F32).ap()
    p_g = nc.dram_tensor("p_g", [1, 2], F32).ap()
    w1q_own = nc.dram_tensor("w1q_own", [c.W1R, c.DFF], BF16).ap()
    w2q_own = nc.dram_tensor("w2q_own", [c.W2R, c.D], BF16).ap()
    w1q = nc.dram_tensor("w1q", [c.D, c.DFF], BF16).ap()
    w2q = nc.dram_tensor("w2q", [c.DFF, c.D], BF16).ap()
    a_d = nc.dram_tensor("a_d", [c.T, c.D], F32).ap()
    if c.taps:
        tap_ad = nc.dram_tensor("tap_ad", [c.T, c.D], F32, kind="ExternalOutput").ap()
        tap_kg = nc.dram_tensor("tap_kg", [c.G * c.H * c.DH, c.T], BF16, kind="ExternalOutput").ap()
        tap_vg = nc.dram_tensor("tap_vg", [c.G * c.T, c.H * c.DH], BF16, kind="ExternalOutput").ap()
        tap_w1q = nc.dram_tensor("tap_w1q", [c.D, c.DFF], BF16, kind="ExternalOutput").ap()
        tap_w2q = nc.dram_tensor("tap_w2q", [c.DFF, c.D], BF16, kind="ExternalOutput").ap()
        tap_g = nc.dram_tensor("tap_g", [c.FC * 128, c.SQW], BF16, kind="ExternalOutput").ap()
        tap_sth = nc.dram_tensor("tap_sth", [128, c.SQW], BF16, kind="ExternalOutput").ap()
        tap_stl = nc.dram_tensor("tap_stl", [128, c.SQW], BF16, kind="ExternalOutput").ap()
        tap_trsb = nc.dram_tensor("tap_trsb", [128, 2 * 2 * 128], F32, kind="ExternalOutput").ap()

    p = Prog(nc)

    st = ExitStack()
    e = st.enter_context
    with st:
        p.mksems(st, ["dB", "dX0", "dX1", "dQ0", "dQ1", "dW0", "dW1", "dWV0", "dWV1",
                      "dKst0", "dKst1", "dVst0", "dVst1", "dP", "dQst0", "dQst1",
                      "dKV0", "dKV1", "dWo0", "dWo1", "dA", "dAD", "dF10", "dF11",
                      "dF20", "dF21", "pe", "act", "dve", "pool",
                      "ccS", "ccK", "ccV", "cc1", "cc2", "dO"])

        xnT = e(nc.sbuf_tensor("xnT", [128, c.DC, c.T], BF16))
        identb = e(nc.sbuf_tensor("identb", [128, 128], BF16))
        identf = e(nc.sbuf_tensor("identf", [128, 128], F32))
        onesb_row = e(nc.sbuf_tensor("onesb_row", [1, 128], BF16))
        onesb_col = e(nc.sbuf_tensor("onesb_col", [128, 1], BF16))
        ones_row = e(nc.sbuf_tensor("ones_row", [1, 128], F32))
        ones_col = e(nc.sbuf_tensor("ones_col", [128, 1], F32))
        b1c = e(nc.sbuf_tensor("b1c", [128, c.FC], F32))
        b2c = e(nc.sbuf_tensor("b2c", [128, c.DC], F32))
        brow = e(nc.sbuf_tensor("brow", [max(c.FC, c.DC), 128], F32))
        rsc = e(nc.sbuf_tensor("rsc", [128, 2], F32))
        scl = e(nc.sbuf_tensor("scl", [1, 4], F32))
        sT = e(nc.sbuf_tensor("sT", [128, c.SQW], F32))
        sTh = e(nc.sbuf_tensor("sTh", [128, c.SQW], BF16))
        sTl = e(nc.sbuf_tensor("sTl", [128, c.SQW], BF16))
        tr_sb = e(nc.sbuf_tensor("tr_sb", [128, 2, 2, 128], F32))

        block = e(nc.Block())

        @block.gpsimd
        def _(g):
            for ident in (identb, identf):
                g.memset(ident[:, :], 0.0).then_inc(p.sem["pool"], 1)
                p.cnt["pool"] += 1
                g.wait_ge(p.sem["pool"], p.cnt["pool"])
                ins = g.affine_select(out=ident[:, :], in_=ident[:, :],
                                      compare_op=ALU.not_equal, fill=1.0,
                                      base=0, pattern=[[-1, 128]], channel_multiplier=1)
                p.inc(ins, "pool")
            p.inc(g.memset(onesb_row[:, :], 1.0), "pool")
            p.inc(g.memset(onesb_col[:, :], 1.0), "pool")
            p.inc(g.memset(ones_row[:, :], 1.0), "pool")
            p.inc(g.memset(ones_col[:, :], 1.0), "pool")
        ev_const = p.ev("pool")

        with nc.psum_tensor("psB", [128, max(c.FC, c.DC)], F32) as psB:
            @block.sync
            def _(s):
                p.inc(s.dma_start(out=brow[0:c.FC, :], in_=b1.rearrange("(a p) -> a p", p=128)), "dB", 16)
            ev_b1row = p.ev("dB")

            @block.tensor
            def _(t):
                p.waits(t, "t", [ev_b1row, ev_const])
                p.inc(t.transpose(psB[:, 0:c.FC], brow[0:c.FC, :], identf[0:c.FC, 0:c.FC]), "pe")
            ev_tb1 = p.ev("pe")

            @block.vector
            def _(v):
                p.wait(v, "v", ev_tb1)
                p.inc(v.tensor_copy(b1c[:, :], psB[:, 0:c.FC]), "dve")
            ev_b1c = p.ev("dve")

            @block.sync
            def _(s):
                p.wait(s, "s", ev_tb1)
                p.inc(s.dma_start(out=brow[0:c.DC, :], in_=b2.rearrange("(a p) -> a p", p=128)), "dB", 16)
            ev_b2row = p.ev("dB")

            @block.tensor
            def _(t):
                p.waits(t, "t", [ev_b2row, ev_b1c])
                p.inc(t.transpose(psB[:, 0:c.DC], brow[0:c.DC, :], identf[0:c.DC, 0:c.DC]), "pe")
            ev_tb2 = p.ev("pe")

            @block.vector
            def _(v):
                p.wait(v, "v", ev_tb2)
                p.inc(v.tensor_copy(b2c[:, :], psB[:, 0:c.DC]), "dve")
            ev_b2c = p.ev("dve")

        st0 = ExitStack()
        with st0:
            e0 = st0.enter_context
            xstage = e0(nc.sbuf_tensor("xstage", [128, 2, c.D], F32))
            xn_tok = e0(nc.sbuf_tensor("xn_tok", [128, 2, c.D], BF16))
            lncols = e0(nc.sbuf_tensor("lncols", [128, 8], F32))
            ln_scr = e0(nc.sbuf_tensor("ln_scr", [128, c.QW], F32))
            psTr = e0(nc.psum_tensor("psTr", [128, 2, 1024], BF16))

            ev_x = [None] * c.TT
            ev_sq = [None] * c.TT
            ev_xn = [None] * c.TT
            ev_tr = {}
            ev_evac = {}

            for tt in range(c.TT):
                sl = tt % 2

                @block.sync
                def _(s, tt=tt, sl=sl):
                    if tt >= 2:
                        p.wait(s, "s", ev_xn[tt - 2])
                    sem = "dX0" if tt % 2 == 0 else "dX1"
                    p.inc(s.dma_start(out=xstage[:, sl, :], in_=x_own[tt * 128:(tt + 1) * 128, :]), sem, 16)
                    ev_x[tt] = p.ev(sem)

                @block.vector
                def _(v, tt=tt, sl=sl):
                    p.wait(v, "v", ev_x[tt])
                    p.inc(v.tensor_reduce(lncols[:, 0:1], xstage[:, sl, :], axis=AX.X, op=ALU.add), "dve")
                ev_s1 = p.ev("dve")

                @block.vector
                def _(v, tt=tt, sl=sl):
                    p.wait(v, "v", ev_x[tt])
                    if tt >= 1:
                        p.wait(v, "v", ev_sq[tt - 1])
                    p.inc(v.tensor_tensor(ln_scr[:, 0:c.D], xstage[:, sl, :], xstage[:, sl, :], op=ALU.mult), "dve")
                    p.wait(v, "v", p.ev("dve"))
                    ins = v.tensor_reduce(lncols[:, 1:2], ln_scr[:, 0:c.D], axis=AX.X, op=ALU.add)
                    ev_sq[tt] = p.inc(ins, "dve")

                @block.vector
                def _(v, tt=tt, ev_s1=ev_s1):
                    p.waits(v, "v", [ev_sq[tt], ev_s1])
                    inv_d = 1.0 / c.D
                    p.inc(v.tensor_scalar(lncols[:, 2:3], lncols[:, 0:1], inv_d, None, op0=ALU.mult, op1=ALU.bypass), "dve")
                    p.wait(v, "v", p.ev("dve"))
                    p.inc(v.tensor_scalar(lncols[:, 3:4], lncols[:, 1:2], inv_d, 1e-5, op0=ALU.mult, op1=ALU.add), "dve")
                    p.wait(v, "v", p.ev("dve"))
                    p.inc(v.tensor_tensor(lncols[:, 4:5], lncols[:, 2:3], lncols[:, 2:3], op=ALU.mult), "dve")
                    p.wait(v, "v", p.ev("dve"))
                    p.inc(v.tensor_tensor(lncols[:, 3:4], lncols[:, 3:4], lncols[:, 4:5], op=ALU.subtract), "dve")
                ev_var = p.ev("dve")

                @block.scalar
                def _(a, tt=tt):
                    p.wait(a, "a", ev_var)
                    p.inc(a.activation(lncols[:, 5:6], lncols[:, 3:4], AF.Ln), "act")
                    p.wait(a, "a", p.ev("act"))
                    p.inc(a.activation(lncols[:, 5:6], lncols[:, 5:6], AF.Exp, scale=-0.5), "act")
                ev_rstd = p.ev("act")

                @block.vector
                def _(v, tt=tt):
                    p.wait(v, "v", ev_rstd)
                    p.inc(v.tensor_tensor(lncols[:, 6:7], lncols[:, 2:3], lncols[:, 5:6], op=ALU.mult), "dve")
                    p.wait(v, "v", p.ev("dve"))
                    p.inc(v.tensor_scalar(lncols[:, 6:7], lncols[:, 6:7], -1.0, None, op0=ALU.mult, op1=ALU.bypass), "dve")
                ev_stats = p.ev("dve")

                @block.vector
                def _(v, tt=tt, sl=sl):
                    p.wait(v, "v", ev_stats)
                    if tt >= 2:
                        p.wait(v, "v", ev_tr[(tt - 2, c.DC - 1)])
                    ins = v.tensor_scalar(xn_tok[:, sl, :], xstage[:, sl, :],
                                          lncols[:, 5:6], lncols[:, 6:7], op0=ALU.mult, op1=ALU.add)
                    ev_xn[tt] = p.inc(ins, "dve")

                for ec in range(c.DC):
                    pslot = ec % 2

                    @block.tensor
                    def _(t, tt=tt, sl=sl, ec=ec, pslot=pslot):
                        p.waits(t, "t", [ev_xn[tt], ev_const])
                        prev = None
                        if ec >= 2:
                            prev = (tt, ec - 2)
                        elif tt >= 1:
                            prev = (tt - 1, c.DC - 2 + ec)
                        if prev in ev_evac:
                            p.wait(t, "t", ev_evac[prev])
                        ins = t.transpose(psTr[:, pslot, 0:128], xn_tok[:, sl, ec * 128:(ec + 1) * 128], identb[:, :])
                        ev_tr[(tt, ec)] = p.inc(ins, "pe")

                    @block.vector
                    def _(v, tt=tt, ec=ec, pslot=pslot):
                        p.wait(v, "v", ev_tr[(tt, ec)])
                        ins = v.tensor_copy(xnT[:, ec, tt * 128:(tt + 1) * 128], psTr[:, pslot, 0:128])
                        ev_evac[(tt, ec)] = p.inc(ins, "dve")

            ev_xnT = p.ev("dve")

            qtiles = []
            for which, R, W in (("w1", c.W1R, c.DFF), ("w2", c.W2R, c.D)):
                for i in range((R + 127) // 128):
                    for j in range((W + c.QW - 1) // c.QW):
                        qtiles.append((which, i, j, min(128, R - i * 128), min(c.QW, W - j * c.QW)))

            with (
                nc.sbuf_tensor("qin", [128, 2, c.QW], F32) as qin,
                nc.sbuf_tensor("acol", [128, 3], F32) as acol,
                nc.psum_tensor("psS", [1, 2], F32) as psS,
            ):
                ev_qin = {}
                ev_abs = {}

                @block.vector
                def _(v):
                    p.inc(v.memset(acol[:, 1:3], 0.0), "dve")
                ev_az = p.ev("dve")

                for idx, (which, i, j, rows, w) in enumerate(qtiles):
                    sl = idx % 2
                    src = w1s if which == "w1" else w2s

                    @block.sync
                    def _(s, idx=idx, sl=sl, src=src, i=i, j=j, rows=rows, w=w):
                        if idx >= 2:
                            p.wait(s, "s", ev_abs[idx - 2])
                        sem = "dQ0" if idx % 2 == 0 else "dQ1"
                        p.inc(s.dma_start(out=qin[0:rows, sl, 0:w],
                                          in_=src[i * 128:i * 128 + rows, j * c.QW:j * c.QW + w]), sem, 16)
                        ev_qin[idx] = p.ev(sem)

                    @block.vector
                    def _(v, idx=idx, sl=sl, rows=rows, w=w):
                        p.wait(v, "v", ev_qin[idx])
                        if idx >= 1:
                            p.wait(v, "v", ev_abs[idx - 1])
                            p.wait(v, "v", p.ev("dve"))
                        ins = v.tensor_reduce(acol[0:rows, 0:1], qin[0:rows, sl, 0:w], axis=AX.X,
                                              op=ALU.add, apply_absolute_value=True)
                        ev_abs[idx] = p.inc(ins, "dve")

                    @block.vector
                    def _(v, idx=idx, which=which, rows=rows):
                        p.waits(v, "v", [ev_abs[idx], ev_az])
                        col = 1 if which == "w1" else 2
                        p.wait(v, "v", p.ev("dve"))
                        ins = v.tensor_tensor(acol[0:rows, col:col + 1], acol[0:rows, col:col + 1],
                                              acol[0:rows, 0:1], op=ALU.add)
                        p.inc(ins, "dve")
                ev_acc = p.ev("dve")

                @block.tensor
                def _(t):
                    p.waits(t, "t", [ev_acc, ev_const])
                    p.inc(t.matmul(psS[0:1, 0:1], ones_col[:, :], acol[:, 1:2], start=True, stop=True), "pe")
                    p.inc(t.matmul(psS[0:1, 1:2], ones_col[:, :], acol[:, 2:3], start=True, stop=True), "pe")
                ev_psS = p.ev("pe")

                @block.vector
                def _(v):
                    p.wait(v, "v", ev_psS)
                    p.inc(v.tensor_copy(scl[0:1, 0:2], psS[0:1, 0:2]), "dve")
                ev_partial = p.ev("dve")

        def emit_dummy_out():
            dummy = nc.dram_tensor("dummy_src", [c.T, c.D], F32).ap()
            @block.gpsimd
            def _(g):
                p.waits(g, "g", [p.ev("pe"), p.ev("act"), p.ev("dve")])
                p.inc(g.dma_start(out=out_d[:, :], in_=dummy[:, :]), "dO", 16)
                g.wait_ge(p.sem["dO"], p.cnt["dO"])

        if c.stop_after == "p0":
            emit_dummy_out()
            return nc

        stq = ExitStack()
        eq = stq.enter_context
        qT = eq(nc.sbuf_tensor("qT", [128, c.H, c.T], BF16))
        combn = eq(nc.sbuf_tensor("combn", [128, c.H, c.T], BF16))

        fence0 = [p.ev("pe"), p.ev("act"), p.ev("dve")]
        st1 = ExitStack()
        with st1:
            e1 = st1.enter_context
            wst = e1(nc.sbuf_tensor("wst", [128, 2, c.DC, 128], BF16))
            wvm = e1(nc.sbuf_tensor("wvm", [128, 2, c.DC, c.VW], BF16))
            kstage = e1(nc.sbuf_tensor("kstage", [128, 2, c.T], BF16))
            vstage = e1(nc.sbuf_tensor("vstage", [128, 2, c.VW], BF16))
            psQK = e1(nc.psum_tensor("psQK", [128, 2, c.NSQ, 512], F32))
            psV = e1(nc.psum_tensor("psV", [128, 2, 512], F32))

            NQK = 2 * c.H
            ev_w = [None] * NQK
            ev_mmqk = [None] * NQK
            ev_evqk = [None] * NQK
            ev_kdma = [None] * c.H
            w_qkv_r = w_qkv.rearrange("(ec pp) (hc j) -> hc pp ec j", pp=128, j=128)

            oc_order = list(range(c.H, 2 * c.H)) + list(range(c.H))
            for pos, oc in enumerate(oc_order):
                sl = pos % 2
                prev_oc = oc_order[pos - 2] if pos >= 2 else None

                @block.gpsimd
                def _(g, oc=oc, sl=sl, pos=pos, prev_oc=prev_oc):
                    if pos < 2:
                        p.waits(g, "g", fence0)
                    else:
                        p.wait(g, "g", ev_mmqk[prev_oc])
                    sem = "dW0" if pos % 2 == 0 else "dW1"
                    p.inc(g.dma_start(out=wst[:, sl, :, :], in_=w_qkv_r[oc]), sem, 16)
                    ev_w[oc] = p.ev(sem)

                @block.tensor
                def _(t, oc=oc, sl=sl, pos=pos, prev_oc=prev_oc):
                    p.waits(t, "t", [ev_w[oc], ev_xnT])
                    if pos >= 2:
                        p.wait(t, "t", ev_evqk[prev_oc])
                    for sq in range(c.NSQ):
                        for ec in range(c.DC):
                            ins = t.matmul(psQK[:, sl, sq, 0:c.SQW],
                                           wst[:, sl, ec, :],
                                           xnT[:, ec, sq * c.SQW:(sq + 1) * c.SQW],
                                           start=(ec == 0), stop=(ec == c.DC - 1))
                    ev_mmqk[oc] = p.inc(ins, "pe")

                if oc < c.H:
                    @block.vector
                    def _(v, oc=oc, sl=sl):
                        p.wait(v, "v", ev_mmqk[oc])
                        ins = v.tensor_copy(qT[:, oc, :], psQK[:, sl, :, 0:c.SQW])
                        ev_evqk[oc] = p.inc(ins, "dve")
                else:
                    h = oc - c.H

                    @block.vector
                    def _(v, oc=oc, sl=sl, h=h):
                        p.wait(v, "v", ev_mmqk[oc])
                        if h >= 2:
                            p.wait(v, "v", ev_kdma[h - 2])
                        ins = v.tensor_copy(kstage[:, h % 2, :], psQK[:, sl, :, 0:c.SQW])
                        ev_evqk[oc] = p.inc(ins, "dve")

                    @block.gpsimd
                    def _(g, h=h):
                        p.wait(g, "g", ev_evqk[c.H + h])
                        ins = g.dma_start(out=k_own[h * 128:(h + 1) * 128, :], in_=kstage[:, h % 2, :])
                        ev_kdma[h] = p.inc(ins, "dKst0" if h % 2 == 0 else "dKst1", 16)

                    if h == c.H - 1:
                        @block.gpsimd
                        def _(g):
                            p.wait(g, "g", ev_kdma[c.H - 1])
                            if c.H >= 2:
                                p.wait(g, "g", ev_kdma[c.H - 2])
                            ins = g.collective_compute("AllGather", ALU.bypass, replica_groups=RG_BATCH,
                                                       ins=[k_own.opt()], outs=[k_g.opt()])
                            p.inc(ins, "ccK")

            ev_qT = ev_evqk[c.H - 1]

            NVG = (c.H * c.DH) // c.VW
            ev_wv = [None] * NVG
            ev_mmv = {}
            ev_evv = {}
            ev_vdma = [None] * c.TT
            w_qkv_v = w_qkv.rearrange("(ec pp) j -> pp ec j", pp=128)

            ev_vst = {}
            for vg in range(NVG):
                sl = vg % 2

                @block.gpsimd
                def _(g, vg=vg, sl=sl):
                    if vg < 2:
                        p.waits(g, "g", fence0)
                    else:
                        p.wait(g, "g", ev_mmv[(vg - 2, c.TT - 1)])
                    base = 2 * c.H * c.DH
                    sem = "dWV0" if vg % 2 == 0 else "dWV1"
                    p.inc(g.dma_start(out=wvm[:, sl, :, :],
                                      in_=w_qkv_v[:, :, base + vg * c.VW: base + (vg + 1) * c.VW]), sem, 16)
                    ev_wv[vg] = p.ev(sem)

                for tt in range(c.TT):
                    k = vg * c.TT + tt
                    ksl = k % 2

                    @block.tensor
                    def _(t, vg=vg, sl=sl, tt=tt):
                        p.waits(t, "t", [ev_wv[vg], ev_xnT])
                        prev = None
                        if tt >= 2:
                            prev = (vg, tt - 2)
                        elif vg >= 1:
                            prev = (vg - 1, c.TT - 2 + tt) if c.TT >= 2 else (vg - 1, tt)
                        if prev in ev_evv:
                            p.wait(t, "t", ev_evv[prev])
                        for ec in range(c.DC):
                            ins = t.matmul(psV[:, tt % 2, 0:c.VW], xnT[:, ec, tt * 128:(tt + 1) * 128],
                                           wvm[:, sl, ec, :], start=(ec == 0), stop=(ec == c.DC - 1))
                        ev_mmv[(vg, tt)] = p.inc(ins, "pe")

                    @block.vector
                    def _(v, vg=vg, tt=tt, k=k, ksl=ksl):
                        p.wait(v, "v", ev_mmv[(vg, tt)])
                        if k >= 2:
                            p.wait(v, "v", ev_vst[k - 2])
                        ins = v.tensor_copy(vstage[:, ksl, 0:c.VW], psV[:, tt % 2, 0:c.VW])
                        ev_evv[(vg, tt)] = p.inc(ins, "dve")

                    @block.gpsimd
                    def _(g, vg=vg, tt=tt, k=k, ksl=ksl):
                        p.wait(g, "g", ev_evv[(vg, tt)])
                        ins = g.dma_start(out=v_own[tt * 128:(tt + 1) * 128, vg * c.VW:(vg + 1) * c.VW],
                                          in_=vstage[:, ksl, 0:c.VW])
                        ev_vst[k] = p.inc(ins, "dVst0" if ksl == 0 else "dVst1", 16)

            for tt in range(c.TT):
                ev_vdma[tt] = ev_vst[(NVG - 1) * c.TT + tt]

            @block.gpsimd
            def _(g):
                p.wait(g, "g", ev_vdma[c.TT - 1])
                if c.TT >= 2:
                    p.wait(g, "g", ev_vdma[c.TT - 2])
                ins = g.collective_compute("AllGather", ALU.bypass, replica_groups=RG_BATCH,
                                           ins=[v_own.opt()], outs=[v_g.opt()])
                p.inc(ins, "ccV")
            ev_ccK = p.ev("ccK")
            ev_ccV = p.ev("ccV")

        @block.gpsimd
        def _(g):
            p.wait(g, "g", ev_partial)
            p.inc(g.dma_start(out=p_own[:, :], in_=scl[0:1, 0:2]), "dP", 16)
            p.wait(g, "g", p.ev("dP"))
            ins = g.collective_compute("AllReduce", ALU.add, replica_groups=RG_ALL,
                                       ins=[p_own.opt()], outs=[p_g.opt()])
            p.inc(ins, "ccS")
            p.wait(g, "g", p.ev("ccS"))
            p.inc(g.dma_start(out=scl[0:1, 2:4], in_=p_g[:, :]), "dP", 16)
        ev_pg = p.ev("dP")

        with nc.psum_tensor("psBc", [128, 2], F32) as psBc:
            @block.vector
            def _(v):
                p.wait(v, "v", ev_pg)
                n1 = 1.0 / (c.D * c.DFF)
                p.inc(v.tensor_scalar(scl[0:1, 2:3], scl[0:1, 2:3], n1, 1e-8, op0=ALU.mult, op1=ALU.max), "dve")
                p.wait(v, "v", p.ev("dve"))
                p.inc(v.tensor_scalar(scl[0:1, 3:4], scl[0:1, 3:4], n1, 1e-8, op0=ALU.mult, op1=ALU.max), "dve")
                p.wait(v, "v", p.ev("dve"))
                p.inc(v.reciprocal(scl[0:1, 2:4], scl[0:1, 2:4]), "dve")
            ev_rs = p.ev("dve")

            @block.tensor
            def _(t):
                p.waits(t, "t", [ev_rs, ev_const])
                p.inc(t.matmul(psBc[:, 0:2], ones_row[0:1, :], scl[0:1, 2:4], start=True, stop=True), "pe")
            ev_bc = p.ev("pe")

            @block.vector
            def _(v):
                p.wait(v, "v", ev_bc)
                p.inc(v.tensor_copy(rsc[:, 0:2], psBc[:, 0:2]), "dve")
            ev_rsc = p.ev("dve")


        stq2 = ExitStack()
        if True:
            qin2 = stq2.enter_context(nc.sbuf_tensor("qin2", [128, 2, c.QW], F32, side="right"))
            qout = stq2.enter_context(nc.sbuf_tensor("qout", [128, 2, c.QW], BF16, side="right"))
            fence1 = [p.ev("pe"), p.ev("dve"), p.ev("act"), ev_vdma[c.TT - 1], ev_vdma[c.TT - 2] if c.TT >= 2 else None]
            ev_qdma = {}
            ev_qv = {}
            n1t = len([1 for q in qtiles if q[0] == "w1"])
            for idx, (which, i, j, rows, w) in enumerate(qtiles):
                sl = idx % 2
                src = w1s if which == "w1" else w2s
                dst = w1q_own if which == "w1" else w2q_own
                rcol = 0 if which == "w1" else 1

                @block.sync
                def _(s, idx=idx, sl=sl, src=src, i=i, j=j, rows=rows, w=w):
                    if idx < 2:
                        p.waits(s, "s", fence1)
                    else:
                        p.wait(s, "s", ev_qv[idx - 2])
                    sem = "dQ0" if idx % 2 == 0 else "dQ1"
                    p.inc(s.dma_start(out=qin2[0:rows, sl, 0:w],
                                      in_=src[i * 128:i * 128 + rows, j * c.QW:j * c.QW + w]), sem, 16)
                    ev_qdma[idx] = p.ev(sem)

                @block.vector
                def _(v, idx=idx, sl=sl, rows=rows, w=w, rcol=rcol):
                    p.waits(v, "v", [ev_qdma[idx], ev_rsc])
                    a = qin2[0:rows, sl, 0:w]
                    p.inc(v.tensor_scalar(a, a, rsc[0:rows, rcol:rcol + 1], CLIP, op0=ALU.mult, op1=ALU.min), "dve")
                    p.wait(v, "v", p.ev("dve"))
                    p.inc(v.tensor_scalar(a, a, -CLIP, MAGIC, op0=ALU.max, op1=ALU.add), "dve")
                    p.wait(v, "v", p.ev("dve"))
                    if idx >= 2:
                        p.wait(v, "v", ev_qdma.get(("out", idx - 2)))
                    ins = v.tensor_scalar(qout[0:rows, sl, 0:w], a, MAGIC, None, op0=ALU.subtract, op1=ALU.bypass)
                    ev_qv[idx] = p.inc(ins, "dve")

                @block.gpsimd
                def _(g, idx=idx, sl=sl, dst=dst, i=i, j=j, rows=rows, w=w):
                    p.wait(g, "g", ev_qv[idx])
                    ins = g.dma_start(out=dst[i * 128:i * 128 + rows, j * c.QW:j * c.QW + w],
                                      in_=qout[0:rows, sl, 0:w])
                    ev_qdma[("out", idx)] = p.inc(ins, "dQst0" if idx % 2 == 0 else "dQst1", 16)

            @block.gpsimd
            def _(g):
                p.wait(g, "g", ev_qdma[("out", n1t - 1)])
                if n1t >= 2:
                    p.wait(g, "g", ev_qdma[("out", n1t - 2)])
                ins = g.collective_compute("AllGather", ALU.bypass, replica_groups=RG_ALL,
                                           ins=[w1q_own.opt()], outs=[w1q.opt()])
                p.inc(ins, "cc1")
                p.wait(g, "g", ev_qdma[("out", len(qtiles) - 1)])
                if len(qtiles) >= 2:
                    p.wait(g, "g", ev_qdma[("out", len(qtiles) - 2)])
                ins = g.collective_compute("AllGather", ALU.bypass, replica_groups=RG_ALL,
                                           ins=[w2q_own.opt()], outs=[w2q.opt()])
                p.inc(ins, "cc2")
        ev_cc1 = p.ev("cc1")
        ev_cc2 = p.ev("cc2")

        if c.stop_after == "p1":
            emit_dummy_out()
            stq2.close()
            stq.close()
            return nc

        st2 = ExitStack()
        with st2:
            e2 = st2.enter_context
            kbuf = e2(nc.sbuf_tensor("kbuf", [128, 2, c.G, c.SR], BF16))
            vbuf = e2(nc.sbuf_tensor("vbuf", [128, 2, c.ST, c.DH], BF16))
            Ebuf = e2(nc.sbuf_tensor("Ebuf", [128, 3, 2, c.NSQ, c.SQW], BF16))
            Zp = e2(nc.sbuf_tensor("Zp", [128, 2, c.NSQ, c.SQW], BF16))
            rzb = e2(nc.sbuf_tensor("rzb", [1, c.NSQ, c.SQW], F32))
            rzbh = e2(nc.sbuf_tensor("rzbh", [1, c.NSQ, c.SQW], BF16))
            rzbc = e2(nc.sbuf_tensor("rzbc", [128, c.NSQ, c.SQW], F32))
            psL = e2(nc.psum_tensor("psL", [128, 2, c.NSQ, 512], F32))
            psC = e2(nc.psum_tensor("psC", [128, c.NSQ, 512], F32))
            psZ = e2(nc.psum_tensor("psZ", [128, c.NSQ, 512], F32))

            ev_kv = [None] * c.H
            ev_exp = {}
            ev_mmL = {}
            ev_mmC = {}
            ev_zadd = {}
            ev_comb = [None] * c.H
            ev_rzc = [None] * c.H
            ev_onemm = [None] * c.H
            ev_bcmm = [None] * c.H
            ev_rzbc = [None] * c.H

            k_gr = k_g.rearrange("(r hc pp) t -> r hc pp t", r=c.G, pp=128)
            v_gr = v_g.rearrange("(r o pp) (hc dv) -> r hc pp o dv", r=c.G, pp=128, dv=c.DH)

            def emit_head_dma(h):
                sl = h % 2

                @block.sync
                def _(s, h=h, sl=sl):
                    p.waits(s, "s", [ev_ccK, ev_ccV])
                    if h >= 2:
                        p.wait(s, "s", ev_mmL[(h - 2, c.ST - 1)])
                        p.wait(s, "s", ev_mmC[(h - 2, c.ST - 1)])
                    sem = "dKV0" if h % 2 == 0 else "dKV1"
                    for r in range(c.G):
                        p.inc(s.dma_start(out=kbuf[:, sl, r, :], in_=k_gr[r, h]), sem, 16)
                        p.inc(s.dma_start(out=vbuf[:, sl, r * c.SRT:(r + 1) * c.SRT, :], in_=v_gr[r, h]), sem, 16)
                    ev_kv[h] = p.ev(sem)

            emit_head_dma(0)
            emit_head_dma(1)

            ev_zaddp = {}
            NG = c.ST // 2

            def emit_mmL(h, g):
                sl = h % 2
                for t in (2 * g, 2 * g + 1):
                    @block.tensor
                    def _(tns, h=h, sl=sl, t=t, g=g):
                        p.waits(tns, "t", [ev_kv[h], ev_qT])
                        if g >= 1:
                            p.wait(tns, "t", ev_exp[(h, g - 1)])
                        elif h >= 1:
                            p.wait(tns, "t", ev_exp[(h - 1, NG - 1)])
                        r, col = t // c.SRT, (t % c.SRT) * 128
                        for sq in range(c.NSQ):
                            ins = tns.matmul(psL[:, t % 2, sq, 0:c.SQW], kbuf[:, sl, r, col:col + 128],
                                             qT[:, h, sq * c.SQW:(sq + 1) * c.SQW], start=True, stop=True)
                        ev_mmL[(h, t)] = p.inc(ins, "pe")

            for h in range(c.H):
                sl = h % 2
                if h == 0:
                    emit_mmL(h, 0)
                for g in range(NG):
                    gsl = (h * NG + g) % 3
                    ts_ = (2 * g, 2 * g + 1)

                    @block.scalar
                    def _(a, h=h, g=g, gsl=gsl, ts_=ts_):
                        p.wait(a, "a", ev_mmL[(h, ts_[-1])])
                        gp3 = h * NG + g - 3
                        pg = (gp3 // NG, gp3 % NG) if gp3 >= 0 else None
                        if pg is not None:
                            pt1 = 2 * pg[1] + 1
                            p.waits(a, "a", [ev_mmC.get((pg[0], pt1)), ev_zadd.get((pg[0], pt1)),
                                             ev_zaddp.get((pg[0], pt1))])
                        ins = a.activation(Ebuf[:, gsl, :, :, :], psL[:, :, :, 0:c.SQW], AF.Exp, scale=c.scale)
                        ev_exp[(h, g)] = p.inc(ins, "act")

                    if g + 1 < NG:
                        emit_mmL(h, g + 1)
                    elif h + 1 < c.H:
                        emit_mmL(h + 1, 0)

                    for t in ts_:
                        @block.tensor
                        def _(tns, h=h, sl=sl, t=t, g=g, gsl=gsl):
                            p.wait(tns, "t", ev_exp[(h, g)])
                            if t == 0 and h >= 1:
                                p.wait(tns, "t", ev_comb[h - 1])
                            for sq in range(c.NSQ):
                                ins = tns.matmul(psC[:, sq, 0:c.SQW], vbuf[:, sl, t, :],
                                                 Ebuf[:, gsl, t % 2, sq, :],
                                                 start=(t == 0), stop=(t == c.ST - 1))
                            ev_mmC[(h, t)] = p.inc(ins, "pe")

                    @block.vector
                    def _(v, h=h, g=g, gsl=gsl, ts_=ts_):
                        p.wait(v, "v", ev_exp[(h, g)])
                        if g == 0 and h >= 1:
                            p.wait(v, "v", ev_onemm[h - 1])
                        t = ts_[0]
                        p.wait(v, "v", p.ev("dve"))
                        for sq in range(c.NSQ):
                            if sq:
                                p.wait(v, "v", p.ev("dve"))
                            if g == 0:
                                ins = v.tensor_copy(Zp[:, 0, sq, :], Ebuf[:, gsl, 0, sq, :])
                            else:
                                ins = v.tensor_tensor(Zp[:, 0, sq, :], Zp[:, 0, sq, :],
                                                      Ebuf[:, gsl, 0, sq, :], op=ALU.add)
                        ev_zadd[(h, t)] = p.inc(ins, "dve")
                        ev_zadd[(h, ts_[1])] = ev_zadd[(h, t)]

                    @block.gpsimd
                    def _(gp, h=h, g=g, gsl=gsl, ts_=ts_):
                        p.wait(gp, "g", ev_exp[(h, g)])
                        if g == 0 and h >= 1:
                            p.wait(gp, "g", ev_onemm[h - 1])
                        p.wait(gp, "g", p.ev("pool"))
                        for sq in range(c.NSQ):
                            if sq:
                                p.wait(gp, "g", p.ev("pool"))
                            if g == 0:
                                ins = gp.tensor_copy(Zp[:, 1, sq, :], Ebuf[:, gsl, 1, sq, :])
                            else:
                                ins = gp.tensor_tensor(Zp[:, 1, sq, :], Zp[:, 1, sq, :],
                                                       Ebuf[:, gsl, 1, sq, :], op=ALU.add)
                        ev_zaddp[(h, ts_[1])] = p.inc(ins, "pool")
                        ev_zaddp[(h, ts_[0])] = ev_zaddp[(h, ts_[1])]

                if h + 2 < c.H:
                    emit_head_dma(h + 2)

                @block.tensor
                def _(tns, h=h):
                    p.wait(tns, "t", ev_zadd[(h, c.ST - 1)])
                    if ev_zaddp.get((h, c.ST - 1)) is not None:
                        p.wait(tns, "t", ev_zaddp[(h, c.ST - 1)])
                    if h >= 1:
                        p.wait(tns, "t", ev_rzbc[h - 1])
                    for sq in range(c.NSQ):
                        tns.matmul(psZ[0:1, sq, 0:c.SQW], onesb_col[:, :], Zp[:, 0, sq, :], start=True, stop=False)
                        ins = tns.matmul(psZ[0:1, sq, 0:c.SQW], onesb_col[:, :], Zp[:, 1, sq, :], start=False, stop=True)
                    ev_onemm[h] = p.inc(ins, "pe")

                @block.vector
                def _(v, h=h):
                    p.wait(v, "v", ev_onemm[h])
                    ins = v.tensor_copy(rzbh[0:1, :, :], psZ[0:1, :, 0:c.SQW])
                    ev_rzc[h] = p.inc(ins, "dve")

                @block.tensor
                def _(tns, h=h):
                    p.wait(tns, "t", ev_rzc[h])
                    for sq in range(c.NSQ):
                        ins = tns.matmul(psZ[:, sq, 0:c.SQW], onesb_row[0:1, :], rzbh[0:1, sq, :], start=True, stop=True)
                    ev_bcmm[h] = p.inc(ins, "pe")

                @block.vector
                def _(v, h=h):
                    p.wait(v, "v", ev_bcmm[h])
                    ev_rzbc[h] = p.inc(v.reciprocal(rzbc[:, :, :], psZ[:, :, 0:c.SQW]), "dve")
                    p.wait(v, "v", p.ev("dve"))
                    p.wait(v, "v", ev_mmC[(h, c.ST - 1)])
                    for sq in range(c.NSQ):
                        if sq:
                            p.wait(v, "v", p.ev("dve"))
                        ins = v.tensor_tensor(combn[:, h, sq * c.SQW:(sq + 1) * c.SQW],
                                              psC[:, sq, 0:c.SQW], rzbc[:, sq, :], op=ALU.mult)
                    ev_comb[h] = p.inc(ins, "dve")

            ev_attn = ev_comb[c.H - 1]

        def emit_tr_add(sh, dc, sl, prev1, psIn, psT, ev_mm_d, ev_evo_d, ev_tr_d, ev_add_d,
                        ev_xload, bcol):
            NH = max(1, c.NOT // 2)
            for hf in range(NH):
                grp = (sh, dc, "g", hf)
                pgrp = (sh, dc, "g", hf - 1) if hf >= 1 else \
                       ((prev1[0], prev1[1], "g", NH - 1) if prev1 is not None else None)
                tiles = [i for i in range(2 * hf, min(2 * hf + 2, c.NOT))]
                w0, w1_ = tiles[0] * 128, (tiles[-1] + 1) * 128

                @block.vector
                def _(v, sh=sh, dc=dc, sl=sl, grp=grp, pgrp=pgrp, w0=w0, w1_=w1_, hf=hf):
                    p.wait(v, "v", ev_mm_d[(sh, dc)])
                    if bcol is not None:
                        p.wait(v, "v", ev_b2c)
                    if pgrp is not None and ("sth", pgrp) in ev_add_d:
                        p.wait(v, "v", ev_add_d[("sth", pgrp)])
                    if bcol is not None:
                        ins = v.tensor_scalar(sT[:, w0:w1_], psIn[:, sl, w0:w1_], bcol, None,
                                              op0=ALU.add, op1=ALU.bypass)
                    else:
                        ins = v.tensor_copy(sT[:, w0:w1_], psIn[:, sl, w0:w1_])
                    p.inc(ins, "dve")
                    ev_evo_d[grp] = p.ev("dve")
                if hf == NH - 1:
                    ev_evo_d[(sh, dc)] = ev_evo_d[grp]

                @block.vector
                def _(v, sh=sh, dc=dc, grp=grp, pgrp=pgrp, w0=w0, w1_=w1_):
                    p.wait(v, "v", ev_evo_d[grp])
                    if pgrp is not None and pgrp in ev_tr_d:
                        p.wait(v, "v", ev_tr_d[pgrp])
                    p.inc(v.tensor_copy(sTh[:, w0:w1_], sT[:, w0:w1_]), "dve")
                    ev_add_d[("sth", grp)] = p.ev("dve")

                @block.vector
                def _(v, sh=sh, dc=dc, grp=grp, pgrp=pgrp, w0=w0, w1_=w1_):
                    p.wait(v, "v", ev_add_d[("sth", grp)])
                    if pgrp is not None and pgrp in ev_tr_d:
                        p.wait(v, "v", ev_tr_d[pgrp])
                    p.inc(v.tensor_tensor(sTl[:, w0:w1_], sT[:, w0:w1_], sTh[:, w0:w1_], op=ALU.subtract), "dve")
                    ev_add_d[("stl", grp)] = p.ev("dve")

                @block.tensor
                def _(t, sh=sh, dc=dc, grp=grp, pgrp=pgrp, tiles=tiles):
                    p.waits(t, "t", [ev_add_d[("stl", grp)], ev_add_d[("sth", grp)]])
                    if pgrp is not None and ("ev", pgrp) in ev_add_d:
                        p.wait(t, "t", ev_add_d[("ev", pgrp)])
                    for i in tiles:
                        j = i % 2
                        ins = t.transpose(psT[:, j, 0, 0:128], sTh[:, i * 128:(i + 1) * 128], identb[:, :])
                        p.inc(ins, "pe")
                        ins = t.transpose(psT[:, j, 1, 0:128], sTl[:, i * 128:(i + 1) * 128], identb[:, :])
                        ev_tr_d[(sh, dc, i)] = p.inc(ins, "pe")
                    ev_tr_d[grp] = p.ev("pe")
                if hf == NH - 1:
                    ev_tr_d[(sh, dc)] = ev_tr_d[grp]

                @block.vector
                def _(v, sh=sh, dc=dc, grp=grp, pgrp=pgrp, tiles=tiles):
                    for i in tiles:
                        j = i % 2
                        p.wait(v, "v", ev_tr_d[(sh, dc, i)])
                        if pgrp is not None and (pgrp, j) in ev_add_d:
                            p.wait(v, "v", ev_add_d[(pgrp, j)])
                        v.tensor_copy(tr_sb[:, j, 0, :], psT[:, j, 0, 0:128]).then_inc(p.sem["dve"], 1)
                        p.cnt["dve"] += 1
                        ins = v.tensor_copy(tr_sb[:, j, 1, :], psT[:, j, 1, 0:128])
                        p.inc(ins, "dve")
                        ev_add_d[("evt", (sh, dc, i))] = p.ev("dve")
                    ev_add_d[("ev", grp)] = p.ev("dve")

                @block.vector
                def _(v, sh=sh, dc=dc, grp=grp, tiles=tiles):
                    p.wait(v, "v", ev_xload)
                    for i in tiles:
                        j = i % 2
                        p.wait(v, "v", ev_add_d[("evt", (sh, dc, i))])
                        a = A_stage[:, i, dc * 128:(dc + 1) * 128]
                        v.tensor_tensor(a, a, tr_sb[:, j, 0, :], op=ALU.add).then_inc(p.sem["dve"], 1)
                        p.cnt["dve"] += 1
                        p.wait(v, "v", p.ev("dve"))
                        ins = v.tensor_tensor(a, a, tr_sb[:, j, 1, :], op=ALU.add)
                        ev_add_d[(sh, dc, i)] = p.inc(ins, "dve")
                        ev_add_d[(grp, j)] = p.ev("dve")

        stq2.close()
        if c.stop_after == "p2":
            emit_dummy_out()
            stq.close()
            return nc
        stA = ExitStack()
        eA = stA.enter_context
        A_stage = eA(nc.sbuf_tensor("A_stage", [128, c.NOT, c.D], F32, side="right"))

        with (
            nc.sbuf_tensor("wo", [128, 2, c.DC, 128], BF16) as wo,
            nc.psum_tensor("psO", [128, 2, 512], F32) as psO,
            nc.psum_tensor("psT2", [128, 2, 2, 1024], BF16) as psT2,
        ):
            ev_wo = {}
            ev_xa = {}
            ev_mmo = {}
            ev_evo = {}
            ev_tr2 = {}
            ev_add = {}
            ev_adma = {}
            w_or = w_out.rearrange("(ec pp) (dc j) -> dc pp ec j", pp=128, j=128)

            ev_qst = [p.ev("dQst0"), p.ev("dQst1")]
            for sh in range(c.NSQ):
                @block.sync
                def _(s, sh=sh):
                    p.wait(s, "s", ev_attn)
                    p.waits(s, "s", ev_qst)
                    if sh >= 1:
                        p.wait(s, "s", ev_adma[sh - 1])
                    for i in range(c.NOT):
                        row = (sh * c.NOT + i) * 128
                        p.inc(s.dma_start(out=A_stage[:, i, :], in_=x_own[row:row + 128, :]), "dA", 16)
                    ev_xa[sh] = p.ev("dA")

                for dc in range(c.DC):
                    sl = dc % 2
                    prev2 = None
                    if dc >= 2:
                        prev2 = (sh, dc - 2)
                    elif sh >= 1:
                        prev2 = (sh - 1, c.DC - 2 + dc)
                    prev1 = (sh, dc - 1) if dc >= 1 else ((sh - 1, c.DC - 1) if sh >= 1 else None)

                    @block.gpsimd
                    def _(g, sh=sh, dc=dc, sl=sl, prev2=prev2):
                        p.wait(g, "g", ev_attn)
                        if prev2 in ev_mmo:
                            p.wait(g, "g", ev_mmo[prev2])
                        sem = "dWo0" if dc % 2 == 0 else "dWo1"
                        p.inc(g.dma_start(out=wo[:, sl, :, :], in_=w_or[dc]), sem, 16)
                        ev_wo[(sh, dc)] = p.ev(sem)

                    @block.tensor
                    def _(t, sh=sh, dc=dc, sl=sl, prev2=prev2):
                        p.waits(t, "t", [ev_wo[(sh, dc)], ev_attn])
                        if prev2 in ev_evo:
                            p.wait(t, "t", ev_evo[prev2])
                        for ec in range(c.DC):
                            ins = t.matmul(psO[:, sl, 0:c.SQW], wo[:, sl, ec, :],
                                           combn[:, ec, sh * c.SQW:(sh + 1) * c.SQW],
                                           start=(ec == 0), stop=(ec == c.DC - 1))
                        ev_mmo[(sh, dc)] = p.inc(ins, "pe")

                    emit_tr_add(sh, dc, sl, prev1, psO, psT2, ev_mmo, ev_evo, ev_tr2, ev_add,
                                ev_xa[sh], None)

                @block.gpsimd
                def _(g, sh=sh):
                    for i in range(c.NOT):
                        p.wait(g, "g", ev_add[(sh, c.DC - 1, i)])
                        row = (sh * c.NOT + i) * 128
                        p.inc(g.dma_start(out=a_d[row:row + 128, :], in_=A_stage[:, i, :]), "dAD", 16)
                    ev_adma[sh] = p.ev("dAD")

        if c.stop_after == "p25":
            emit_dummy_out()
            stA.close()
            stq.close()
            return nc
        stq.close()
        fence3 = [p.ev("pe"), p.ev("act"), p.ev("dve")]

        with (
            nc.sbuf_tensor("gbuf", [128, c.FC, c.SQW], BF16) as gbuf,
            nc.sbuf_tensor("hbuf", [128, 2, min(4, c.FC), c.SQW], BF16) as hbuf,
            nc.sbuf_tensor("w1st", [128, 2, c.DC, 128], BF16) as w1st,
            nc.sbuf_tensor("w2st", [128, 2, c.FC, 128], BF16) as w2st,
            nc.psum_tensor("psH", [128, 2, 512], F32) as psH,
            nc.psum_tensor("psO2", [128, 2, 512], F32) as psO2,
            nc.psum_tensor("psT3", [128, 2, 2, 1024], BF16) as psT3,
        ):
            w1r = w1q.rearrange("(ec pp) (fc j) -> fc pp ec j", pp=128, j=128)
            w2r = w2q.rearrange("(fc pp) (dc j) -> dc pp fc j", pp=128, j=128)
            ev_w1 = {}
            ev_w2 = {}
            ev_mmh = {}
            ev_g = {}
            ev_mm2 = {}
            ev_ev2 = {}
            ev_tr3 = {}
            ev_add3 = {}
            ev_odma = {}
            ev_xa3 = {}

            for sh in range(c.NSQ):
                @block.sync
                def _(s, sh=sh):
                    p.wait(s, "s", ev_adma[c.NSQ - 1])
                    if sh >= 1:
                        p.wait(s, "s", ev_odma[sh - 1])
                    for i in range(c.NOT):
                        row = (sh * c.NOT + i) * 128
                        p.inc(s.dma_start(out=A_stage[:, i, :], in_=a_d[row:row + 128, :]), "dA", 16)
                    ev_xa3[sh] = p.ev("dA")

                FCG = min(4, c.FC)
                NFG = c.FC // FCG
                ev_hb = {}
                for fc in range(c.FC):
                    sl = fc % 2
                    fg = fc // FCG
                    fgsl = fg % 2
                    prev2 = None
                    if fc >= 2:
                        prev2 = (sh, fc - 2)
                    elif sh >= 1:
                        prev2 = (sh - 1, c.FC - 2 + fc)

                    @block.sync
                    def _(s, sh=sh, fc=fc, sl=sl, prev2=prev2):
                        p.wait(s, "s", ev_cc1)
                        if sh == 0 and fc < 2:
                            p.waits(s, "s", fence3)
                        if prev2 in ev_mmh:
                            p.wait(s, "s", ev_mmh[prev2])
                        sem = "dF10" if fc % 2 == 0 else "dF11"
                        p.inc(s.dma_start(out=w1st[:, sl, :, :], in_=w1r[fc]), sem, 16)
                        ev_w1[(sh, fc)] = p.ev(sem)

                    @block.tensor
                    def _(t, sh=sh, fc=fc, sl=sl, prev2=prev2):
                        p.wait(t, "t", ev_w1[(sh, fc)])
                        if prev2 in ev_hb:
                            p.wait(t, "t", ev_hb[prev2])
                        for ec in range(c.DC):
                            ins = t.matmul(psH[:, sl, 0:c.SQW], w1st[:, sl, ec, :],
                                           xnT[:, ec, sh * c.SQW:(sh + 1) * c.SQW],
                                           start=(ec == 0), stop=(ec == c.DC - 1))
                        ev_mmh[(sh, fc)] = p.inc(ins, "pe")

                    @block.vector
                    def _(v, sh=sh, fc=fc, sl=sl, fg=fg, fgsl=fgsl):
                        p.waits(v, "v", [ev_mmh[(sh, fc)], ev_b1c])
                        pfg = (sh, fg - 2) if fg >= 2 else ((sh - 1, NFG - 2 + fg) if sh >= 1 else None)
                        if pfg in ev_g:
                            p.wait(v, "v", ev_g[pfg])
                        ins = v.tensor_scalar(hbuf[:, fgsl, fc % FCG, :], psH[:, sl, 0:c.SQW],
                                              b1c[:, fc:fc + 1], None, op0=ALU.add, op1=ALU.bypass)
                        ev_hb[(sh, fc)] = p.inc(ins, "dve")

                    if fc % FCG == FCG - 1:
                        @block.scalar
                        def _(a, sh=sh, fc=fc, fg=fg, fgsl=fgsl):
                            p.wait(a, "a", ev_hb[(sh, fc)])
                            if sh >= 1 and fg == 0:
                                p.wait(a, "a", ev_mm2[(sh - 1, c.DC - 1)])
                            ins = a.activation(gbuf[:, fg * FCG:(fg + 1) * FCG, :],
                                               hbuf[:, fgsl, :, :], AF.Gelu)
                            ev_g[(sh, fg)] = p.inc(ins, "act")

                for dc in range(c.DC):
                    sl = dc % 2
                    prev2 = None
                    if dc >= 2:
                        prev2 = (sh, dc - 2)
                    elif sh >= 1:
                        prev2 = (sh - 1, c.DC - 2 + dc)
                    prev1 = (sh, dc - 1) if dc >= 1 else ((sh - 1, c.DC - 1) if sh >= 1 else None)

                    @block.sync
                    def _(s, sh=sh, dc=dc, sl=sl, prev2=prev2):
                        p.wait(s, "s", ev_cc2)
                        if sh == 0 and dc < 2:
                            p.waits(s, "s", fence3)
                        if prev2 in ev_mm2:
                            p.wait(s, "s", ev_mm2[prev2])
                        sem = "dF20" if dc % 2 == 0 else "dF21"
                        for f0 in range(0, c.FC, 16):
                            f1 = min(f0 + 16, c.FC)
                            p.inc(s.dma_start(out=w2st[:, sl, f0:f1, :], in_=w2r[dc][:, f0:f1, :]), sem, 16)
                        ev_w2[(sh, dc)] = p.ev(sem)

                    @block.tensor
                    def _(t, sh=sh, dc=dc, sl=sl, prev2=prev2):
                        p.waits(t, "t", [ev_w2[(sh, dc)], ev_g[(sh, c.FC // min(4, c.FC) - 1)]])
                        if prev2 in ev_ev2:
                            p.wait(t, "t", ev_ev2[prev2])
                        for fc in range(c.FC):
                            ins = t.matmul(psO2[:, sl, 0:c.SQW], w2st[:, sl, fc, :], gbuf[:, fc, :],
                                           start=(fc == 0), stop=(fc == c.FC - 1))
                        ev_mm2[(sh, dc)] = p.inc(ins, "pe")

                    emit_tr_add(sh, dc, sl, prev1, psO2, psT3, ev_mm2, ev_ev2, ev_tr3, ev_add3,
                                ev_xa3[sh], b2c[:, dc:dc + 1])

                @block.gpsimd
                def _(g, sh=sh):
                    for i in range(c.NOT):
                        p.wait(g, "g", ev_add3[(sh, c.DC - 1, i)])
                        row = (sh * c.NOT + i) * 128
                        p.inc(g.dma_start(out=out_d[row:row + 128, :], in_=A_stage[:, i, :]), "dO", 16)
                    ev_odma[sh] = p.ev("dO")

            @block.gpsimd
            def _(g):
                if c.taps:
                    p.inc(g.dma_start(out=tap_g.rearrange("(fc pp) w -> pp fc w", pp=128), in_=gbuf[:, :, :]), "dO", 16)
                    p.inc(g.dma_start(out=tap_ad[:, :], in_=a_d[:, :]), "dO", 16)
                    p.inc(g.dma_start(out=tap_sth[:, :], in_=sTh[:, :]), "dO", 16)
                    p.inc(g.dma_start(out=tap_stl[:, :], in_=sTl[:, :]), "dO", 16)
                    p.inc(g.dma_start(out=tap_trsb[:, :], in_=tr_sb.rearrange("p a b j -> p (a b j)")), "dO", 16)
                    p.inc(g.dma_start(out=tap_kg[:, :], in_=k_g[:, :]), "dO", 16)
                    p.inc(g.dma_start(out=tap_vg[:, :], in_=v_g[:, :]), "dO", 16)
                    p.inc(g.dma_start(out=tap_w1q[:, :], in_=w1q[:, :]), "dO", 16)
                    p.inc(g.dma_start(out=tap_w2q[:, :], in_=w2q[:, :]), "dO", 16)
                g.wait_ge(p.sem["dO"], p.cnt["dO"])

        stA.close()

    return nc



def kernel(L0_attn_x, L0_attn_w_qkv, L0_attn_w_out, L0_ffn_w1, L0_ffn_b1, L0_ffn_w2, L0_ffn_b2):
    c = Cfg()
    x = np.ascontiguousarray(np.asarray(L0_attn_x, dtype=np.float32).reshape(c.NTOK, c.D))
    w_qkv = np.ascontiguousarray(np.asarray(L0_attn_w_qkv, dtype=np.float32))
    w_out = np.ascontiguousarray(np.asarray(L0_attn_w_out, dtype=np.float32))
    w1T = np.ascontiguousarray(np.asarray(L0_ffn_w1, dtype=np.float32).T)
    w2T = np.ascontiguousarray(np.asarray(L0_ffn_w2, dtype=np.float32).T)
    b1 = np.ascontiguousarray(np.asarray(L0_ffn_b1, dtype=np.float32))
    b2 = np.ascontiguousarray(np.asarray(L0_ffn_b2, dtype=np.float32))
    in_maps = []
    for i in range(c.NC):
        in_maps.append({
            "x_own": np.ascontiguousarray(x[i * c.T:(i + 1) * c.T]),
            "w_qkv": w_qkv,
            "w_out": w_out,
            "w1s": np.ascontiguousarray(w1T[i * c.W1R:(i + 1) * c.W1R]),
            "w2s": np.ascontiguousarray(w2T[i * c.W2R:(i + 1) * c.W2R]),
            "b1": b1,
            "b2": b2,
        })
    nc = build_program(c)
    res = run_bass_kernel_spmd(nc, in_maps, core_ids=list(range(c.NC)))
    out = np.concatenate([np.asarray(r["out"]) for r in res.results], axis=0)
    return out.reshape(c.B, c.S, c.D)
